# version 06
# speedup vs baseline: 1.3930x; 1.0529x over previous
import numpy as np
import ml_dtypes
from contextlib import ExitStack

import concourse.bass as bass
import concourse.mybir as mybir
from concourse.bass_utils import run_bass_kernel_spmd

F32 = mybir.dt.float32
F32R = mybir.dt.float32r
BF16 = mybir.dt.bfloat16
I32 = mybir.dt.int32
AF = mybir.ActivationFunctionType
ALU = mybir.AluOpType
AX = mybir.AxisListType.X
BF16NP = ml_dtypes.bfloat16

NCORES = 8
T = 16384
D = 1024
TSL = T // NCORES
NT = TSL // 128
NC = D // 128
DI = 3072
EPS = 1e-5

ENGINES = ("sync", "vector", "scalar", "tensor", "gpsimd")


class Sem:
    __slots__ = ("h", "n")
    def __init__(self, h):
        self.h = h
        self.n = 0


class Ctx:
    def __init__(self, nc):
        self.nc = nc
        self.st = ExitStack()
        self.q = {k: [] for k in ENGINES}
        self._sems = {}
        self.scopes = [self.st]

    def sem(self, name):
        if name not in self._sems:
            self._sems[name] = Sem(self.st.enter_context(self.nc.semaphore(name)))
        return self._sems[name]

    def sbuf(self, name, shape, dt):
        return self.scopes[-1].enter_context(self.nc.sbuf_tensor(name, shape, dt))

    def psum(self, name, shape, dt):
        return self.scopes[-1].enter_context(self.nc.psum_tensor(name, shape, dt))

    def push_scope(self):
        self.scopes.append(ExitStack())

    def pop_scope(self):
        self.scopes.pop().close()

    def barrier(self):
        for eng in ENGINES:
            for s in self._sems.values():
                if s.n > 0:
                    self.wait(eng, s)

    def op(self, eng, fn, inc=None):
        if inc is not None:
            sem, k = inc
            sem.n += k
            self.q[eng].append(lambda e, fn=fn, s=sem.h, k=k: fn(e).then_inc(s, k))
        else:
            self.q[eng].append(fn)

    def wait(self, eng, sem, val=None):
        v = sem.n if val is None else val
        self.q[eng].append(lambda e, s=sem.h, v=v: e.wait_ge(s, v))

    def chainop(self, eng, sem, fn):
        self.wait(eng, sem)
        self.op(eng, fn, inc=(sem, 1))

    def build_block(self):
        with self.nc.Block() as block:
            @block.sync
            def _(e):
                for fn in self.q["sync"]:
                    fn(e)

            @block.vector
            def _(e):
                for fn in self.q["vector"]:
                    fn(e)

            @block.scalar
            def _(e):
                for fn in self.q["scalar"]:
                    fn(e)

            @block.tensor
            def _(e):
                for fn in self.q["tensor"]:
                    fn(e)

            @block.gpsimd
            def _(e):
                for fn in self.q["gpsimd"]:
                    fn(e)

    def close(self):
        self.st.close()


class Consts:
    def __init__(self, c, values):
        self.vals = list(values)
        self.t = c.sbuf("consts", [128, max(1, len(self.vals))], F32)
        self.sem = c.sem("s_consts")
        for i, v in enumerate(self.vals):
            c.op("gpsimd", lambda e, i=i, v=v: e.memset(self.t[:, i:i + 1], v),
                 inc=(self.sem, 1) if i == len(self.vals) - 1 else None)

    def __getitem__(self, v):
        return self.t[:, self.vals.index(v):self.vals.index(v) + 1]


def make_identity(c, name, dt):
    ident = c.sbuf(name, [128, 128], dt)
    s = c.sem(name + "_s")
    c.op("gpsimd", lambda e, i=ident: e.memset(i[:, :], 0.0), inc=(s, 1))
    c.wait("gpsimd", s)
    c.op("gpsimd", lambda e, i=ident: e.affine_select(
        out=i[:, :], in_=i[:, :], compare_op=ALU.not_equal, fill=1.0,
        base=0, pattern=[[-1, 128]], channel_multiplier=1), inc=(s, 1))
    return ident, s


def emit_ln(c, name, x_sb, nt, h_tm, scr, stats, ident, s_h, sem_xready,
            eps_ap, h_fm=None, p_tr=None, s_fm=None):
    sx, sxv = sem_xready
    ssum = lambda t: stats[:, t:t + 1]
    ssq = lambda t: stats[:, nt + t:nt + t + 1]
    mean = stats[:, 2 * nt:3 * nt]
    msq = stats[:, 3 * nt:4 * nt]
    var = stats[:, 4 * nt:5 * nt]
    rstd = stats[:, 5 * nt:6 * nt]
    negmr = stats[:, 6 * nt:7 * nt]
    sd = stats[:, 7 * nt:8 * nt]

    s_stat = c.sem(name + "_stat")
    s_sq = c.sem(name + "_sq")

    c.wait("vector", sx, sxv)
    c.wait("scalar", sx, sxv)
    for t in range(nt):
        xt = x_sb[:, t * 1024:(t + 1) * 1024]
        c.op("vector", lambda e, xt=xt, o=ssum(t): e.reduce_sum(o, xt, axis=AX),
             inc=(s_stat, 1))
        if t >= 2:
            c.wait("scalar", s_sq, s_sq.n - 1)
        half = (t % 2) * 1024
        c.op("scalar", lambda e, xt=xt, s=scr[:, half:half + 1024], o=ssq(t):
             e.activation(s, xt, AF.Square, accum_out=o), inc=(s_sq, 1))
    c.wait("vector", s_stat)
    c.wait("vector", s_sq)
    c.op("vector", lambda e: e.tensor_scalar_mul(mean, stats[:, 0:nt], 1.0 / 1024.0),
         inc=(s_stat, 1))
    c.op("vector", lambda e: e.tensor_scalar_mul(msq, stats[:, nt:2 * nt], 1.0 / 1024.0),
         inc=(s_stat, 1))
    c.chainop("vector", s_stat,
              lambda e: e.tensor_tensor(var, mean, mean, ALU.mult))
    c.chainop("vector", s_stat,
              lambda e: e.tensor_tensor(var, msq, var, ALU.subtract))
    c.wait("scalar", s_stat)
    c.op("scalar", lambda e: e.activation(sd, var, AF.Sqrt, bias=eps_ap),
         inc=(s_stat, 1))
    c.wait("vector", s_stat)
    c.op("vector", lambda e: e.reciprocal(rstd, sd), inc=(s_stat, 1))
    c.chainop("vector", s_stat, lambda e: e.scalar_tensor_tensor(
        negmr, mean, -1.0, rstd, ALU.mult, ALU.mult))
    c.wait("scalar", s_stat)
    for t in range(nt):
        xt = x_sb[:, t * 1024:(t + 1) * 1024]
        c.op("scalar", lambda e, xt=xt, o=h_tm[:, t * 1024:(t + 1) * 1024], t=t:
             e.activation(o, xt, AF.Identity, scale=rstd[:, t:t + 1],
                          bias=negmr[:, t:t + 1]), inc=(s_h, 1))

    if h_fm is not None:
        emit_transpose_tiles(c, nt, h_tm, h_fm, p_tr, ident, s_h, s_fm,
                             name=name)


def emit_transpose_tiles(c, nt, src_tm, dst_fm, p_tr, ident, s_src, s_dst,
                         name, tokens_total=None, s_src_base=0):
    if tokens_total is None:
        tokens_total = nt * 128
    if s_src_base == 0:
        s_src_base = s_src.n - nt
    ident, s_id = ident
    c.wait("tensor", s_id)
    s_tp = c.sem(name + "_tp")
    for t in range(nt):
        half = (t % 2) * 1024
        c.wait("tensor", s_src, s_src_base + t + 1)
        if t >= 2:
            c.wait("tensor", s_dst, s_dst.n - 1)
        for ch in range(NC):
            c.op("tensor", lambda e, t=t, ch=ch, half=half: e.transpose(
                p_tr[:, half + ch * 128:half + (ch + 1) * 128],
                src_tm[:, t * 1024 + ch * 128:t * 1024 + (ch + 1) * 128],
                ident[:, :]),
                inc=(s_tp, 1) if ch == NC - 1 else None)
        c.wait("vector", s_tp)
        dst = dst_fm.ap().rearrange("p (c w) -> p c w", c=NC)[
            :, :, t * 128:(t + 1) * 128]
        c.op("vector", lambda e, dst=dst, half=half: e.tensor_copy(
            dst, p_tr.ap().rearrange("p (c w) -> p c w", c=16)[
                :, half // 128:half // 128 + 8, :]),
            inc=(s_dst, 1))


def build_launch_A(with_fm=True, with_absmax=False):
    nc = bass.Bass("TRN2", target_bir_lowering=False)
    x_in = nc.dram_tensor("x", [TSL, D], F32, kind="ExternalInput").ap()
    htm_out = nc.dram_tensor("h_tm", [TSL, D], BF16, kind="ExternalOutput").ap()
    if with_fm:
        hfm_out = nc.dram_tensor("h_fm", [D, TSL], BF16, kind="ExternalOutput").ap()
    if with_absmax:
        amax_out = nc.dram_tensor("amax", [1, 1], F32, kind="ExternalOutput").ap()

    c = Ctx(nc)
    x_sb = c.sbuf("x_sb", [128, NT * 1024], F32)
    h_tm = c.sbuf("h_tm_sb", [128, NT * 1024], BF16)
    h_fm = c.sbuf("h_fm_sb", [128, NC * TSL], BF16) if with_fm else None
    scr = c.sbuf("scr", [128, 2048], BF16)
    stats = c.sbuf("stats", [128, 8 * NT], F32)
    p_tr = c.psum("p_tr", [128, 2048], BF16)
    ident = make_identity(c, "ident", BF16)
    consts = Consts(c, [EPS])

    s_x = c.sem("s_x")
    s_h = c.sem("s_h")
    s_fm = c.sem("s_fm")
    s_out = c.sem("s_out")

    c.op("sync", lambda e: e.dma_start(
        out=x_sb.ap().rearrange("p (t d) -> p t d", t=NT),
        in_=x_in.rearrange("(t p) d -> p t d", p=128)), inc=(s_x, 16))

    c.wait("scalar", consts.sem)
    emit_ln(c, "ln0", x_sb, NT, h_tm, scr, stats, ident, s_h, (s_x, 16),
            consts[EPS], h_fm=h_fm, p_tr=p_tr, s_fm=s_fm)

    c.wait("sync", s_h)
    c.op("sync", lambda e: e.dma_start(
        out=htm_out.rearrange("(t p) d -> p t d", p=128),
        in_=h_tm.ap().rearrange("p (t d) -> p t d", t=NT)), inc=(s_out, 16))
    if with_fm:
        c.wait("sync", s_fm)
        c.op("sync", lambda e: e.dma_start(
            out=hfm_out.rearrange("(ch p) s -> p ch s", p=128),
            in_=h_fm.ap().rearrange("p (ch s) -> p ch s", ch=NC)), inc=(s_out, 16))
    if with_absmax:
        s_am = c.sem("s_am")
        amax_pp = c.sbuf("amax_pp", [128, 1], F32)
        amax_r = c.sbuf("amax_r", [1, 1], F32)
        p_am = c.psum("p_am", [1, 128], F32)
        identf = make_identity(c, "identf", F32)
        c.wait("vector", s_h)
        c.op("vector", lambda e: e.tensor_reduce(
            amax_pp[:, :], h_tm[:, :], axis=AX, op=ALU.max,
            apply_absolute_value=True), inc=(s_am, 1))
        c.wait("tensor", s_am)
        c.wait("tensor", identf[1])
        c.op("tensor", lambda e: e.transpose(
            p_am[:, :], amax_pp[:, :], identf[0][:, :]), inc=(s_am, 1))
        c.wait("vector", s_am)
        c.op("vector", lambda e: e.tensor_reduce(
            amax_r[:, :], p_am[:, :], axis=AX, op=ALU.max,
            apply_absolute_value=True), inc=(s_am, 1))
        c.wait("sync", s_am)
        c.op("sync", lambda e: e.dma_start(out=amax_out, in_=amax_r[:, :]),
             inc=(s_out, 16))
    c.wait("sync", s_out)

    c.build_block()
    c.close()
    return nc


def build_launch_D():
    nc = bass.Bass("TRN2", target_bir_lowering=False)
    hfm_in = nc.dram_tensor("h_fm", [D, TSL], BF16, kind="ExternalInput").ap()
    attw_in = nc.dram_tensor("attw", [5 * D, D], F32, kind="ExternalInput").ap()
    qfm_out = nc.dram_tensor("q_fm", [D, TSL], BF16, kind="ExternalOutput").ap()
    akfm_out = nc.dram_tensor("ak_fm", [D, TSL], BF16, kind="ExternalOutput").ap()
    utm_out = nc.dram_tensor("u_tm", [TSL, D], BF16, kind="ExternalOutput").ap()
    wT_d = nc.dram_tensor("attwT", [5 * D, D], BF16, kind="Internal").ap()

    c = Ctx(nc)
    h_fm = c.sbuf("h_fm_sb", [128, NC * TSL], BF16)
    wb = [c.sbuf(f"wb{i}", [128, NC * 1024], BF16) for i in range(2)]
    src2 = c.sbuf("src2", [128, NC * TSL], BF16)
    stg = [c.sbuf(f"stg{i}", [128, 2048], BF16) for i in range(2)]
    ident = make_identity(c, "ident", BF16)
    p_w_cm = nc.psum_tensor("p_w", [128, 2048], BF16)
    p_w = p_w_cm.__enter__()

    s_h = c.sem("s_h")
    s_wl = [c.sem(f"s_wl{i}") for i in range(2)]
    s_mm = c.sem("s_mm")
    s_ev = c.sem("s_ev")
    s_st = [c.sem(f"s_st{i}") for i in range(2)]

    c.op("sync", lambda e: e.dma_start(
        out=h_fm.ap().rearrange("p (ch s) -> p ch s", ch=NC),
        in_=hfm_in.rearrange("(ch p) s -> p ch s", p=128)), inc=(s_h, 16))

    s_wt, _, s_wt_ev = emit_weight_transpose(
        c, "awt", attw_in, wT_d, 5 * D, D, ident, p_w, o_block=D)
    p_w_cm.__exit__(None, None, None)
    psP = [c.psum(f"psP{i}", [128, 2048], F32) for i in range(2)]

    def proj(wi, slot, src, src_sem, src_val, dst_dram, dst_sbuf, kind,
             prev_mm):
        sl = slot % 2
        if prev_mm is not None:
            c.wait("sync", s_mm, prev_mm)
        c.wait("sync", s_wt[0])
        c.wait("sync", s_wt[1])
        c.op("sync", lambda e, wi=wi, sl=sl: e.dma_start(
            out=wb[sl].ap().rearrange("p (ch m) -> p ch m", ch=NC),
            in_=wT_d[wi * D:(wi + 1) * D, :].rearrange(
                "(ch p) m -> p ch m", p=128)), inc=(s_wl[sl], 16))
        c.wait("tensor", s_wl[sl])
        c.wait("tensor", s_wt[0])
        c.wait("tensor", s_wt[1])
        c.wait("tensor", src_sem, src_val)
        for mc in range(8):
            ps = psP[mc % 2]
            if s_ev.n >= 1:
                c.wait("tensor", s_ev, s_ev.n - 1)
            if kind == "fm":
                for ch in range(NC):
                    for hh in range(4):
                        last = (ch == NC - 1 and hh == 3)
                        c.op("tensor", lambda e, ps=ps, ch=ch, hh=hh, mc=mc, sl=sl, src=src:
                             e.matmul(ps[:, hh * 512:(hh + 1) * 512],
                                      wb[sl][:, ch * 1024 + mc * 128:ch * 1024 + (mc + 1) * 128],
                                      src[:, ch * TSL + hh * 512:ch * TSL + (hh + 1) * 512],
                                      start=(ch == 0), stop=(ch == NC - 1)),
                             inc=(s_mm, 1) if last else None)
            else:
                for tt in range(2):
                    t = mc * 2 + tt
                    for ch in range(NC):
                        for hh in range(2):
                            last = (tt == 1 and ch == NC - 1 and hh == 1)
                            c.op("tensor", lambda e, ps=ps, ch=ch, hh=hh, t=t, tt=tt, sl=sl, src=src:
                                 e.matmul(ps[:, tt * 1024 + hh * 512:tt * 1024 + (hh + 1) * 512],
                                          src[:, ch * TSL + t * 128:ch * TSL + (t + 1) * 128],
                                          wb[sl][:, ch * 1024 + hh * 512:ch * 1024 + (hh + 1) * 512],
                                          start=(ch == 0), stop=(ch == NC - 1)),
                                 inc=(s_mm, 1) if last else None)
            c.wait("vector", s_mm)
            if dst_sbuf is not None:
                c.op("vector", lambda e, ps=ps, mc=mc, dst_sbuf=dst_sbuf:
                     e.tensor_copy(dst_sbuf[:, mc * TSL:mc * TSL + 2048], ps[:, :]),
                     inc=(s_ev, 1))
            else:
                c.wait("vector", s_st[mc % 2])
                c.op("vector", lambda e, ps=ps, mc=mc: e.tensor_copy(
                    stg[mc % 2][:, :], ps[:, :]), inc=(s_ev, 1))
                c.wait("sync", s_ev)
                if kind == "fm":
                    c.op("sync", lambda e, mc=mc, dst=dst_dram: e.dma_start(
                        out=dst[mc * 128:(mc + 1) * 128, :],
                        in_=stg[mc % 2][:, :]), inc=(s_st[mc % 2], 16))
                else:
                    c.op("sync", lambda e, mc=mc, dst=dst_dram: e.dma_start(
                        out=dst[mc * 256:(mc + 1) * 256, :].rearrange(
                            "(tt p) d -> p tt d", p=128),
                        in_=stg[mc % 2].ap().rearrange(
                            "p (tt d) -> p tt d", tt=2)),
                        inc=(s_st[mc % 2], 16))
        return s_mm.n

    n0 = proj(0, 0, h_fm[:, :], s_h, 16, None, src2, "fm", None)
    n1 = proj(3, 1, src2[:, :], s_ev, s_ev.n, akfm_out, None, "fm", None)
    n2 = proj(1, 0, h_fm[:, :], s_h, 16, qfm_out, None, "fm", n0)
    n3 = proj(2, 1, h_fm[:, :], s_h, 16, None, src2, "fm", n1)
    n4 = proj(4, 0, src2[:, :], s_ev, s_ev.n, utm_out, None, "tm", n2)

    c.wait("sync", s_st[0])
    c.wait("sync", s_st[1])

    c.build_block()
    c.close()
    return nc


def emit_wsum_rep(c, W_dram, wsum_d, p_bf=None,
                  ld_sem_names=("s_wl0", "s_wl1")):
    wbuf = [c.sbuf(f"wbuf{i}", [128, 1024], F32) for i in range(2)]
    wsum_pc = c.sbuf("wsum_pc", [128, 8], F32)
    wrow = c.sbuf("wrow", [1, 1024], F32)
    ones_row = c.sbuf("ones_row", [1, 128], F32)
    wsum_rep = c.sbuf("wsum_rep", [128, 1024], F32)
    if p_bf is None:
        p_rep = c.psum("p_rep", [128, 512], F32)
    s_w = c.sem("s_w")
    s_wr = c.sem("s_wr")

    s_wl = [c.sem(n) for n in ld_sem_names]
    c.op("gpsimd", lambda e: e.memset(ones_row[:, :], 1.0), inc=(s_w, 1))
    for ch in range(8):
        sl = ch % 2
        if ch >= 2:
            c.wait("sync", s_w, s_w.n - 1)
        c.op("sync", lambda e, ch=ch, sl=sl: e.dma_start(
            out=wbuf[sl][:, :],
            in_=W_dram[ch * 128:(ch + 1) * 128, :]), inc=(s_wl[sl], 16))
        c.wait("vector", s_wl[sl], s_wl[sl].n)
        c.op("vector", lambda e, ch=ch, sl=sl: e.reduce_sum(
            wsum_pc[:, ch:ch + 1], wbuf[sl][:, :], axis=AX),
            inc=(s_w, 1))
    c.wait("sync", s_w, s_w.n)
    for ch in range(8):
        c.wait("sync", s_wr)
        c.op("sync", lambda e, ch=ch: e.dma_start(
            out=wsum_d[0:1, ch * 128:(ch + 1) * 128]
                .rearrange("r (p o) -> p (r o)", p=128),
            in_=wsum_pc[:, ch:ch + 1]), inc=(s_wr, 16))
    c.wait("sync", s_wr)
    c.op("sync", lambda e: e.dma_start(out=wrow[:, :], in_=wsum_d),
         inc=(s_wr, 16))
    if p_bf is None:
        c.wait("tensor", s_wr)
        c.wait("tensor", s_w, s_w.n)
        for hh in range(2):
            if hh == 1:
                c.wait("tensor", s_wr)
            c.op("tensor", lambda e, hh=hh: e.matmul(
                p_rep[:, :], ones_row[:, :], wrow[:, hh * 512:(hh + 1) * 512],
                start=True, stop=True), inc=(s_wr, 1))
            c.wait("vector", s_wr)
            c.op("vector", lambda e, hh=hh: e.tensor_copy(
                wsum_rep[:, hh * 512:(hh + 1) * 512], p_rep[:, :]),
                inc=(s_wr, 1))
        return wsum_rep, s_wr
    def mm_cb(p_ap):
        c.wait("tensor", s_wr, s_wr.n)
        c.wait("tensor", s_w, s_w.n)
        for hh in range(2):
            c.op("tensor", lambda e, hh=hh: e.matmul(
                p_ap[:, hh * 512:(hh + 1) * 512], ones_row[:, :],
                wrow[:, hh * 512:(hh + 1) * 512],
                start=True, stop=True), inc=(s_wr, 1) if hh == 1 else None)
        c.wait("vector", s_wr)
        c.op("vector", lambda e: e.tensor_copy(
            wsum_rep[:, :], p_ap[:, 0:1024]), inc=(s_wr, 1))
    return wsum_rep, s_wr, mm_cb


def build_launch_B():
    nc = bass.Bass("TRN2", target_bir_lowering=False)
    x_in = nc.dram_tensor("x", [TSL, D], F32, kind="ExternalInput").ap()
    hfm_in = nc.dram_tensor("h_fm", [D, T], BF16, kind="ExternalInput").ap()
    htm_in = nc.dram_tensor("h_tm", [T, D], BF16, kind="ExternalInput").ap()
    W_in = nc.dram_tensor("W", [D, D], F32, kind="ExternalInput").ap()
    x1_out = nc.dram_tensor("x1", [TSL, D], F32, kind="ExternalOutput").ap()
    E_dram = nc.dram_tensor("E_scr", [TSL, T], BF16, kind="Internal").ap()
    wsum_d = nc.dram_tensor("wsum_d", [1, D], F32, kind="Internal").ap()

    c = Ctx(nc)
    NKB = T // 1024

    x_t = [c.sbuf(f"x_t{i}", [128, 1024], F32) for i in range(2)]
    hq_fm = c.sbuf("hq_fm", [128, NC * TSL], BF16)
    kfm = [c.sbuf(f"kfm{i}", [128, NC * 1024], BF16) for i in range(2)]
    ktm = [c.sbuf(f"ktm{i}", [128, 8 * 1024], BF16) for i in range(2)]
    Escr = [c.sbuf(f"Escr{i}", [128, 1024], BF16) for i in range(2)]
    Embuf = [c.sbuf(f"Embuf{i}", [128, 1024], BF16) for i in range(2)]
    Emm = [c.sbuf(f"Emm{i}", [128, 1024], BF16) for i in range(2)]
    EmT = [c.sbuf(f"EmT{i}", [128, 1024], BF16) for i in range(2)]
    acc = c.sbuf("acc", [128, 8 * 1024], F32)
    Zparts = c.sbuf("Zparts", [128, NT * NKB], F32)
    SpParts = c.sbuf("SpParts", [128, NT * NKB], F32)
    zstat = c.sbuf("zstat", [128, 6 * NT], F32)
    psA = c.psum("psA", [128, 1024], F32)
    psB = c.psum("psB", [128, 1024], F32)
    p_tr = c.psum("p_tr", [128, 2048], BF16)
    ident = make_identity(c, "ident", BF16)
    consts = Consts(c, [-32.0])

    Z = zstat[:, 0:NT]
    theta = zstat[:, NT:2 * NT]
    invZ = zstat[:, 2 * NT:3 * NT]
    sp = zstat[:, 3 * NT:4 * NT]
    dinv = zstat[:, 4 * NT:5 * NT]
    coef2 = zstat[:, 5 * NT:6 * NT]

    s_x = c.sem("s_x")
    s_xl = [c.sem(f"s_xl{i}") for i in range(2)]
    s_xo = [c.sem(f"s_xo{i}") for i in range(2)]
    s_kf = [c.sem(f"s_kf{i}") for i in range(2)]
    s_kt = [c.sem(f"s_kt{i}") for i in range(2)]
    s_est = [c.sem(f"s_est{i}") for i in range(2)]
    s_eld = [c.sem(f"s_eld{i}") for i in range(2)]
    s_sc = c.sem("s_sc")
    s_exp = c.sem("s_exp")
    s_msk = c.sem("s_msk")
    s_tp = c.sem("s_tp")
    s_ev = c.sem("s_ev")
    s_av = c.sem("s_av")
    s_ac = c.sem("s_ac")
    s_fin = c.sem("s_fin")
    s_out = c.sem("s_out")

    hq_in = nc.dram_tensor("hq", [D, TSL], BF16, kind="ExternalInput").ap()
    c.op("sync", lambda e: e.dma_start(
        out=hq_fm.ap().rearrange("p (ch s) -> p ch s", ch=NC),
        in_=hq_in.rearrange("(ch p) s -> p ch s", p=128)), inc=(s_x, 16))

    wsum_rep, s_wr = emit_wsum_rep(c, W_in, wsum_d)

    c.wait("scalar", consts.sem)
    c.wait("tensor", s_x)
    for kb in range(NKB):
        if kb >= 2:
            c.wait("sync", s_sc, 16 * (kb - 1))
        c.op("sync", lambda e, kb=kb: e.dma_start(
            out=kfm[kb % 2].ap().rearrange("p (ch s) -> p ch s", ch=NC),
            in_=hfm_in[:, kb * 1024:(kb + 1) * 1024]
                .rearrange("(ch p) s -> p ch s", p=128)), inc=(s_kf[kb % 2], 16))
        for qt in range(NT):
            i = kb * NT + qt
            ps = psA if i % 2 == 0 else psB
            c.wait("tensor", s_kf[kb % 2], 16 * (kb // 2 + 1))
            if i >= 2:
                c.wait("tensor", s_exp, i - 1)
            for ch in range(NC):
                for hh in range(2):
                    last = (ch == NC - 1 and hh == 1)
                    c.op("tensor", lambda e, ps=ps, ch=ch, hh=hh, qt=qt, kb=kb:
                         e.matmul(ps[:, hh * 512:(hh + 1) * 512],
                                  hq_fm[:, ch * TSL + qt * 128:ch * TSL + (qt + 1) * 128],
                                  kfm[kb % 2][:, ch * 1024 + hh * 512:ch * 1024 + (hh + 1) * 512],
                                  start=(ch == 0), stop=(ch == NC - 1)),
                         inc=(s_sc, 1) if last else None)
            c.wait("scalar", s_sc)
            if i >= 2:
                c.wait("scalar", s_est[i % 2], 16 * (i // 2))
            c.op("scalar", lambda e, ps=ps, i=i, qt=qt, kb=kb: e.activation(
                Escr[i % 2][:, :], ps[:, :], AF.Exp, bias=consts[-32.0],
                scale=1.0 / 32.0, accum_out=Zparts[:, i:i + 1]),
                inc=(s_exp, 1))
            c.wait("gpsimd", s_exp)
            c.op("gpsimd", lambda e, i=i, qt=qt, kb=kb: e.dma_start(
                out=E_dram[qt * 128:(qt + 1) * 128, kb * 1024:(kb + 1) * 1024],
                in_=Escr[i % 2][:, :]), inc=(s_est[i % 2], 16))

    c.wait("vector", s_exp)
    c.op("vector", lambda e: e.reduce_sum(
        Z, Zparts.ap().rearrange("p (kb qt) -> p qt kb", qt=NT), axis=AX),
        inc=(s_fin, 1))
    c.chainop("vector", s_fin, lambda e: e.tensor_scalar_mul(theta, Z, 0.01))
    c.chainop("vector", s_fin, lambda e: e.reciprocal(invZ, Z))

    c.wait("tensor", s_exp)
    c.wait("tensor", ident[1])
    for g in range(2):
        for kb in range(NKB):
            j = g * NKB + kb
            if j >= 2:
                c.wait("sync", s_av, (j - 1) * 8)
            c.op("sync", lambda e, kb=kb, j=j: e.dma_start(
                out=ktm[j % 2].ap().rearrange("p (sc f) -> p sc f", sc=8),
                in_=htm_in[kb * 1024:(kb + 1) * 1024, :]
                    .rearrange("(sc p) f -> p sc f", p=128)),
                inc=(s_kt[j % 2], 16))
            for ti in range(8):
                qt = g * 8 + ti
                i = j * 8 + ti
                ist = kb * NT + qt
                c.wait("sync", s_est[ist % 2], 16 * (ist // 2 + 1))
                if i >= 2:
                    c.wait("sync", s_msk, i - 1)
                c.op("sync", lambda e, qt=qt, kb=kb, i=i: e.dma_start(
                    out=Embuf[i % 2][:, :],
                    in_=E_dram[qt * 128:(qt + 1) * 128, kb * 1024:(kb + 1) * 1024]),
                    inc=(s_eld[i % 2], 16))
                c.wait("vector", s_eld[i % 2])
                c.wait("vector", s_fin, 3)
                if i >= 2:
                    c.wait("vector", s_tp, i - 1)
                c.op("vector", lambda e, i=i, qt=qt, kb=kb: e.scalar_tensor_tensor(
                    Emm[i % 2][:, :], Embuf[i % 4][:, :], theta[:, qt:qt + 1],
                    Embuf[i % 4][:, :], ALU.is_ge, ALU.mult,
                    accum_out=SpParts[:, kb * NT + qt:kb * NT + qt + 1]),
                    inc=(s_msk, 1))
                c.wait("tensor", s_msk)
                if i >= 2:
                    c.wait("tensor", s_ev, i - 1)
                half = (i % 2) * 1024
                for ch in range(8):
                    c.op("tensor", lambda e, i=i, ch=ch, half=half: e.transpose(
                        p_tr[:, half + ch * 128:half + (ch + 1) * 128],
                        Emm[i % 2][:, ch * 128:(ch + 1) * 128], ident[0][:, :]),
                        inc=(s_tp, 1) if ch == 7 else None)
                c.wait("vector", s_tp)
                if i >= 2:
                    c.wait("vector", s_av, i - 1)
                c.op("vector", lambda e, i=i, half=half: e.tensor_copy(
                    EmT[i % 2][:, :], p_tr[:, half:half + 1024]), inc=(s_ev, 1))
                c.wait("tensor", s_ev)
                c.wait("tensor", s_kt[j % 2], 16 * (j // 2 + 1))
                ps = psA if i % 2 == 0 else psB
                if i >= 2:
                    c.wait("tensor", s_ac, i - 1)
                for sc in range(8):
                    for hh in range(2):
                        last = (sc == 7 and hh == 1)
                        c.op("tensor", lambda e, ps=ps, sc=sc, hh=hh, i=i, j=j:
                             e.matmul(ps[:, hh * 512:(hh + 1) * 512],
                                      EmT[i % 2][:, sc * 128:(sc + 1) * 128],
                                      ktm[j % 2][:, sc * 1024 + hh * 512:sc * 1024 + (hh + 1) * 512],
                                      start=(sc == 0), stop=(sc == 7)),
                             inc=(s_av, 1) if last else None)
                c.wait("vector", s_av)
                if kb == 0:
                    c.wait("vector", s_fin)
                else:
                    c.wait("vector", s_ac, i - 7)
                c.op("vector", lambda e, ps=ps, ti=ti, kb=kb: (
                    e.tensor_copy(acc[:, ti * 1024:(ti + 1) * 1024], ps[:, :])
                    if kb == 0 else
                    e.tensor_add(acc[:, ti * 1024:(ti + 1) * 1024],
                                 acc[:, ti * 1024:(ti + 1) * 1024], ps[:, :])),
                    inc=(s_ac, 1))

        c.wait("vector", s_msk)
        c.wait("vector", s_ac)
        c.op("vector", lambda e, g=g: e.reduce_sum(
            sp[:, g * 8:(g + 1) * 8],
            SpParts.ap().rearrange("p (kb qt) -> p qt kb", qt=NT)[:, g * 8:(g + 1) * 8, :],
            axis=AX), inc=(s_fin, 1))
        c.chainop("vector", s_fin, lambda e, g=g: e.scalar_tensor_tensor(
            dinv[:, g * 8:(g + 1) * 8], Z[:, g * 8:(g + 1) * 8], 1e-9,
            sp[:, g * 8:(g + 1) * 8], ALU.mult, ALU.add))
        c.chainop("vector", s_fin, lambda e, g=g: e.reciprocal(
            dinv[:, g * 8:(g + 1) * 8], dinv[:, g * 8:(g + 1) * 8]))
        c.chainop("vector", s_fin, lambda e, g=g: e.tensor_sub(
            coef2[:, g * 8:(g + 1) * 8], Z[:, g * 8:(g + 1) * 8],
            sp[:, g * 8:(g + 1) * 8]))
        c.chainop("vector", s_fin, lambda e, g=g: e.tensor_scalar_mul(
            coef2[:, g * 8:(g + 1) * 8], coef2[:, g * 8:(g + 1) * 8], 0.01))
        c.chainop("vector", s_fin, lambda e, g=g: e.tensor_tensor(
            coef2[:, g * 8:(g + 1) * 8], coef2[:, g * 8:(g + 1) * 8],
            invZ[:, g * 8:(g + 1) * 8], ALU.mult))
        c.wait("vector", s_wr)
        for ti in range(8):
            qt = g * 8 + ti
            xt = x_t[qt % 2]
            if qt >= 2:
                c.wait("sync", s_xo[qt % 2], 16 * (qt // 2))
            c.op("sync", lambda e, qt=qt, xt=xt: e.dma_start(
                out=xt[:, :], in_=x_in[qt * 128:(qt + 1) * 128, :]),
                inc=(s_xl[qt % 2], 16))
            c.wait("vector", s_xl[qt % 2])
            c.chainop("vector", s_fin, lambda e, xt=xt, ti=ti, qt=qt:
                      e.scalar_tensor_tensor(
                          xt[:, :], acc[:, ti * 1024:(ti + 1) * 1024],
                          dinv[:, qt:qt + 1], xt[:, :], ALU.mult, ALU.add))
            c.chainop("vector", s_fin, lambda e, xt=xt, qt=qt:
                      e.scalar_tensor_tensor(
                          xt[:, :], wsum_rep[:, :], coef2[:, qt:qt + 1], xt[:, :],
                          ALU.mult, ALU.add))
            c.wait("sync", s_fin)
            c.op("sync", lambda e, qt=qt, xt=xt: e.dma_start(
                out=x1_out[qt * 128:(qt + 1) * 128, :], in_=xt[:, :]),
                inc=(s_xo[qt % 2], 16))

    c.wait("sync", s_xo[0])
    c.wait("sync", s_xo[1])

    c.build_block()
    c.close()
    return nc


def emit_weight_transpose(c, name, w_in, wT_out, O, I, ident, p_w,
                          o_block=None, in_f32=True, sp=None, bufs=None,
                          defer=False):
    if o_block is None:
        o_block = O
    if sp is None:
        sp = name
    ldt = F32 if in_f32 else BF16
    if bufs is not None:
        ldb, cab, evb = bufs
    else:
        ldb = [c.sbuf(f"{name}_ld{i}", [128, I], ldt) for i in range(2)]
        cab = [c.sbuf(f"{name}_ca{i}", [128, I], BF16) for i in range(len(ldb))] \
            if in_f32 else ldb
        evb = [c.sbuf(f"{name}_ev{i}", [128, min(I, 1024)], BF16)
               for i in range(2)]
    nld = len(ldb)
    s_ld = [c.sem(f"{sp}_ld{i}") for i in range(nld)]
    s_ca = c.sem(f"{sp}_ca")
    s_tp = c.sem(f"{sp}_tp")
    s_ev = c.sem(f"{sp}_ev")
    s_st = [c.sem(f"{sp}_st{i}") for i in range(2)]
    NIC = I // 128
    NOC = O // 128
    NG = (NIC + 7) // 8
    c.wait("tensor", ident[1])

    def emit_oc(oc):
        sl = oc % nld
        if oc >= nld:
            if in_f32:
                c.wait("sync", s_ca, s_ca.n - (nld - 1))
            else:
                c.wait("sync", s_tp, s_tp.n - (nld - 1) * NG)
        c.op("sync", lambda e, oc=oc, sl=sl: e.dma_start(
            out=ldb[sl][:, :], in_=w_in[oc * 128:(oc + 1) * 128, :]),
            inc=(s_ld[sl], 16))
        if in_f32:
            c.wait("vector", s_ld[sl], s_ld[sl].n)
            if oc >= nld:
                c.wait("vector", s_tp, s_tp.n - (nld - 1) * NG)
            c.op("vector", lambda e, sl=sl: e.tensor_copy(
                cab[sl][:, :], ldb[sl][:, :]), inc=(s_ca, 1))
            c.wait("tensor", s_ca)
        else:
            c.wait("tensor", s_ld[sl], s_ld[sl].n)
        nfc = o_block // 128
        blk, fc = oc // nfc, oc % nfc
        nh = p_w.shape[1] // 1024
        for g in range(NG):
            nic_g = min(8, NIC - g * 8)
            gg = oc * NG + g
            half = (gg % nh) * 1024
            if gg >= nh:
                c.wait("tensor", s_ev, s_ev.n - (nh - 1))
            for ic in range(nic_g):
                c.op("tensor", lambda e, sl=sl, g=g, ic=ic, half=half:
                     e.transpose(p_w[:, half + ic * 128:half + (ic + 1) * 128],
                                 cab[sl][:, (g * 8 + ic) * 128:(g * 8 + ic + 1) * 128],
                                 ident[0][:, :]),
                     inc=(s_tp, 1) if ic == nic_g - 1 else None)
            c.wait("vector", s_tp)
            if gg >= 2:
                c.wait("vector", s_st[gg % 2], s_st[gg % 2].n)
            c.op("vector", lambda e, gg=gg, half=half, nic_g=nic_g:
                 e.tensor_copy(evb[gg % 2][:, 0:nic_g * 128],
                               p_w[:, half:half + nic_g * 128]), inc=(s_ev, 1))
            c.wait("sync", s_ev)
            dst = wT_out[blk * I + g * 1024:blk * I + g * 1024 + nic_g * 128, :]\
                .rearrange("(ic p) o -> p ic o", p=128)[
                    :, :, fc * 128:(fc + 1) * 128]
            c.op("sync", lambda e, dst=dst, gg=gg, nic_g=nic_g: e.dma_start(
                out=dst, in_=evb[gg % 2].ap().rearrange(
                    "p (ic j) -> p ic j", ic=nic_g)), inc=(s_st[gg % 2], 16))

    if defer:
        return s_st, NOC, s_ev, emit_oc
    for oc in range(NOC):
        emit_oc(oc)
    return s_st, NOC, s_ev


def build_launch_E():
    nc = bass.Bass("TRN2", target_bir_lowering=False)
    x_in = nc.dram_tensor("x2", [TSL, D], F32, kind="ExternalInput").ap()
    qfm_in = nc.dram_tensor("q_fm", [D, TSL], BF16, kind="ExternalInput").ap()
    akfm_in = nc.dram_tensor("ak_fm", [D, T], BF16, kind="ExternalInput").ap()
    utm_in = nc.dram_tensor("u_tm", [T, D], BF16, kind="ExternalInput").ap()
    x3_out = nc.dram_tensor("x3", [TSL, D], F32, kind="ExternalOutput").ap()

    c = Ctx(nc)
    NKB = T // 1024

    q_fm = c.sbuf("q_fm_sb", [128, NC * TSL], BF16)
    kfm = [c.sbuf(f"kfm{i}", [128, NC * 1024], BF16) for i in range(2)]
    ktm = [c.sbuf(f"ktm{i}", [128, 8 * 1024], BF16) for i in range(2)]
    Escr = [c.sbuf(f"Escr{i}", [128, 1024], BF16) for i in range(2)]
    EmT = [c.sbuf(f"EmT{i}", [128, 1024], BF16) for i in range(2)]
    acc = c.sbuf("acc", [128, NT * 1024], F32)
    Zparts = c.sbuf("Zparts", [128, NT * NKB], F32)
    fstat = c.sbuf("fstat", [128, 4 * NT + 4], F32)
    x_t = [c.sbuf(f"x_t{i}", [128, 1024], F32) for i in range(2)]
    psA = c.psum("psA", [128, 1024], F32)
    psB = c.psum("psB", [128, 1024], F32)
    p_tr = c.psum("p_tr", [128, 2048], BF16)
    ps_av = c.psum("ps_av", [128, 1024], F32)
    ident = make_identity(c, "ident", BF16)

    mr = fstat[:, 0:NT]
    negm = fstat[:, NT:2 * NT]
    Zr = fstat[:, 2 * NT:3 * NT]
    invZ = fstat[:, 3 * NT:4 * NT]
    bm = fstat[:, 4 * NT:4 * NT + 1]
    dcol = fstat[:, 4 * NT + 1:4 * NT + 2]
    alph = fstat[:, 4 * NT + 2:4 * NT + 3]

    s_q = c.sem("s_q")
    s_kf = [c.sem(f"s_kf{i}") for i in range(2)]
    s_kt = [c.sem(f"s_kt{i}") for i in range(2)]
    s_xl = [c.sem(f"s_xl{i}") for i in range(2)]
    s_xo = [c.sem(f"s_xo{i}") for i in range(2)]
    s_sc = c.sem("s_sc")
    s_v = c.sem("s_v")
    s_al = c.sem("s_al")
    s_exp = c.sem("s_exp")
    s_tp = c.sem("s_tp")
    s_av = c.sem("s_av")

    ev_cnt = {}
    upd_cnt = {}
    al_cnt = {}

    c.op("sync", lambda e: e.dma_start(
        out=q_fm.ap().rearrange("p (ch s) -> p ch s", ch=NC),
        in_=qfm_in.rearrange("(ch p) s -> p ch s", p=128)), inc=(s_q, 16))
    c.wait("tensor", s_q)
    c.wait("tensor", ident[1])

    for kb in range(NKB):
        if kb >= 2:
            c.wait("sync", s_sc, 16 * (kb - 1))
        c.op("sync", lambda e, kb=kb: e.dma_start(
            out=kfm[kb % 2].ap().rearrange("p (ch s) -> p ch s", ch=NC),
            in_=akfm_in[:, kb * 1024:(kb + 1) * 1024]
                .rearrange("(ch p) s -> p ch s", p=128)), inc=(s_kf[kb % 2], 16))
        if kb >= 2:
            c.wait("sync", s_av, (kb - 1) * 16)
        c.op("sync", lambda e, kb=kb: e.dma_start(
            out=ktm[kb % 2].ap().rearrange("p (sc f) -> p sc f", sc=8),
            in_=utm_in[kb * 1024:(kb + 1) * 1024, :]
                .rearrange("(sc p) f -> p sc f", p=128)), inc=(s_kt[kb % 2], 16))
        for qt in range(NT):
            i = kb * NT + qt
            ps = psA if i % 2 == 0 else psB
            c.wait("tensor", s_kf[kb % 2], 16 * (kb // 2 + 1))
            if i >= 2:
                c.wait("tensor", s_exp, i - 1)
            for ch in range(NC):
                for hh in range(2):
                    last = (ch == NC - 1 and hh == 1)
                    c.op("tensor", lambda e, ps=ps, ch=ch, hh=hh, qt=qt, kb=kb:
                         e.matmul(ps[:, hh * 512:(hh + 1) * 512],
                                  q_fm[:, ch * TSL + qt * 128:ch * TSL + (qt + 1) * 128],
                                  kfm[kb % 2][:, ch * 1024 + hh * 512:ch * 1024 + (hh + 1) * 512],
                                  start=(ch == 0), stop=(ch == NC - 1)),
                         inc=(s_sc, 1) if last else None)
            c.wait("vector", s_sc)
            if kb == 0:
                c.chainop("vector", s_v, lambda e, ps=ps, qt=qt: e.reduce_max(
                    mr[:, qt:qt + 1], ps[:, :], axis=AX))
                c.chainop("vector", s_v, lambda e, qt=qt: e.tensor_scalar_mul(
                    negm[:, qt:qt + 1], mr[:, qt:qt + 1], -1.0))
            else:
                if i - 1 in al_cnt:
                    c.wait("vector", s_al, al_cnt[i - 1])
                c.chainop("vector", s_v, lambda e, ps=ps: e.reduce_max(
                    bm, ps[:, :], axis=AX))
                c.chainop("vector", s_v, lambda e, qt=qt: e.tensor_sub(
                    dcol, mr[:, qt:qt + 1], bm))
                c.chainop("vector", s_v, lambda e: e.tensor_scalar_min(
                    dcol, dcol, 0.0))
                c.chainop("vector", s_v, lambda e, qt=qt: e.tensor_max(
                    mr[:, qt:qt + 1], mr[:, qt:qt + 1], bm))
                c.chainop("vector", s_v, lambda e, qt=qt: e.tensor_scalar_mul(
                    negm[:, qt:qt + 1], mr[:, qt:qt + 1], -1.0))
            v_after_chain = s_v.n
            c.wait("scalar", s_v, v_after_chain)
            if kb > 0:
                c.op("scalar", lambda e: e.activation(alph, dcol, AF.Exp),
                     inc=(s_al, 1))
                al_cnt[i] = s_al.n
            if i >= 2:
                c.wait("scalar", s_tp, i - 1)
            c.op("scalar", lambda e, ps=ps, i=i, qt=qt: e.activation(
                Escr[i % 2][:, :], ps[:, :], AF.Exp, bias=negm[:, qt:qt + 1],
                accum_out=Zparts[:, i:i + 1]), inc=(s_exp, 1))
            c.wait("tensor", s_exp)
            if i >= 2:
                c.wait("tensor", s_v, ev_cnt[i - 2])
            half = (i % 2) * 1024
            for ch in range(8):
                c.op("tensor", lambda e, i=i, ch=ch, half=half: e.transpose(
                    p_tr[:, half + ch * 128:half + (ch + 1) * 128],
                    Escr[i % 2][:, ch * 128:(ch + 1) * 128], ident[0][:, :]),
                    inc=(s_tp, 1) if ch == 7 else None)
            c.wait("vector", s_tp)
            if i >= 2:
                c.wait("vector", s_av, i - 1)
            c.chainop("vector", s_v, lambda e, i=i, half=half: e.tensor_copy(
                EmT[i % 2][:, :], p_tr[:, half:half + 1024]))
            ev_cnt[i] = s_v.n
            c.wait("tensor", s_v, ev_cnt[i])
            c.wait("tensor", s_kt[kb % 2], 16 * (kb // 2 + 1))
            if i >= 1:
                c.wait("tensor", s_v, upd_cnt[i - 1])
            for sc in range(8):
                for hh in range(2):
                    last = (sc == 7 and hh == 1)
                    c.op("tensor", lambda e, sc=sc, hh=hh, i=i, kb=kb:
                         e.matmul(ps_av[:, hh * 512:(hh + 1) * 512],
                                  EmT[i % 2][:, sc * 128:(sc + 1) * 128],
                                  ktm[kb % 2][:, sc * 1024 + hh * 512:sc * 1024 + (hh + 1) * 512],
                                  start=(sc == 0), stop=(sc == 7)),
                         inc=(s_av, 1) if last else None)
            at = acc[:, qt * 1024:(qt + 1) * 1024]
            c.wait("vector", s_av)
            if kb > 0:
                c.wait("vector", s_al, al_cnt[i])
            if kb == 0:
                c.chainop("vector", s_v, lambda e, at=at: e.tensor_copy(
                    at, ps_av[:, :]))
                c.chainop("vector", s_v, lambda e, qt=qt, i=i: e.tensor_copy(
                    Zr[:, qt:qt + 1], Zparts[:, i:i + 1]))
            else:
                c.chainop("vector", s_v, lambda e, at=at: e.scalar_tensor_tensor(
                    at, at, alph, ps_av[:, :], ALU.mult, ALU.add))
                c.chainop("vector", s_v, lambda e, qt=qt, i=i:
                          e.scalar_tensor_tensor(
                              Zr[:, qt:qt + 1], Zr[:, qt:qt + 1], alph,
                              Zparts[:, i:i + 1], ALU.mult, ALU.add))
            upd_cnt[i] = s_v.n

    c.chainop("vector", s_v, lambda e: e.reciprocal(invZ, Zr))
    for qt in range(NT):
        xt = x_t[qt % 2]
        if qt >= 2:
            c.wait("sync", s_xo[qt % 2], 16 * (qt // 2))
        c.op("sync", lambda e, qt=qt, xt=xt: e.dma_start(
            out=xt[:, :], in_=x_in[qt * 128:(qt + 1) * 128, :]),
            inc=(s_xl[qt % 2], 16))
        c.wait("vector", s_xl[qt % 2], 16 * (qt // 2 + 1))
        c.chainop("vector", s_v, lambda e, xt=xt, qt=qt: e.scalar_tensor_tensor(
            xt[:, :], acc[:, qt * 1024:(qt + 1) * 1024], invZ[:, qt:qt + 1],
            xt[:, :], ALU.mult, ALU.add))
        c.wait("sync", s_v)
        c.op("sync", lambda e, qt=qt, xt=xt: e.dma_start(
            out=x3_out[qt * 128:(qt + 1) * 128, :], in_=xt[:, :]),
            inc=(s_xo[qt % 2], 16))

    c.wait("sync", s_xo[0])
    c.wait("sync", s_xo[1])

    c.build_block()
    c.close()
    return nc


def build_launch_C():
    nc = bass.Bass("TRN2", target_bir_lowering=False)
    x_in = nc.dram_tensor("x1", [TSL, D], F32, kind="ExternalInput").ap()
    gw_in = nc.dram_tensor("gate_w", [8, D], F32, kind="ExternalInput").ap()
    ew_in = nc.dram_tensor("expert_w", [8 * D, D], F32, kind="ExternalInput").ap()
    x2_out = nc.dram_tensor("x2", [TSL, D], F32, kind="ExternalOutput").ap()
    ewT_d = nc.dram_tensor("ewT", [8 * D, D], BF16, kind="Internal").ap()

    c = Ctx(nc)
    x_sb = c.sbuf("x_sb", [128, NT * 1024], F32)
    h_tm = c.sbuf("h_tm", [128, NT * 1024], BF16)
    h_fm = c.sbuf("h_fm", [128, NC * TSL], BF16)
    scr = c.sbuf("scr", [128, 2048], BF16)
    stats = c.sbuf("stats", [128, 8 * NT], F32)
    gwT = c.sbuf("gwT", [128, NC * 8], BF16)
    gwb = c.sbuf("gwb", [8, D], F32)
    gwb16 = c.sbuf("gwb16", [8, D], BF16)
    mask8 = c.sbuf("mask8", [128, NT * 8], F32)
    gmax = c.sbuf("gmax", [128, NT], F32)
    wq = [c.sbuf(f"wq{e}", [128, NC * 256], BF16) for e in range(8)]
    accq = c.sbuf("accq", [128, NT * 256], BF16)
    ident = make_identity(c, "ident", BF16)
    consts = Consts(c, [EPS])

    s_x = c.sem("s_x")
    s_h = c.sem("s_h")
    s_fm = c.sem("s_fm")
    s_gw = c.sem("s_gw")
    s_gt = c.sem("s_gt")
    s_wq = [c.sem(f"s_wq{e}") for e in range(8)]
    s_pe = c.sem("s_pe")
    s_evq = c.sem("s_evq")
    s_fin = c.sem("s_fin")
    s_out = c.sem("s_out")

    p_tr_cm = nc.psum_tensor("p_tr", [128, 2048], BF16)
    p_tr = p_tr_cm.__enter__()
    p_g_cm = nc.psum_tensor("p_g", [128, 8], F32)
    p_g = p_g_cm.__enter__()
    p_w_cm = nc.psum_tensor("p_w", [128, 2048], BF16)
    p_w = p_w_cm.__enter__()

    c.op("sync", lambda e: e.dma_start(
        out=x_sb.ap().rearrange("p (t d) -> p t d", t=NT),
        in_=x_in.rearrange("(t p) d -> p t d", p=128)), inc=(s_x, 16))
    c.wait("scalar", consts.sem)
    emit_ln(c, "ln1", x_sb, NT, h_tm, scr, stats, ident, s_h, (s_x, 16),
            consts[EPS], h_fm=h_fm, p_tr=p_tr, s_fm=s_fm)

    s_wt, _, s_ewt_ev = emit_weight_transpose(c, "ewt", ew_in, ewT_d, 8 * D, D,
                                              ident, p_w, o_block=D)

    c.op("sync", lambda e: e.dma_start(out=gwb[:, :], in_=gw_in),
         inc=(s_gw, 16))
    c.wait("vector", s_gw)
    c.op("vector", lambda e: e.tensor_copy(gwb16[:, :], gwb[:, :]),
         inc=(s_gw, 1))
    c.wait("tensor", s_gw)
    c.wait("tensor", ident[1])
    c.wait("tensor", s_fm)
    for ch in range(NC):
        c.op("tensor", lambda e, ch=ch: e.transpose(
            p_tr[:, ch * 128:ch * 128 + 8],
            gwb16[:, ch * 128:(ch + 1) * 128], ident[0][0:8, 0:8]),
            inc=(s_gw, 1) if ch == NC - 1 else None)
    c.wait("vector", s_gw)
    c.op("vector", lambda e: e.tensor_copy(
        gwT.ap().rearrange("p (ch j) -> p ch j", ch=NC),
        p_tr.ap().rearrange("p (ch j) -> p ch j", ch=16)[:, 0:NC, 0:8]),
        inc=(s_gw, 1))

    for t in range(NT):
        c.wait("tensor", s_gw)
        c.wait("tensor", s_fm, t + 1)
        if t >= 1:
            c.wait("tensor", s_gt, 2 * t)
        for ch in range(NC):
            c.op("tensor", lambda e, ch=ch, t=t: e.matmul(
                p_g[:, :], h_fm[:, ch * TSL + t * 128:ch * TSL + (t + 1) * 128],
                gwT[:, ch * 8:(ch + 1) * 8],
                start=(ch == 0), stop=(ch == NC - 1)),
                inc=(s_pe, 1) if ch == NC - 1 else None)
        c.wait("vector", s_pe)
        c.op("vector", lambda e, t=t: e.reduce_max(
            gmax[:, t:t + 1], p_g[:, :], axis=AX), inc=(s_gt, 1))
        c.wait("vector", s_gt)
        c.op("vector", lambda e, t=t: e.tensor_single_scalar(
            mask8[:, t * 8:(t + 1) * 8], p_g[:, :], gmax[:, t:t + 1],
            ALU.is_ge), inc=(s_gt, 1))

    p_w_cm.__exit__(None, None, None)
    p_g_cm.__exit__(None, None, None)
    p_tr_cm.__exit__(None, None, None)
    psE = [c.psum(f"psE{i}", [128, 8 * 256], F32) for i in range(2)]

    c.wait("tensor", s_gt)
    c.wait("tensor", s_gw)
    c.wait("tensor", s_fm)
    c.wait("tensor", s_ewt_ev)

    for fq in range(4):
        for e8 in range(8):
            if fq >= 1:
                c.wait("sync", s_pe, NT + fq * NT)
            c.op("sync", lambda e, e8=e8, fq=fq: e.dma_start(
                out=wq[e8].ap().rearrange("p (ch j) -> p ch j", ch=NC),
                in_=ewT_d[e8 * D:(e8 + 1) * D, fq * 256:(fq + 1) * 256]
                    .rearrange("(ch p) j -> p ch j", p=128)),
                inc=(s_wq[e8], 16))
        for t in range(NT):
            gi = fq * NT + t
            grp = psE[gi % 2]
            for e8 in range(8):
                c.wait("tensor", s_wq[e8], 16 * (fq + 1))
            c.wait("tensor", s_wt[0], 16 * 32)
            c.wait("tensor", s_wt[1], 16 * 32)
            if gi >= 2:
                c.wait("tensor", s_evq, (gi - 1) * 8)
            for e8 in range(8):
                for ch in range(NC):
                    last = (ch == NC - 1 and e8 == 7)
                    c.op("tensor", lambda e, grp=grp, ch=ch, e8=e8, t=t:
                         e.matmul(grp[:, e8 * 256:(e8 + 1) * 256],
                                  h_fm[:, ch * TSL + t * 128:ch * TSL + (t + 1) * 128],
                                  wq[e8][:, ch * 256:(ch + 1) * 256],
                                  start=(ch == 0), stop=(ch == NC - 1)),
                         inc=(s_pe, 1) if last else None)
            c.wait("vector", s_pe)
            for e8 in range(8):
                aq = accq[:, t * 256:(t + 1) * 256]
                if e8 == 0:
                    if fq >= 1:
                        c.wait("vector", s_fin, 2 * ((fq - 1) * NT + t + 1))
                    c.op("vector", lambda e, grp=grp, aq=aq, t=t: e.tensor_single_scalar(
                        aq, grp[:, 0:256], mask8[:, t * 8:t * 8 + 1], ALU.mult),
                        inc=(s_evq, 1))
                else:
                    c.wait("vector", s_evq)
                    c.op("vector", lambda e, grp=grp, aq=aq, t=t, e8=e8:
                         e.scalar_tensor_tensor(
                             aq, grp[:, e8 * 256:(e8 + 1) * 256],
                             mask8[:, t * 8 + e8:t * 8 + e8 + 1], aq,
                             ALU.mult, ALU.add), inc=(s_evq, 1))
            xq = x_sb[:, t * 1024 + fq * 256:t * 1024 + (fq + 1) * 256]
            hq = h_tm[:, t * 1024 + fq * 256:t * 1024 + (fq + 1) * 256]
            c.wait("vector", s_evq)
            c.op("vector", lambda e, xq=xq, hq=hq: e.tensor_add(xq, xq, hq),
                 inc=(s_fin, 1))
            c.chainop("vector", s_fin, lambda e, xq=xq, t=t: e.tensor_add(
                xq, xq, accq[:, t * 256:(t + 1) * 256]))

    c.wait("sync", s_fin)
    c.op("sync", lambda e: e.dma_start(
        out=x2_out.rearrange("(t p) d -> p t d", p=128),
        in_=x_sb.ap().rearrange("p (t d) -> p t d", t=NT)), inc=(s_out, 16))
    c.wait("sync", s_out)

    c.build_block()
    c.close()
    return nc


def emit_bcast_scales(c, name, am_in, nrows, qmax, identf):
    amb = c.sbuf(name + "_amb", [nrows, 1], F32)
    amr = c.sbuf(name + "_amr", [1, 4], F32)
    ones_row = c.sbuf(name + "_ones", [1, 128], F32)
    bc = c.sbuf(name + "_bc", [128, 3], F32)
    p_t = c.psum(name + "_pt", [1, 128], F32)
    p_b = c.psum(name + "_pb", [128, 3], F32)
    s = c.sem("bcs_s")
    c.op("gpsimd", lambda e: e.memset(ones_row[:, :], 1.0), inc=(s, 1))
    c.op("sync", lambda e: e.dma_start(out=amb[:, :], in_=am_in,
                                       allow_slow_non_contiguous=True),
         inc=(s, 16))
    c.wait("tensor", s)
    c.wait("tensor", identf[1])
    c.op("tensor", lambda e: e.transpose(
        p_t[:, 0:nrows], amb[:, :], identf[0][0:nrows, 0:nrows]), inc=(s, 1))
    c.wait("vector", s)
    c.op("vector", lambda e: e.reduce_max(
        amr[:, 0:1], p_t[:, 0:nrows], axis=AX), inc=(s, 1))
    c.chainop("vector", s, lambda e: e.tensor_scalar(
        amr[:, 1:2], amr[:, 0:1], 1.0 / qmax, 1e-12, ALU.mult, ALU.max))
    c.chainop("vector", s, lambda e: e.reciprocal(amr[:, 2:3], amr[:, 1:2]))
    c.chainop("vector", s, lambda e: e.tensor_scalar_mul(
        amr[:, 3:4], amr[:, 2:3], -1.0))
    c.wait("tensor", s)
    c.op("tensor", lambda e: e.matmul(
        p_b[:, :], ones_row[:, :], amr[:, 1:4], start=True, stop=True),
        inc=(s, 1))
    c.wait("vector", s)
    c.op("vector", lambda e: e.tensor_copy(bc[:, :], p_b[:, :]), inc=(s, 1))
    return bc, s


def emit_round(c, eng_s, src, w, bufs, inv_neg_scale, s_ch):
    ib, cfb, db, ab, rb = bufs["i"], bufs["cf"], bufs["d"], bufs["a"], bufs["r"]
    c.wait("scalar", s_ch)
    c.op("scalar", lambda e: e.activation(
        ib[:, 0:w], src, AF.Identity, scale=inv_neg_scale), inc=(eng_s, 1))
    c.wait("scalar", eng_s)
    c.op("scalar", lambda e: e.activation(cfb[:, 0:w], ib[:, 0:w], AF.Copy),
         inc=(eng_s, 1))
    c.wait("vector", eng_s)
    c.chainop("vector", s_ch, lambda e: e.scalar_tensor_tensor(
        db[:, 0:w], src, inv_neg_scale, cfb[:, 0:w], ALU.mult, ALU.subtract))
    c.chainop("vector", s_ch, lambda e: e.scalar_tensor_tensor(
        ab[:, 0:w], db[:, 0:w], 0.5, cfb[:, 0:w], ALU.is_ge, ALU.add))
    c.chainop("vector", s_ch, lambda e: e.scalar_tensor_tensor(
        rb[:, 0:w], db[:, 0:w], -0.5, ab[:, 0:w], ALU.is_le, ALU.subtract))


def mk_round_bufs(c, name, w):
    cf = c.sbuf(name + "_cf", [128, w], F32)
    return {"i": c.sbuf(name + "_i", [128, w], I32),
            "cf": cf,
            "d": c.sbuf(name + "_d", [128, w], F32),
            "a": cf,
            "r": c.sbuf(name + "_r", [128, w], F32)}


def emit_dequant4(c, name, w_in, wdq_out, O, I, rbufs=None, nld=2, sp=None):
    if sp is None:
        sp = name
    nb = I // 32
    ld = [c.sbuf(f"{name}_ld{i}", [128, I], F32) for i in range(nld)]
    if nld == 1:
        ld = ld * 2
    pm = c.sbuf(f"{name}_pm", [128, nb], F32)
    sb = c.sbuf(f"{name}_sb", [128, nb], F32)
    iv = c.sbuf(f"{name}_iv", [128, nb], F32)
    yb = c.sbuf(f"{name}_yb", [128, I], F32)
    ob = [c.sbuf(f"{name}_ob{i}", [128, I], BF16) for i in range(2)]
    if rbufs is None:
        rbufs = mk_round_bufs(c, name + "_rb", I)
    s_l = [c.sem(f"{sp}_l{i}") for i in range(2)]
    s_ch = c.sem(f"{sp}_ch")
    s_a = c.sem(f"{sp}_a")
    s_st = [c.sem(f"{sp}_st{i}") for i in range(2)]
    for oc in range(O // 128):
        sl = (oc % 2) if nld == 2 else 0
        if oc >= (2 if nld == 2 else 1):
            c.wait("sync", s_ch, s_ch.n)
        c.op("sync", lambda e, oc=oc, sl=sl: e.dma_start(
            out=ld[sl][:, :], in_=w_in[oc * 128:(oc + 1) * 128, :]),
            inc=(s_l[sl], 16))
        c.wait("vector", s_l[sl], s_l[sl].n)
        c.chainop("vector", s_ch, lambda e, sl=sl: e.tensor_reduce(
            pm[:, :], ld[sl].ap().rearrange("p (b j) -> p b j", b=nb),
            axis=AX, op=ALU.max, apply_absolute_value=True))
        c.chainop("vector", s_ch, lambda e: e.tensor_scalar(
            sb[:, :], pm[:, :], 1.0 / 7.0, 1e-12, ALU.mult, ALU.max))
        c.chainop("vector", s_ch, lambda e: e.reciprocal(iv[:, :], sb[:, :]))
        c.chainop("vector", s_ch, lambda e: e.tensor_scalar_mul(
            iv[:, :], iv[:, :], -1.0))
        c.chainop("vector", s_ch, lambda e, sl=sl: e.tensor_tensor(
            yb.ap().rearrange("p (b j) -> p b j", b=nb), ld[sl].ap().rearrange(
                "p (b j) -> p b j", b=nb),
            iv.ap().unsqueeze(2).broadcast_to([128, nb, 32]), ALU.mult))
        c.wait("scalar", s_ch)
        c.op("scalar", lambda e: e.activation(
            rbufs["i"][:, :], yb[:, :], AF.Identity), inc=(s_a, 1))
        c.wait("scalar", s_a)
        c.op("scalar", lambda e: e.activation(
            rbufs["cf"][:, :], rbufs["i"][:, :], AF.Copy), inc=(s_a, 1))
        c.wait("vector", s_a)
        c.chainop("vector", s_ch, lambda e: e.tensor_sub(
            rbufs["d"][:, :], yb[:, :], rbufs["cf"][:, :]))
        c.chainop("vector", s_ch, lambda e: e.scalar_tensor_tensor(
            rbufs["a"][:, :], rbufs["d"][:, :], 0.5, rbufs["cf"][:, :],
            ALU.is_ge, ALU.add))
        c.chainop("vector", s_ch, lambda e: e.scalar_tensor_tensor(
            rbufs["r"][:, :], rbufs["d"][:, :], -0.5, rbufs["a"][:, :],
            ALU.is_le, ALU.subtract))
        c.chainop("vector", s_ch, lambda e: e.tensor_scalar(
            rbufs["r"][:, :], rbufs["r"][:, :], 7.0, -8.0, ALU.min, ALU.max))
        c.wait("vector", s_st[sl], s_st[sl].n)
        c.chainop("vector", s_ch, lambda e, sl=sl: e.tensor_tensor(
            ob[sl].ap().rearrange("p (b j) -> p b j", b=nb),
            rbufs["r"].ap().rearrange("p (b j) -> p b j", b=nb),
            sb.ap().unsqueeze(2).broadcast_to([128, nb, 32]), ALU.mult))
        c.wait("sync", s_ch)
        c.op("sync", lambda e, oc=oc, sl=sl: e.dma_start(
            out=wdq_out[oc * 128:(oc + 1) * 128, :], in_=ob[sl][:, :]),
            inc=(s_st[sl], 16))
    return s_st, s_ch, s_a


def build_launch_G():
    nc = bass.Bass("TRN2", target_bir_lowering=False)
    h3_in = nc.dram_tensor("h3", [TSL, D], BF16, kind="ExternalInput").ap()
    am1_in = nc.dram_tensor("amax1", [8, 1], F32, kind="ExternalInput").ap()
    wug_in = nc.dram_tensor("wug", [2 * DI, D], F32, kind="ExternalInput").ap()
    up_out = nc.dram_tensor("up_raw", [TSL, DI], BF16, kind="ExternalOutput").ap()
    gt_out = nc.dram_tensor("gate_raw", [TSL, DI], BF16, kind="ExternalOutput").ap()
    am23_out = nc.dram_tensor("amax23", [2, 1], F32, kind="ExternalOutput").ap()
    wdq_d = nc.dram_tensor("wdq", [2 * DI, D], BF16, kind="Internal").ap()
    wT_d = nc.dram_tensor("wT", [2 * D, DI], BF16, kind="Internal").ap()

    c = Ctx(nc)
    identf = make_identity(c, "identf", F32)
    ident = make_identity(c, "ident", BF16)
    bc1, s_bc1 = emit_bcast_scales(c, "sc1", am1_in, 8, 32767.0, identf)
    h3_sb = c.sbuf("h3_sb", [128, NT * 1024], BF16)
    xq_tm = c.sbuf("xq_tm", [128, NT * 1024], BF16)
    xq_fm = c.sbuf("xq_fm", [128, NC * TSL], BF16)
    rb = mk_round_bufs(c, "rb", 1024)
    wub = [c.sbuf(f"wub{i}", [128, NC * 512], BF16) for i in range(4)]
    stg = [c.sbuf(f"stg{i}", [128, 512], BF16) for i in range(4)]
    parts = c.sbuf("parts", [128, 2 * 96], F32)
    pfin = c.sbuf("pfin", [128, 2], F32)
    amr = c.sbuf("amr", [2, 1], F32)
    p_am = c.psum("p_am", [2, 128], F32)

    s_h3 = c.sem("s_h3")
    s_ch = c.sem("s_ch")
    s_a = c.sem("s_a")
    s_xq = c.sem("s_xq")
    s_fm = c.sem("s_fm")
    s_wl = [c.sem(f"s_wl{i}") for i in range(4)]
    s_mm = c.sem("s_mm")
    s_evq = c.sem("s_evq")
    s_pt = c.sem("s_pt")
    s_st = [c.sem(f"s_stq{i}") for i in range(4)]
    s_out = c.sem("s_out")

    c.op("sync", lambda e: e.dma_start(
        out=h3_sb.ap().rearrange("p (t d) -> p t d", t=NT),
        in_=h3_in.rearrange("(t p) d -> p t d", p=128)), inc=(s_h3, 16))

    c.wait("scalar", s_bc1)
    c.wait("vector", s_bc1)
    c.wait("scalar", s_h3)
    for t in range(NT):
        src = h3_sb[:, t * 1024:(t + 1) * 1024]
        emit_round(c, s_a, src, 1024, rb, bc1[:, 2:3], s_ch)
        c.wait("scalar", s_ch)
        c.op("scalar", lambda e, t=t: e.activation(
            xq_tm[:, t * 1024:(t + 1) * 1024], rb["r"][:, 0:1024],
            AF.Identity, scale=bc1[:, 0:1]), inc=(s_xq, 1))
        c.wait("vector", s_xq)

    p_tr_cm = nc.psum_tensor("p_tr", [128, 2048], BF16)
    p_tr = p_tr_cm.__enter__()
    emit_transpose_tiles(c, NT, xq_tm, xq_fm, p_tr, ident, s_xq, s_fm,
                         name="xqt")

    s_dq, s_dqch, s_dqa = emit_dequant4(c, "du", wug_in, wdq_d, 2 * DI, D)
    c.wait("sync", s_dq[0])
    c.wait("sync", s_dq[1])
    p_w_cm = nc.psum_tensor("p_w", [128, 2048], BF16)
    p_w = p_w_cm.__enter__()
    s_wt, _, s_wt_ev = emit_weight_transpose(
        c, "wt", wdq_d, wT_d, 2 * DI, D, ident, p_w, o_block=DI, in_f32=False)
    p_w_cm.__exit__(None, None, None)
    p_tr_cm.__exit__(None, None, None)
    psQ = [c.psum(f"psQ{i}", [128, 512], F32) for i in range(4)]

    c.wait("tensor", s_fm)
    c.wait("tensor", s_wt_ev)
    c.wait("sync", s_wt[0])
    c.wait("sync", s_wt[1])
    for ic in range(6):
        for w2 in range(2):
            sl = 2 * (ic % 2) + w2
            if ic >= 2:
                c.wait("sync", s_mm, (ic - 1) * 32)
            c.op("sync", lambda e, ic=ic, w2=w2, sl=sl: e.dma_start(
                out=wub[sl].ap().rearrange("p (ch j) -> p ch j", ch=NC),
                in_=wT_d[w2 * D:(w2 + 1) * D, ic * 512:(ic + 1) * 512]
                    .rearrange("(ch p) j -> p ch j", p=128)),
                inc=(s_wl[sl], 16))
        for t in range(NT):
            for w2 in range(2):
                gi = ic * 32 + t * 2 + w2
                sl = 2 * (ic % 2) + w2
                ps = psQ[2 * (t % 2) + w2]
                c.wait("tensor", s_wl[sl], 16 * (ic // 2 + 1))
                if gi >= 4:
                    c.wait("tensor", s_evq, gi - 3)
                    c.wait("tensor", s_pt, gi - 3)
                for ch in range(NC):
                    c.op("tensor", lambda e, ps=ps, ch=ch, t=t, sl=sl:
                         e.matmul(ps[:, :],
                                  xq_fm[:, ch * TSL + t * 128:ch * TSL + (t + 1) * 128],
                                  wub[sl][:, ch * 512:(ch + 1) * 512],
                                  start=(ch == 0), stop=(ch == NC - 1)),
                         inc=(s_mm, 1) if ch == NC - 1 else None)
                c.wait("scalar", s_mm)
                c.wait("scalar", s_st[2 * (t % 2) + w2],
                       s_st[2 * (t % 2) + w2].n)
                c.wait("scalar", s_pt, max(0, gi - 3))
                c.op("scalar", lambda e, ps=ps, sl2=2 * (t % 2) + w2:
                     e.activation(stg[sl2][:, :], ps[:, :], AF.Identity),
                     inc=(s_evq, 1))
                c.wait("vector", s_evq)
                c.op("vector", lambda e, sl2=2 * (t % 2) + w2,
                     col=w2 * 96 + ic * 16 + t:
                     e.tensor_reduce(parts[:, col:col + 1], stg[sl2][:, :],
                                     axis=AX, op=ALU.max,
                                     apply_absolute_value=True),
                     inc=(s_pt, 1))
                c.wait("sync", s_evq)
                dst = (up_out if w2 == 0 else gt_out)[
                    t * 128:(t + 1) * 128, ic * 512:(ic + 1) * 512]
                c.op("sync", lambda e, dst=dst, sl2=2 * (t % 2) + w2:
                     e.dma_start(out=dst, in_=stg[sl2][:, :]),
                     inc=(s_st[2 * (t % 2) + w2], 16))

    c.wait("vector", s_pt)
    c.chainop("vector", s_ch, lambda e: e.tensor_reduce(
        pfin[:, :], parts.ap().rearrange("p (w k) -> p w k", w=2), axis=AX,
        op=ALU.max, apply_absolute_value=True))
    c.wait("tensor", s_ch)
    c.op("tensor", lambda e: e.transpose(
        p_am[:, :], pfin[:, :], identf[0][:, :]), inc=(s_ch, 1))
    c.wait("vector", s_ch)
    c.chainop("vector", s_ch, lambda e: e.tensor_reduce(
        amr[:, :], p_am[:, :], axis=AX, op=ALU.max, apply_absolute_value=True))
    c.wait("sync", s_ch)
    c.op("sync", lambda e: e.dma_start(out=am23_out, in_=amr[:, :]),
         inc=(s_out, 16))
    c.wait("sync", s_st[0])
    c.wait("sync", s_st[1])
    c.wait("sync", s_st[2])
    c.wait("sync", s_st[3])
    c.wait("sync", s_out)

    c.build_block()
    c.close()
    return nc


def build_launch_H():
    nc = bass.Bass("TRN2", target_bir_lowering=False)
    up_in = nc.dram_tensor("up_raw", [TSL, DI], BF16, kind="ExternalInput").ap()
    gt_in = nc.dram_tensor("gate_raw", [TSL, DI], BF16, kind="ExternalInput").ap()
    am_in = nc.dram_tensor("amax23", [8, 2], F32, kind="ExternalInput").ap()
    upq_out = nc.dram_tensor("up_q", [TSL, DI], BF16, kind="ExternalOutput").ap()
    g2_out = nc.dram_tensor("gate2raw", [TSL, DI], BF16, kind="ExternalOutput").ap()
    am4_out = nc.dram_tensor("amax4", [1, 1], F32, kind="ExternalOutput").ap()

    c = Ctx(nc)
    identf = make_identity(c, "identf", F32)
    bc2, s_bc2 = emit_bcast_scales(c, "sc2", am_in[:, 0:1], 8, 32767.0, identf)
    bc3, s_bc3 = emit_bcast_scales(c, "sc3", am_in[:, 1:2], 8, 32767.0, identf)
    ldu = [c.sbuf(f"ldu{i}", [128, DI], BF16) for i in range(2)]
    ldg = [c.sbuf(f"ldg{i}", [128, DI], BF16) for i in range(2)]
    rb = mk_round_bufs(c, "rb", DI)
    sigb = c.sbuf("sigb", [128, DI], F32)
    g1b = c.sbuf("g1b", [128, DI], BF16)
    outb = [c.sbuf(f"outb{i}", [128, DI], BF16) for i in range(4)]
    parts = c.sbuf("parts", [128, NT], F32)
    pfin = c.sbuf("pfin", [128, 1], F32)
    amr = c.sbuf("amr", [1, 1], F32)
    p_am = c.psum("p_am", [1, 128], F32)

    s_lu = [c.sem(f"s_lu{i}") for i in range(2)]
    s_lg = [c.sem(f"s_lg{i}") for i in range(2)]
    s_ch = c.sem("s_ch")
    s_a = c.sem("s_a")
    s_st = [c.sem(f"s_sth{i}") for i in range(4)]
    s_out = c.sem("s_out")

    c.wait("scalar", s_bc2)
    c.wait("scalar", s_bc3)
    c.wait("vector", s_bc2)
    c.wait("vector", s_bc3)
    for t in range(NT):
        slu, slg = t % 2, t % 2
        if t >= 2:
            c.wait("sync", s_ch, s_ch.n)
        c.op("sync", lambda e, t=t, slu=slu: e.dma_start(
            out=ldu[slu][:, :], in_=up_in[t * 128:(t + 1) * 128, :]),
            inc=(s_lu[slu], 16))
        c.op("sync", lambda e, t=t, slg=slg: e.dma_start(
            out=ldg[slg][:, :], in_=gt_in[t * 128:(t + 1) * 128, :]),
            inc=(s_lg[slg], 16))
        c.wait("scalar", s_lg[slg], 16 * (t // 2 + 1))
        c.wait("vector", s_lg[slg], 16 * (t // 2 + 1))
        emit_round(c, s_a, ldg[slg][:, :], DI, rb, bc3[:, 2:3], s_ch)
        c.wait("scalar", s_ch)
        c.op("scalar", lambda e: e.activation(
            sigb[:, :], rb["r"][:, :], AF.Sigmoid, scale=bc3[:, 0:1]),
            inc=(s_a, 1))
        c.wait("scalar", s_a)
        c.op("scalar", lambda e: e.activation(
            g1b[:, :], rb["r"][:, :], AF.Identity, scale=bc3[:, 0:1]),
            inc=(s_a, 1))
        c.wait("scalar", s_a)
        c.op("scalar", lambda e: e.activation(
            rb["i"][:, :], sigb[:, :], AF.Identity, scale=-65536.0),
            inc=(s_a, 1))
        c.wait("scalar", s_a)
        c.op("scalar", lambda e: e.activation(
            rb["cf"][:, :], rb["i"][:, :], AF.Copy), inc=(s_a, 1))
        c.wait("vector", s_a)
        c.chainop("vector", s_ch, lambda e: e.scalar_tensor_tensor(
            rb["d"][:, :], sigb[:, :], -65536.0, rb["cf"][:, :],
            ALU.mult, ALU.subtract))
        c.chainop("vector", s_ch, lambda e: e.scalar_tensor_tensor(
            rb["a"][:, :], rb["d"][:, :], 0.5, rb["cf"][:, :],
            ALU.is_ge, ALU.add))
        c.chainop("vector", s_ch, lambda e: e.scalar_tensor_tensor(
            rb["r"][:, :], rb["d"][:, :], -0.5, rb["a"][:, :],
            ALU.is_le, ALU.subtract))
        c.chainop("vector", s_ch, lambda e: e.tensor_scalar_min(
            rb["r"][:, :], rb["r"][:, :], 65535.0))
        ob = outb[2 * (t % 2)]
        c.wait("vector", s_st[2 * (t % 2)], s_st[2 * (t % 2)].n)
        c.chainop("vector", s_ch, lambda e, ob=ob: e.scalar_tensor_tensor(
            ob[:, :], rb["r"][:, :], 1.0 / 65536.0, g1b[:, :],
            ALU.mult, ALU.mult))
        c.chainop("vector", s_ch, lambda e, ob=ob, t=t: e.tensor_reduce(
            parts[:, t:t + 1], ob[:, :], axis=AX, op=ALU.max,
            apply_absolute_value=True))
        c.wait("sync", s_ch)
        c.op("sync", lambda e, ob=ob, t=t: e.dma_start(
            out=g2_out[t * 128:(t + 1) * 128, :], in_=ob[:, :]),
            inc=(s_st[2 * (t % 2)], 16))
        c.wait("scalar", s_lu[slu], 16 * (t // 2 + 1))
        emit_round(c, s_a, ldu[slu][:, :], DI, rb, bc2[:, 2:3], s_ch)
        ob2 = outb[2 * (t % 2) + 1]
        c.wait("scalar", s_ch)
        c.wait("scalar", s_st[2 * (t % 2) + 1], s_st[2 * (t % 2) + 1].n)
        c.op("scalar", lambda e, ob2=ob2: e.activation(
            ob2[:, :], rb["r"][:, :], AF.Identity, scale=bc2[:, 0:1]),
            inc=(s_a, 1))
        c.wait("sync", s_a)
        c.op("sync", lambda e, ob2=ob2, t=t: e.dma_start(
            out=upq_out[t * 128:(t + 1) * 128, :], in_=ob2[:, :]),
            inc=(s_st[2 * (t % 2) + 1], 16))
        c.wait("vector", s_a)

    c.chainop("vector", s_ch, lambda e: e.tensor_reduce(
        pfin[:, :], parts[:, :], axis=AX, op=ALU.max,
        apply_absolute_value=True))
    c.wait("tensor", s_ch)
    c.wait("tensor", identf[1])
    c.op("tensor", lambda e: e.transpose(
        p_am[:, :], pfin[:, :], identf[0][:, :]), inc=(s_ch, 1))
    c.wait("vector", s_ch)
    c.chainop("vector", s_ch, lambda e: e.tensor_reduce(
        amr[:, :], p_am[:, :], axis=AX, op=ALU.max, apply_absolute_value=True))
    c.wait("sync", s_ch)
    c.op("sync", lambda e: e.dma_start(out=am4_out, in_=amr[:, :]),
         inc=(s_out, 16))
    for ss in s_st:
        c.wait("sync", ss)
    c.wait("sync", s_out)

    c.build_block()
    c.close()
    return nc


def build_launch_I():
    nc = bass.Bass("TRN2", target_bir_lowering=False)
    g2_in = nc.dram_tensor("gate2raw", [TSL, DI], BF16, kind="ExternalInput").ap()
    up_in = nc.dram_tensor("up_q", [TSL, DI], BF16, kind="ExternalInput").ap()
    am_in = nc.dram_tensor("amax4", [8, 1], F32, kind="ExternalInput").ap()
    o_out = nc.dram_tensor("o_raw", [TSL, DI], BF16, kind="ExternalOutput").ap()
    am5_out = nc.dram_tensor("amax5", [1, 1], F32, kind="ExternalOutput").ap()

    c = Ctx(nc)
    identf = make_identity(c, "identf", F32)
    bc4, s_bc4 = emit_bcast_scales(c, "sc4", am_in, 8, 32767.0, identf)
    ldg = [c.sbuf(f"ldg{i}", [128, DI], BF16) for i in range(2)]
    ldu = [c.sbuf(f"ldu{i}", [128, DI], BF16) for i in range(2)]
    rb = mk_round_bufs(c, "rb", DI)
    outb = [c.sbuf(f"outb{i}", [128, DI], BF16) for i in range(2)]
    parts = c.sbuf("parts", [128, NT], F32)
    pfin = c.sbuf("pfin", [128, 1], F32)
    amr = c.sbuf("amr", [1, 1], F32)
    p_am = c.psum("p_am", [1, 128], F32)

    s_lg = [c.sem(f"s_lg{i}") for i in range(2)]
    s_lu = [c.sem(f"s_lu{i}") for i in range(2)]
    s_ch = c.sem("s_ch")
    s_a = c.sem("s_a")
    s_st = [c.sem(f"s_sti{i}") for i in range(2)]
    s_out = c.sem("s_out")

    c.wait("scalar", s_bc4)
    c.wait("vector", s_bc4)
    for t in range(NT):
        sl = t % 2
        if t >= 2:
            c.wait("sync", s_ch, s_ch.n)
        c.op("sync", lambda e, t=t, sl=sl: e.dma_start(
            out=ldg[sl][:, :], in_=g2_in[t * 128:(t + 1) * 128, :]),
            inc=(s_lg[sl], 16))
        c.op("sync", lambda e, t=t, sl=sl: e.dma_start(
            out=ldu[sl][:, :], in_=up_in[t * 128:(t + 1) * 128, :]),
            inc=(s_lu[sl], 16))
        c.wait("scalar", s_lg[sl], 16 * (t // 2 + 1))
        c.wait("vector", s_lg[sl], 16 * (t // 2 + 1))
        emit_round(c, s_a, ldg[sl][:, :], DI, rb, bc4[:, 2:3], s_ch)
        c.wait("vector", s_lu[sl], 16 * (t // 2 + 1))
        ob = outb[t % 2]
        c.wait("vector", s_st[t % 2], s_st[t % 2].n)
        c.chainop("vector", s_ch, lambda e, ob=ob, sl=sl:
                  e.scalar_tensor_tensor(ob[:, :], rb["r"][:, :], bc4[:, 0:1],
                                         ldu[sl][:, :], ALU.mult, ALU.mult))
        c.chainop("vector", s_ch, lambda e, ob=ob, t=t: e.tensor_reduce(
            parts[:, t:t + 1], ob[:, :], axis=AX, op=ALU.max,
            apply_absolute_value=True))
        c.wait("sync", s_ch)
        c.op("sync", lambda e, ob=ob, t=t: e.dma_start(
            out=o_out[t * 128:(t + 1) * 128, :], in_=ob[:, :]),
            inc=(s_st[t % 2], 16))

    c.chainop("vector", s_ch, lambda e: e.tensor_reduce(
        pfin[:, :], parts[:, :], axis=AX, op=ALU.max,
        apply_absolute_value=True))
    c.wait("tensor", s_ch)
    c.wait("tensor", identf[1])
    c.op("tensor", lambda e: e.transpose(
        p_am[:, :], pfin[:, :], identf[0][:, :]), inc=(s_ch, 1))
    c.wait("vector", s_ch)
    c.chainop("vector", s_ch, lambda e: e.tensor_reduce(
        amr[:, :], p_am[:, :], axis=AX, op=ALU.max, apply_absolute_value=True))
    c.wait("sync", s_ch)
    c.op("sync", lambda e: e.dma_start(out=am5_out, in_=amr[:, :]),
         inc=(s_out, 16))
    c.wait("sync", s_st[0])
    c.wait("sync", s_st[1])
    c.wait("sync", s_out)

    c.build_block()
    c.close()
    return nc


def build_launch_J():
    nc = bass.Bass("TRN2", target_bir_lowering=False)
    o_in = nc.dram_tensor("o_raw", [TSL, DI], BF16, kind="ExternalInput").ap()
    am_in = nc.dram_tensor("amax5", [8, 1], F32, kind="ExternalInput").ap()
    x_in = nc.dram_tensor("x3", [TSL, D], F32, kind="ExternalInput").ap()
    wd_in = nc.dram_tensor("w_down", [D, DI], F32, kind="ExternalInput").ap()
    x4_out = nc.dram_tensor("x4", [TSL, D], F32, kind="ExternalOutput").ap()
    wdq_d = nc.dram_tensor("wdq", [D, DI], BF16, kind="Internal").ap()
    wT_d = nc.dram_tensor("wT", [DI, D], BF16, kind="Internal").ap()

    c = Ctx(nc)
    identf = make_identity(c, "identf", F32)
    ident = make_identity(c, "ident", BF16)
    bc5, s_bc5 = emit_bcast_scales(c, "sc5", am_in, 8, 32767.0, identf)
    ldo = [c.sbuf(f"ldo{i}", [128, DI], BF16) for i in range(2)]
    rb = mk_round_bufs(c, "rb", DI)
    o_b = c.sbuf("o_b", [128, DI], BF16)
    o_fm = [c.sbuf(f"o_fm{i}", [128, DI // 128 * 128], BF16) for i in range(2)]
    wdT = c.sbuf("wdT", [128, (DI // 128) * 1024], BF16)
    x_t = [c.sbuf(f"x_t{i}", [128, 1024], F32) for i in range(2)]

    s_lo = [c.sem(f"s_lo{i}") for i in range(2)]
    s_ch = c.sem("s_ch")
    s_a = c.sem("s_a")
    s_ob = c.sem("s_ob")
    s_tp = c.sem("s_tp")
    s_ofm = c.sem("s_ofm")
    s_mm = c.sem("s_mm")
    s_wl = c.sem("s_wl")
    s_xl = [c.sem(f"s_xl{i}") for i in range(2)]
    s_xo = [c.sem(f"s_xo{i}") for i in range(2)]
    s_fin = c.sem("s_fin")

    s_dq, s_dqch, s_dqa = emit_dequant4(c, "dd", wd_in, wdq_d, D, DI,
                                        rbufs=rb, nld=1)
    c.wait("sync", s_dq[0])
    c.wait("sync", s_dq[1])
    c.wait("scalar", s_dqch)
    c.wait("vector", s_dqch)
    c.wait("scalar", s_dqa)
    c.wait("vector", s_dqa)
    p_w_cm = nc.psum_tensor("p_w", [128, 2048], BF16)
    p_w = p_w_cm.__enter__()
    s_wt, _, s_wt_ev = emit_weight_transpose(
        c, "wt", wdq_d, wT_d, D, DI, ident, p_w, in_f32=False)
    c.wait("sync", s_wt[0])
    c.wait("sync", s_wt[1])
    c.op("sync", lambda e: e.dma_start(
        out=wdT.ap().rearrange("p (ic j) -> p ic j", ic=DI // 128),
        in_=wT_d.rearrange("(ic p) j -> p ic j", p=128)), inc=(s_wl, 16))
    p_w_cm.__exit__(None, None, None)
    p_tr_cm = nc.psum_tensor("p_tr", [128, 2048], BF16)
    p_tr = p_tr_cm.__enter__()
    p_dn_cm = [nc.psum_tensor(f"p_dn{i}", [128, 1024], F32) for i in range(2)]
    p_dn = [p.__enter__() for p in p_dn_cm]

    c.wait("scalar", s_bc5)
    c.wait("vector", s_bc5)
    c.wait("tensor", s_wt_ev)
    c.wait("tensor", ident[1])
    for t in range(NT):
        sl = t % 2
        if t >= 2:
            c.wait("sync", s_ch, s_ch.n)
        c.op("sync", lambda e, t=t, sl=sl: e.dma_start(
            out=ldo[sl][:, :], in_=o_in[t * 128:(t + 1) * 128, :]),
            inc=(s_lo[sl], 16))
        c.wait("scalar", s_lo[sl], 16 * (t // 2 + 1))
        c.wait("vector", s_lo[sl], 16 * (t // 2 + 1))
        emit_round(c, s_a, ldo[sl][:, :], DI, rb, bc5[:, 2:3], s_ch)
        c.wait("scalar", s_ch)
        if t >= 1:
            c.wait("scalar", s_tp, t)
        c.op("scalar", lambda e: e.activation(
            o_b[:, :], rb["r"][:, :], AF.Identity, scale=bc5[:, 0:1]),
            inc=(s_ob, 1))
        c.wait("vector", s_ob)
        c.wait("tensor", s_ob)
        for g in range(3):
            gg = t * 3 + g
            half = (gg % 2) * 1024
            if gg >= 2:
                c.wait("tensor", s_ofm, gg - 1)
            for icc in range(8):
                c.op("tensor", lambda e, g=g, icc=icc, half=half: e.transpose(
                    p_tr[:, half + icc * 128:half + (icc + 1) * 128],
                    o_b[:, (g * 8 + icc) * 128:(g * 8 + icc + 1) * 128],
                    ident[0][:, :]), inc=(s_tp, 1) if icc == 7 else None)
            c.wait("vector", s_tp)
            if t >= 2:
                c.wait("vector", s_mm, t - 1)
            c.op("vector", lambda e, sl=sl, g=g, half=half: e.tensor_copy(
                o_fm[sl][:, g * 1024:(g + 1) * 1024], p_tr[:, half:half + 1024]),
                inc=(s_ofm, 1))
        c.wait("tensor", s_ofm)
        c.wait("tensor", s_wl)
        pd = p_dn[t % 2]
        if t >= 2:
            c.wait("tensor", s_fin, 2 * (t - 1))
        for icc in range(24):
            for hh in range(2):
                last = (icc == 23 and hh == 1)
                c.op("tensor", lambda e, pd=pd, icc=icc, hh=hh, sl=sl:
                     e.matmul(pd[:, hh * 512:(hh + 1) * 512],
                              o_fm[sl][:, icc * 128:(icc + 1) * 128],
                              wdT[:, icc * 1024 + hh * 512:icc * 1024 + (hh + 1) * 512],
                              start=(icc == 0), stop=(icc == 23)),
                     inc=(s_mm, 1) if last else None)
        xt = x_t[t % 2]
        if t >= 2:
            c.wait("sync", s_xo[t % 2], 16 * (t // 2))
        c.op("sync", lambda e, t=t, xt=xt: e.dma_start(
            out=xt[:, :], in_=x_in[t * 128:(t + 1) * 128, :]),
            inc=(s_xl[t % 2], 16))
        c.wait("vector", s_xl[t % 2], 16 * (t // 2 + 1))
        c.wait("vector", s_mm)
        c.op("vector", lambda e, xt=xt, pd=pd: e.scalar_tensor_tensor(
            xt[:, :], pd[:, :], 1.0, xt[:, :], ALU.mult, ALU.add),
            inc=(s_fin, 2))
        c.wait("sync", s_fin)
        c.op("sync", lambda e, t=t, xt=xt: e.dma_start(
            out=x4_out[t * 128:(t + 1) * 128, :], in_=xt[:, :]),
            inc=(s_xo[t % 2], 16))

    c.wait("sync", s_xo[0])
    c.wait("sync", s_xo[1])

    p_dn_cm[1].__exit__(None, None, None)
    p_dn_cm[0].__exit__(None, None, None)
    p_tr_cm.__exit__(None, None, None)
    c.build_block()
    c.close()
    return nc


def _split_tokens(x):
    return [np.ascontiguousarray(x[i * TSL:(i + 1) * TSL]) for i in range(NCORES)]


def _phase_ln(c, pfx, x_dram, sems_unused, ident, eps_ap, h_tm, h_fm, p_tr,
              scope):
    x_sb = scope.__self__ if False else c.sbuf(pfx + "_x_sb", [128, NT * 1024], F32)
    scr = c.sbuf(pfx + "_scr", [128, 2048], BF16)
    stats = c.sbuf(pfx + "_stats", [128, 8 * NT], F32)
    s_x = c.sem("ln_x")
    s_h = c.sem("ln_h")
    s_fm = c.sem("ln_fm")
    c.op("sync", lambda e: e.dma_start(
        out=x_sb.ap().rearrange("p (t d) -> p t d", t=NT),
        in_=x_dram.rearrange("(t p) d -> p t d", p=128)), inc=(s_x, 16))
    emit_ln(c, "lnP", x_sb, NT, h_tm, scr, stats, ident, s_h, (s_x, s_x.n),
            eps_ap, h_fm=h_fm, p_tr=p_tr, s_fm=s_fm)
    return s_h, s_fm, x_sb


def _phase_attn0(c, x_in, hfm_g, htm_g, W_in, x1_out, E_dram, wsum_d, hq_fm,
                 ident, consts, s_hq, tile_cb=None, wsum_pack=None):
    NKB = T // 1024
    x_t = [c.sbuf(f"b_x_t{i}", [128, 1024], F32) for i in range(2)]
    kfm = [c.sbuf(f"b_kfm{i}", [128, NC * 1024], BF16) for i in range(2)]
    ktm = [c.sbuf(f"b_ktm{i}", [128, 8 * 1024], BF16) for i in range(2)]
    Escr = [c.sbuf(f"b_Escr{i}", [128, 1024], BF16) for i in range(2)]
    Embuf = [c.sbuf(f"b_Embuf{i}", [128, 1024], BF16) for i in range(4)]
    Emm = [c.sbuf(f"b_Emm{i}", [128, 1024], BF16) for i in range(2)]
    EmT = [c.sbuf(f"b_EmT{i}", [128, 1024], BF16) for i in range(2)]
    acc = c.sbuf("b_acc", [128, 8 * 1024], F32)
    Zparts = c.sbuf("b_Zparts", [128, NT * NKB], F32)
    SpParts = c.sbuf("b_SpParts", [128, NT * NKB], F32)
    zstat = c.sbuf("b_zstat", [128, 6 * NT], F32)
    psA = c.psum("b_psA", [128, 1024], F32)
    psB = c.psum("b_psB", [128, 1024], F32)
    p_tr = c.psum("b_p_tr", [128, 2048], BF16)

    Z = zstat[:, 0:NT]
    theta = zstat[:, NT:2 * NT]
    invZ = zstat[:, 2 * NT:3 * NT]
    sp = zstat[:, 3 * NT:4 * NT]
    dinv = zstat[:, 4 * NT:5 * NT]
    coef2 = zstat[:, 5 * NT:6 * NT]

    s_xl = [c.sem(f"a_xl{i}") for i in range(2)]
    s_xo = [c.sem(f"a_xo{i}") for i in range(2)]
    s_kf = [c.sem(f"a_kf{i}") for i in range(2)]
    s_kt = [c.sem(f"a_kt{i}") for i in range(2)]
    s_est = [c.sem(f"a_est{i}") for i in range(2)]
    s_eld = [c.sem(f"a_eld{i}") for i in range(4)]
    s_sc = c.sem("a_sc")
    s_exp = c.sem("a_exp")
    s_msk = c.sem("a_msk")
    s_tp = c.sem("a_tp")
    s_ev = c.sem("a_ev")
    s_av = c.sem("a_av")
    s_ac = c.sem("a_ac")
    s_fin = c.sem("a_fin")
    B = {id(x): x.n for x in [s_sc, s_exp, s_msk, s_tp, s_ev, s_av, s_ac,
                              s_fin] + s_xl + s_xo + s_kf + s_kt + s_est + s_eld}

    if wsum_pack is None:
        wsum_rep, s_wr, wsum_mm_cb = emit_wsum_rep(
            c, W_in, wsum_d, p_bf="defer", ld_sem_names=("a_eld0", "a_eld1"))
    else:
        wsum_rep, s_wr, wsum_mm_cb = wsum_pack

    c.wait("scalar", consts.sem)
    c.wait("tensor", s_hq, s_hq.n)
    for kb in range(NKB):
        if kb >= 2:
            c.wait("sync", s_sc, B[id(s_sc)] + 16 * (kb - 1))
        c.op("sync", lambda e, kb=kb: e.dma_start(
            out=kfm[kb % 2].ap().rearrange("p (ch s) -> p ch s", ch=NC),
            in_=hfm_g[(kb // 2) * 1024:(kb // 2 + 1) * 1024,
                      (kb % 2) * 1024:(kb % 2 + 1) * 1024]
                .rearrange("(ch p) s -> p ch s", p=128)), inc=(s_kf[kb % 2], 16))
        for qt in range(NT):
            i = kb * NT + qt
            ps = psA if i % 2 == 0 else psB
            c.wait("tensor", s_kf[kb % 2], s_kf[kb % 2].n)
            if i >= 2:
                c.wait("tensor", s_exp, B[id(s_exp)] + i - 1)
            for ch in range(NC):
                for hh in range(2):
                    last = (ch == NC - 1 and hh == 1)
                    c.op("tensor", lambda e, ps=ps, ch=ch, hh=hh, qt=qt, kb=kb:
                         e.matmul(ps[:, hh * 512:(hh + 1) * 512],
                                  hq_fm[:, ch * TSL + qt * 128:ch * TSL + (qt + 1) * 128],
                                  kfm[kb % 2][:, ch * 1024 + hh * 512:ch * 1024 + (hh + 1) * 512],
                                  start=(ch == 0), stop=(ch == NC - 1)),
                         inc=(s_sc, 1) if last else None)
            c.wait("scalar", s_sc)
            if i >= 2:
                c.wait("scalar", s_est[i % 2], B[id(s_est[i % 2])] + 16 * (i // 2))
            c.op("scalar", lambda e, ps=ps, i=i: e.activation(
                Escr[i % 2][:, :], ps[:, :], AF.Exp, bias=consts[-32.0],
                scale=1.0 / 32.0, accum_out=Zparts[:, i:i + 1]),
                inc=(s_exp, 1))
            c.wait("gpsimd", s_exp)
            c.op("gpsimd", lambda e, i=i, qt=qt, kb=kb: e.dma_start(
                out=E_dram[qt * 128:(qt + 1) * 128, kb * 1024:(kb + 1) * 1024],
                in_=Escr[i % 2][:, :]), inc=(s_est[i % 2], 16))
            if tile_cb is not None:
                tile_cb(i)

    c.wait("tensor", s_exp)
    wsum_mm_cb(psA)
    c.wait("vector", s_exp)
    c.op("vector", lambda e: e.reduce_sum(
        Z, Zparts.ap().rearrange("p (kb qt) -> p qt kb", qt=NT), axis=AX),
        inc=(s_fin, 1))
    c.chainop("vector", s_fin, lambda e: e.tensor_scalar_mul(theta, Z, 0.01))
    c.chainop("vector", s_fin, lambda e: e.reciprocal(invZ, Z))
    bfin3 = s_fin.n

    c.wait("tensor", s_exp)
    c.wait("tensor", ident[1])
    c.wait("tensor", s_wr, s_wr.n)

    def emit_av(a):
        j1 = a // 8
        ps = psA if a % 2 == 0 else psB
        c.wait("tensor", s_ev, B[id(s_ev)] + a + 1)
        c.wait("tensor", s_kt[j1 % 2], s_kt[j1 % 2].n)
        if a >= 2:
            c.wait("tensor", s_ac, B[id(s_ac)] + a - 1)
        for sc in range(8):
            for hh in range(2):
                last = (sc == 7 and hh == 1)
                c.op("tensor", lambda e, ps=ps, sc=sc, hh=hh, a=a, j1=j1:
                     e.matmul(ps[:, hh * 512:(hh + 1) * 512],
                              EmT[a % 2][:, sc * 128:(sc + 1) * 128],
                              ktm[j1 % 2][:, sc * 1024 + hh * 512:sc * 1024 + (hh + 1) * 512],
                              start=(sc == 0), stop=(sc == 7)),
                     inc=(s_av, 1) if last else None)

    def emit_add(a):
        ps = psA if a % 2 == 0 else psB
        ti_a = a % 8
        kb_a = (a // 8) % NKB
        c.wait("vector", s_av, B[id(s_av)] + a + 1)
        if kb_a == 0:
            c.wait("vector", s_fin, s_fin.n)
        c.op("vector", lambda e, ps=ps, ti_a=ti_a, kb_a=kb_a: (
            e.tensor_copy(acc[:, ti_a * 1024:(ti_a + 1) * 1024], ps[:, :])
            if kb_a == 0 else
            e.tensor_add(acc[:, ti_a * 1024:(ti_a + 1) * 1024],
                         acc[:, ti_a * 1024:(ti_a + 1) * 1024], ps[:, :])),
            inc=(s_ac, 1))

    for g in range(2):
        for kb in range(NKB):
            j = g * NKB + kb
            if j >= 2:
                c.wait("sync", s_av, B[id(s_av)] + (j - 1) * 8)
            c.op("sync", lambda e, kb=kb, j=j: e.dma_start(
                out=ktm[j % 2].ap().rearrange("p (sc f) -> p sc f", sc=8),
                in_=htm_g[kb * 1024:(kb + 1) * 1024, :]
                    .rearrange("(sc p) f -> p sc f", p=128)),
                inc=(s_kt[j % 2], 16))
            for ti in range(8):
                qt = g * 8 + ti
                i = j * 8 + ti
                ist = kb * NT + qt
                c.wait("sync", s_est[ist % 2],
                       B[id(s_est[ist % 2])] + 16 * (ist // 2 + 1))
                if i >= 4:
                    c.wait("sync", s_msk, B[id(s_msk)] + i - 3)
                c.op("sync", lambda e, qt=qt, kb=kb, i=i: e.dma_start(
                    out=Embuf[i % 4][:, :],
                    in_=E_dram[qt * 128:(qt + 1) * 128, kb * 1024:(kb + 1) * 1024]),
                    inc=(s_eld[i % 4], 16))
                c.wait("vector", s_eld[i % 4])
                c.wait("vector", s_fin, bfin3)
                if i >= 2:
                    c.wait("vector", s_tp, B[id(s_tp)] + i - 1)
                c.op("vector", lambda e, i=i, qt=qt, kb=kb: e.scalar_tensor_tensor(
                    Emm[i % 2][:, :], Embuf[i % 4][:, :], theta[:, qt:qt + 1],
                    Embuf[i % 4][:, :], ALU.is_ge, ALU.mult,
                    accum_out=SpParts[:, kb * NT + qt:kb * NT + qt + 1]),
                    inc=(s_msk, 1))
                c.wait("tensor", s_msk)
                if i >= 2:
                    c.wait("tensor", s_ev, B[id(s_ev)] + i - 1)
                for ch in range(8):
                    c.op("tensor", lambda e, i=i, ch=ch: e.transpose(
                        p_tr[:, (i % 2) * 1024 + ch * 128:(i % 2) * 1024 + (ch + 1) * 128],
                        Emm[i % 2][:, ch * 128:(ch + 1) * 128], ident[0][:, :]),
                        inc=(s_tp, 1) if ch == 7 else None)
                c.wait("scalar", s_tp, B[id(s_tp)] + i + 1)
                if i >= 2:
                    c.wait("scalar", s_av, B[id(s_av)] + i - 1)
                c.op("scalar", lambda e, i=i: e.activation(
                    EmT[i % 2][:, :], p_tr[:, (i % 2) * 1024:(i % 2 + 1) * 1024],
                    AF.Identity), inc=(s_ev, 1))
                if i % 128 >= 1:
                    emit_av(i - 1)
                if i % 128 >= 2:
                    emit_add(i - 2)
        emit_av(g * 128 + 127)
        emit_add(g * 128 + 126)
        emit_add(g * 128 + 127)

        c.wait("vector", s_msk)
        c.wait("vector", s_ac)
        c.op("vector", lambda e, g=g: e.reduce_sum(
            sp[:, g * 8:(g + 1) * 8],
            SpParts.ap().rearrange("p (kb qt) -> p qt kb", qt=NT)[:, g * 8:(g + 1) * 8, :],
            axis=AX), inc=(s_fin, 1))
        c.chainop("vector", s_fin, lambda e, g=g: e.scalar_tensor_tensor(
            dinv[:, g * 8:(g + 1) * 8], Z[:, g * 8:(g + 1) * 8], 1e-9,
            sp[:, g * 8:(g + 1) * 8], ALU.mult, ALU.add))
        c.chainop("vector", s_fin, lambda e, g=g: e.reciprocal(
            dinv[:, g * 8:(g + 1) * 8], dinv[:, g * 8:(g + 1) * 8]))
        c.chainop("vector", s_fin, lambda e, g=g: e.tensor_sub(
            coef2[:, g * 8:(g + 1) * 8], Z[:, g * 8:(g + 1) * 8],
            sp[:, g * 8:(g + 1) * 8]))
        c.chainop("vector", s_fin, lambda e, g=g: e.tensor_scalar_mul(
            coef2[:, g * 8:(g + 1) * 8], coef2[:, g * 8:(g + 1) * 8], 0.01))
        c.chainop("vector", s_fin, lambda e, g=g: e.tensor_tensor(
            coef2[:, g * 8:(g + 1) * 8], coef2[:, g * 8:(g + 1) * 8],
            invZ[:, g * 8:(g + 1) * 8], ALU.mult))
        c.wait("vector", s_wr, s_wr.n)
        for ti in range(8):
            qt = g * 8 + ti
            xt = x_t[qt % 2]
            if qt >= 2:
                c.wait("sync", s_xo[qt % 2], B[id(s_xo[qt % 2])] + 16 * (qt // 2))
            c.op("sync", lambda e, qt=qt, xt=xt: e.dma_start(
                out=xt[:, :], in_=x_in[qt * 128:(qt + 1) * 128, :]),
                inc=(s_xl[qt % 2], 16))
            c.wait("vector", s_xl[qt % 2])
            c.chainop("vector", s_fin, lambda e, xt=xt, ti=ti, qt=qt:
                      e.scalar_tensor_tensor(
                          xt[:, :], acc[:, ti * 1024:(ti + 1) * 1024],
                          dinv[:, qt:qt + 1], xt[:, :], ALU.mult, ALU.add))
            c.chainop("vector", s_fin, lambda e, xt=xt, qt=qt:
                      e.scalar_tensor_tensor(
                          xt[:, :], wsum_rep[:, :], coef2[:, qt:qt + 1], xt[:, :],
                          ALU.mult, ALU.add))
            c.wait("sync", s_fin)
            c.op("sync", lambda e, qt=qt, xt=xt: e.dma_start(
                out=x1_out[qt * 128:(qt + 1) * 128, :], in_=xt[:, :]),
                inc=(s_xo[qt % 2], 16))
    c.wait("sync", s_xo[0])
    c.wait("sync", s_xo[1])


def _phase_moe(c, x1_in, gw_in, ewT_d, x2_out, ident, consts, s_ewt_st,
               s_ewt_ev):
    x_sb = c.sbuf("c_x_sb", [128, NT * 1024], F32)
    h_tm = c.sbuf("c_h_tm", [128, NT * 1024], BF16)
    h_fm = c.sbuf("c_h_fm", [128, NC * TSL], BF16)
    scr = c.sbuf("c_scr", [128, 2048], BF16)
    stats = c.sbuf("c_stats", [128, 8 * NT], F32)
    gwT = c.sbuf("c_gwT", [128, NC * 8], BF16)
    gwb = c.sbuf("c_gwb", [8, D], F32)
    gwb16 = c.sbuf("c_gwb16", [8, D], BF16)
    mask8 = c.sbuf("c_mask8", [128, NT * 8], F32)
    gmax = c.sbuf("c_gmax", [128, NT], F32)
    wq = [c.sbuf(f"c_wq{e}", [128, NC * 256], BF16) for e in range(8)]
    accq = c.sbuf("c_accq", [128, NT * 256], BF16)

    s_x = c.sem("ln_x")
    s_h = c.sem("ln_h")
    s_fm = c.sem("ln_fm")
    s_gw = c.sem("c_gw")
    s_gt = c.sem("c_gt")
    s_wq1 = c.sem("c_wq")
    s_wq = [s_wq1] * 8
    s_pe = c.sem("c_pe")
    s_evq = c.sem("c_evq")
    s_fin = c.sem("c_fin")
    s_out = c.sem("c_out")
    B = {id(x): x.n for x in [s_fm, s_gt, s_pe, s_evq, s_fin]}

    p_tr_cm = c.nc.psum_tensor("c_p_tr", [128, 2048], BF16)
    p_tr = p_tr_cm.__enter__()
    p_g_cm = c.nc.psum_tensor("c_p_g", [128, 8], F32)
    p_g = p_g_cm.__enter__()

    c.op("sync", lambda e: e.dma_start(
        out=x_sb.ap().rearrange("p (t d) -> p t d", t=NT),
        in_=x1_in.rearrange("(t p) d -> p t d", p=128)), inc=(s_x, 16))
    c.wait("scalar", consts.sem)
    emit_ln(c, "lnP", x_sb, NT, h_tm, scr, stats, ident, s_h, (s_x, s_x.n),
            consts[EPS], h_fm=h_fm, p_tr=p_tr, s_fm=s_fm)

    c.op("sync", lambda e: e.dma_start(out=gwb[:, :], in_=gw_in),
         inc=(s_gw, 16))
    c.wait("vector", s_gw)
    c.op("vector", lambda e: e.tensor_copy(gwb16[:, :], gwb[:, :]),
         inc=(s_gw, 1))
    c.wait("tensor", s_gw)
    c.wait("tensor", ident[1])
    c.wait("tensor", s_fm)
    for ch in range(NC):
        c.op("tensor", lambda e, ch=ch: e.transpose(
            p_tr[:, ch * 128:ch * 128 + 8],
            gwb16[:, ch * 128:(ch + 1) * 128], ident[0][0:8, 0:8]),
            inc=(s_gw, 1) if ch == NC - 1 else None)
    c.wait("vector", s_gw)
    c.op("vector", lambda e: e.tensor_copy(
        gwT.ap().rearrange("p (ch j) -> p ch j", ch=NC),
        p_tr.ap().rearrange("p (ch j) -> p ch j", ch=16)[:, 0:NC, 0:8]),
        inc=(s_gw, 1))

    for t in range(NT):
        c.wait("tensor", s_gw)
        c.wait("tensor", s_fm, B[id(s_fm)] + t + 1)
        if t >= 1:
            c.wait("tensor", s_gt, B[id(s_gt)] + 2 * t)
        for ch in range(NC):
            c.op("tensor", lambda e, ch=ch, t=t: e.matmul(
                p_g[:, :], h_fm[:, ch * TSL + t * 128:ch * TSL + (t + 1) * 128],
                gwT[:, ch * 8:(ch + 1) * 8],
                start=(ch == 0), stop=(ch == NC - 1)),
                inc=(s_pe, 1) if ch == NC - 1 else None)
        c.wait("vector", s_pe)
        c.op("vector", lambda e, t=t: e.reduce_max(
            gmax[:, t:t + 1], p_g[:, :], axis=AX), inc=(s_gt, 1))
        c.wait("vector", s_gt)
        c.op("vector", lambda e, t=t: e.tensor_single_scalar(
            mask8[:, t * 8:(t + 1) * 8], p_g[:, :], gmax[:, t:t + 1],
            ALU.is_ge), inc=(s_gt, 1))

    p_g_cm.__exit__(None, None, None)
    p_tr_cm.__exit__(None, None, None)
    psE = [c.psum(f"c_psE{i}", [128, 8 * 256], F32) for i in range(2)]
    c.wait("tensor", s_gt)
    c.wait("tensor", s_gw)
    c.wait("tensor", s_fm)
    c.wait("tensor", s_ewt_ev)

    for fq in range(4):
        for e8 in range(8):
            if fq >= 1:
                c.wait("sync", s_pe, B[id(s_pe)] + NT + fq * NT)
            c.wait("sync", s_ewt_st[0], s_ewt_st[0].n)
            c.wait("sync", s_ewt_st[1], s_ewt_st[1].n)
            c.wait("sync", s_wq1, s_wq1.n)
            c.op("sync", lambda e, e8=e8, fq=fq: e.dma_start(
                out=wq[e8].ap().rearrange("p (ch j) -> p ch j", ch=NC),
                in_=ewT_d[e8 * D:(e8 + 1) * D, fq * 256:(fq + 1) * 256]
                    .rearrange("(ch p) j -> p ch j", p=128)),
                inc=(s_wq[e8], 16))
        for t in range(NT):
            gi = fq * NT + t
            grp = psE[gi % 2]
            c.wait("tensor", s_wq1, s_wq1.n)
            c.wait("tensor", s_ewt_st[0], s_ewt_st[0].n)
            c.wait("tensor", s_ewt_st[1], s_ewt_st[1].n)
            if gi >= 2:
                c.wait("tensor", s_evq, B[id(s_evq)] + (gi - 1) * 8)
            for e8 in range(8):
                for ch in range(NC):
                    last = (ch == NC - 1 and e8 == 7)
                    c.op("tensor", lambda e, grp=grp, ch=ch, e8=e8, t=t:
                         e.matmul(grp[:, e8 * 256:(e8 + 1) * 256],
                                  h_fm[:, ch * TSL + t * 128:ch * TSL + (t + 1) * 128],
                                  wq[e8][:, ch * 256:(ch + 1) * 256],
                                  start=(ch == 0), stop=(ch == NC - 1)),
                         inc=(s_pe, 1) if last else None)
            c.wait("vector", s_pe)
            for e8 in range(8):
                aq = accq[:, t * 256:(t + 1) * 256]
                if e8 == 0:
                    if fq >= 1:
                        c.wait("vector", s_fin,
                               B[id(s_fin)] + 2 * ((fq - 1) * NT + t + 1))
                    c.op("vector", lambda e, grp=grp, aq=aq, t=t:
                         e.tensor_single_scalar(
                             aq, grp[:, 0:256], mask8[:, t * 8:t * 8 + 1],
                             ALU.mult), inc=(s_evq, 1))
                else:
                    c.wait("vector", s_evq)
                    c.op("vector", lambda e, grp=grp, aq=aq, t=t, e8=e8:
                         e.scalar_tensor_tensor(
                             aq, grp[:, e8 * 256:(e8 + 1) * 256],
                             mask8[:, t * 8 + e8:t * 8 + e8 + 1], aq,
                             ALU.mult, ALU.add), inc=(s_evq, 1))
            xq = x_sb[:, t * 1024 + fq * 256:t * 1024 + (fq + 1) * 256]
            hq = h_tm[:, t * 1024 + fq * 256:t * 1024 + (fq + 1) * 256]
            c.wait("vector", s_evq)
            c.op("vector", lambda e, xq=xq, hq=hq: e.tensor_add(xq, xq, hq),
                 inc=(s_fin, 1))
            c.chainop("vector", s_fin, lambda e, xq=xq, t=t, fq=fq: e.tensor_add(
                xq, xq, accq[:, t * 256:(t + 1) * 256]))

    c.wait("sync", s_fin)
    c.op("sync", lambda e: e.dma_start(
        out=x2_out.rearrange("(t p) d -> p t d", p=128),
        in_=x_sb.ap().rearrange("p (t d) -> p t d", t=NT)), inc=(s_out, 16))
    c.wait("sync", s_out)
    for cm in [psE[1], psE[0]]:
        pass


def _phase_proj(c, h_fm, s_fm_done, wT_d, s_wt, s_wt_ev, qfm_out, akfm_out,
                utm_out, kscr):
    wb = [c.sbuf(f"d_wb{i}", [128, NC * 1024], BF16) for i in range(2)]
    src2 = c.sbuf("d_src2", [128, NC * TSL], BF16)
    stg = [c.sbuf(f"d_stg{i}", [128, 2048], BF16) for i in range(2)]
    psP = [c.psum(f"d_psP{i}", [128, 2048], F32) for i in range(2)]
    s_wl = [c.sem(f"d_wl{i}") for i in range(2)]
    s_mm = c.sem("d_mm")
    s_ev = c.sem("d_ev")
    s_st = [c.sem(f"d_st{i}") for i in range(2)]
    B = {id(x): x.n for x in [s_mm, s_ev]}

    def proj(wi, slot, src, src_sem, src_val, dst_dram, dst_sbuf, kind,
             prev_mm):
        sl = slot % 2
        if prev_mm is not None:
            c.wait("sync", s_mm, prev_mm)
        c.wait("sync", s_wt[0], s_wt[0].n)
        c.wait("sync", s_wt[1], s_wt[1].n)
        c.op("sync", lambda e, wi=wi, sl=sl: e.dma_start(
            out=wb[sl].ap().rearrange("p (ch m) -> p ch m", ch=NC),
            in_=wT_d[wi * D:(wi + 1) * D, :].rearrange(
                "(ch p) m -> p ch m", p=128)), inc=(s_wl[sl], 16))
        c.wait("tensor", s_wl[sl])
        c.wait("tensor", s_wt[0], s_wt[0].n)
        c.wait("tensor", s_wt[1], s_wt[1].n)
        c.wait("tensor", src_sem, src_val)
        for mc in range(8):
            ps = psP[mc % 2]
            if s_ev.n - B[id(s_ev)] >= 1:
                c.wait("tensor", s_ev, s_ev.n - 1)
            if kind == "fm":
                for ch in range(NC):
                    for hh in range(4):
                        last = (ch == NC - 1 and hh == 3)
                        c.op("tensor", lambda e, ps=ps, ch=ch, hh=hh, mc=mc, sl=sl, src=src:
                             e.matmul(ps[:, hh * 512:(hh + 1) * 512],
                                      wb[sl][:, ch * 1024 + mc * 128:ch * 1024 + (mc + 1) * 128],
                                      src[:, ch * TSL + hh * 512:ch * TSL + (hh + 1) * 512],
                                      start=(ch == 0), stop=(ch == NC - 1)),
                             inc=(s_mm, 1) if last else None)
            else:
                for tt in range(2):
                    t = mc * 2 + tt
                    for ch in range(NC):
                        for hh in range(2):
                            last = (tt == 1 and ch == NC - 1 and hh == 1)
                            c.op("tensor", lambda e, ps=ps, ch=ch, hh=hh, t=t, tt=tt, sl=sl, src=src:
                                 e.matmul(ps[:, tt * 1024 + hh * 512:tt * 1024 + (hh + 1) * 512],
                                          src[:, ch * TSL + t * 128:ch * TSL + (t + 1) * 128],
                                          wb[sl][:, ch * 1024 + hh * 512:ch * 1024 + (hh + 1) * 512],
                                          start=(ch == 0), stop=(ch == NC - 1)),
                                 inc=(s_mm, 1) if last else None)
            c.wait("vector", s_mm)
            if dst_sbuf is not None:
                c.op("vector", lambda e, ps=ps, mc=mc, dst_sbuf=dst_sbuf:
                     e.tensor_copy(dst_sbuf[:, mc * TSL:mc * TSL + 2048], ps[:, :]),
                     inc=(s_ev, 1))
            else:
                c.wait("vector", s_st[mc % 2], s_st[mc % 2].n)
                c.op("vector", lambda e, ps=ps, mc=mc: e.tensor_copy(
                    stg[mc % 2][:, :], ps[:, :]), inc=(s_ev, 1))
                c.wait("sync", s_ev)
                if kind == "fm":
                    c.op("sync", lambda e, mc=mc, dst=dst_dram: e.dma_start(
                        out=dst[mc * 128:(mc + 1) * 128, :],
                        in_=stg[mc % 2][:, :]), inc=(s_st[mc % 2], 16))
                else:
                    c.op("sync", lambda e, mc=mc, dst=dst_dram: e.dma_start(
                        out=dst[mc * 256:(mc + 1) * 256, :].rearrange(
                            "(tt p) d -> p tt d", p=128),
                        in_=stg[mc % 2].ap().rearrange(
                            "p (tt d) -> p tt d", tt=2)),
                        inc=(s_st[mc % 2], 16))
        return s_mm.n

    n0 = proj(0, 0, h_fm[:, :], s_fm_done, s_fm_done.n, None, src2, "fm", None)
    n1 = proj(3, 1, src2[:, :], s_ev, s_ev.n, akfm_out, None, "fm", None)
    n2 = proj(1, 0, h_fm[:, :], s_fm_done, s_fm_done.n, qfm_out, None, "fm", n0)
    n3 = proj(2, 1, h_fm[:, :], s_fm_done, s_fm_done.n, None, src2, "fm", n1)
    n4 = proj(4, 0, src2[:, :], s_ev, s_ev.n, utm_out, None, "tm", n2)
    c.wait("sync", s_st[0])
    c.wait("sync", s_st[1])
    return s_st


def _phase_attn1(c, x2_in, qfm_d, akfm_g, utm_g, x3_out, ident):
    NKB = T // 1024
    q_fm = c.sbuf("e_qfm", [128, NC * TSL], BF16)
    kfm = [c.sbuf(f"e_kfm{i}", [128, NC * 1024], BF16) for i in range(2)]
    ktm = [c.sbuf(f"e_ktm{i}", [128, 8 * 1024], BF16) for i in range(2)]
    Escr = [c.sbuf(f"e_Escr{i}", [128, 1024], BF16) for i in range(2)]
    EmT = [c.sbuf(f"e_EmT{i}", [128, 1024], BF16) for i in range(2)]
    acc = c.sbuf("e_acc", [128, NT * 1024], F32)
    Zparts = c.sbuf("e_Zparts", [128, NT * NKB], F32)
    fstat = c.sbuf("e_fstat", [128, 4 * NT + 9], F32)
    x_t = [c.sbuf(f"e_x_t{i}", [128, 1024], F32) for i in range(2)]
    psc = [c.psum(f"e_psc{i}", [128, 1024], F32) for i in range(2)]
    p_tr = c.psum("e_p_tr", [128, 1024], BF16)
    ps_av = c.psum("e_ps_av", [128, 1024], F32)

    mr = fstat[:, 0:NT]
    negm = fstat[:, NT:2 * NT]
    Zr = fstat[:, 2 * NT:3 * NT]
    invZ = fstat[:, 3 * NT:4 * NT]
    bm = fstat[:, 4 * NT:4 * NT + 1]
    dcol = fstat[:, 4 * NT + 1:4 * NT + 5]
    alph = fstat[:, 4 * NT + 5:4 * NT + 9]

    s_q = c.sem("e_q")
    s_kf = [c.sem(f"a_kf{i}") for i in range(2)]
    s_kt = [c.sem(f"a_kt{i}") for i in range(2)]
    s_xl = [c.sem(f"a_xl{i}") for i in range(2)]
    s_xo = [c.sem(f"a_xo{i}") for i in range(2)]
    s_sc = c.sem("a_sc")
    s_v = c.sem("e_v")
    s_al = c.sem("e_al")
    s_exp = c.sem("a_exp")
    s_tp = c.sem("a_tp")
    s_av = c.sem("a_av")
    s_ev = c.sem("a_ev")
    s_ac = c.sem("a_ac")
    B = {id(x): x.n for x in [s_sc, s_exp, s_tp, s_av, s_ev, s_ac] + s_kf +
         s_kt + s_xl + s_xo}
    v_cnt, al_cnt = {}, {}

    def emit_maxchain(i):
        qt = i % NT
        kb = i // NT
        c.wait("vector", s_sc, B[id(s_sc)] + i + 1)
        if kb == 0:
            c.chainop("vector", s_v, lambda e, qt=qt, i=i: e.reduce_max(
                mr[:, qt:qt + 1], psc[i % 2][:, :], axis=AX))
            c.chainop("vector", s_v, lambda e, qt=qt: e.tensor_scalar_mul(
                negm[:, qt:qt + 1], mr[:, qt:qt + 1], -1.0))
        else:
            if i - 4 in al_cnt:
                c.wait("vector", s_al, al_cnt[i - 4])
            c.chainop("vector", s_v, lambda e, i=i: e.reduce_max(
                bm, psc[i % 2][:, :], axis=AX))
            c.chainop("vector", s_v, lambda e, qt=qt, i=i: e.tensor_sub(
                dcol[:, i % 4:i % 4 + 1], mr[:, qt:qt + 1], bm))
            c.chainop("vector", s_v, lambda e, i=i: e.tensor_scalar_min(
                dcol[:, i % 4:i % 4 + 1], dcol[:, i % 4:i % 4 + 1], 0.0))
            c.chainop("vector", s_v, lambda e, qt=qt: e.tensor_max(
                mr[:, qt:qt + 1], mr[:, qt:qt + 1], bm))
            c.chainop("vector", s_v, lambda e, qt=qt: e.tensor_scalar_mul(
                negm[:, qt:qt + 1], mr[:, qt:qt + 1], -1.0))
        v_cnt[i] = s_v.n

    def emit_exp(i):
        qt = i % NT
        kb = i // NT
        c.wait("scalar", s_v, v_cnt[i])
        if kb > 0:
            if i >= 4:
                c.wait("scalar", s_ac, B[id(s_ac)] + i - 3)
            c.op("scalar", lambda e, i=i: e.activation(
                alph[:, i % 4:i % 4 + 1], dcol[:, i % 4:i % 4 + 1], AF.Exp),
                inc=(s_al, 1))
            al_cnt[i] = s_al.n
        if i >= 2:
            c.wait("scalar", s_tp, B[id(s_tp)] + i - 1)
        c.op("scalar", lambda e, i=i, qt=qt: e.activation(
            Escr[i % 2][:, :], psc[i % 2][:, :], AF.Exp,
            bias=negm[:, qt:qt + 1], accum_out=Zparts[:, i:i + 1]),
            inc=(s_exp, 1))

    def emit_transp(a):
        c.wait("tensor", s_exp, B[id(s_exp)] + a + 1)
        if a >= 1:
            c.wait("tensor", s_ev, B[id(s_ev)] + a)
        for ch in range(8):
            c.op("tensor", lambda e, a=a, ch=ch: e.transpose(
                p_tr[:, ch * 128:(ch + 1) * 128],
                Escr[a % 2][:, ch * 128:(ch + 1) * 128], ident[0][:, :]),
                inc=(s_tp, 1) if ch == 7 else None)

    def emit_evac(a):
        c.wait("scalar", s_tp, B[id(s_tp)] + a + 1)
        if a >= 2:
            c.wait("scalar", s_av, B[id(s_av)] + a - 1)
        c.op("scalar", lambda e, a=a: e.activation(
            EmT[a % 2][:, :], p_tr[:, 0:1024], AF.Identity), inc=(s_ev, 1))

    def emit_av(a):
        kb_a = a // NT
        c.wait("tensor", s_ev, B[id(s_ev)] + a + 1)
        c.wait("tensor", s_kt[kb_a % 2], s_kt[kb_a % 2].n)
        if a >= 1:
            c.wait("tensor", s_ac, B[id(s_ac)] + a)
        for sc in range(8):
            for hh in range(2):
                last = (sc == 7 and hh == 1)
                c.op("tensor", lambda e, sc=sc, hh=hh, a=a, kb_a=kb_a:
                     e.matmul(ps_av[:, hh * 512:(hh + 1) * 512],
                              EmT[a % 2][:, sc * 128:(sc + 1) * 128],
                              ktm[kb_a % 2][:, sc * 1024 + hh * 512:sc * 1024 + (hh + 1) * 512],
                              start=(sc == 0), stop=(sc == 7)),
                     inc=(s_av, 1) if last else None)

    def emit_update(a):
        qt = a % NT
        kb_a = a // NT
        at = acc[:, qt * 1024:(qt + 1) * 1024]
        c.wait("vector", s_av, B[id(s_av)] + a + 1)
        if kb_a == 0:
            c.op("vector", lambda e, at=at: e.tensor_copy(at, ps_av[:, :]))
            c.op("vector", lambda e, qt=qt, a=a: e.tensor_copy(
                Zr[:, qt:qt + 1], Zparts[:, a:a + 1]), inc=(s_ac, 1))
        else:
            c.wait("vector", s_al, al_cnt[a])
            c.op("vector", lambda e, at=at, a=a: e.scalar_tensor_tensor(
                at, at, alph[:, a % 4:a % 4 + 1], ps_av[:, :],
                ALU.mult, ALU.add))
            c.op("vector", lambda e, qt=qt, a=a: e.scalar_tensor_tensor(
                Zr[:, qt:qt + 1], Zr[:, qt:qt + 1], alph[:, a % 4:a % 4 + 1],
                Zparts[:, a:a + 1], ALU.mult, ALU.add), inc=(s_ac, 1))

    c.op("sync", lambda e: e.dma_start(
        out=q_fm.ap().rearrange("p (ch s) -> p ch s", ch=NC),
        in_=qfm_d.rearrange("(ch p) s -> p ch s", p=128)), inc=(s_q, 16))
    c.wait("tensor", s_q)
    c.wait("tensor", ident[1])

    for kb in range(NKB):
        if kb >= 2:
            c.wait("sync", s_sc, B[id(s_sc)] + 16 * (kb - 1))
        c.op("sync", lambda e, kb=kb: e.dma_start(
            out=kfm[kb % 2].ap().rearrange("p (ch s) -> p ch s", ch=NC),
            in_=akfm_g[(kb // 2) * 1024:(kb // 2 + 1) * 1024,
                       (kb % 2) * 1024:(kb % 2 + 1) * 1024]
                .rearrange("(ch p) s -> p ch s", p=128)), inc=(s_kf[kb % 2], 16))
        if kb >= 2:
            c.wait("sync", s_av, B[id(s_av)] + (kb - 1) * 16)
        c.op("sync", lambda e, kb=kb: e.dma_start(
            out=ktm[kb % 2].ap().rearrange("p (sc f) -> p sc f", sc=8),
            in_=utm_g[kb * 1024:(kb + 1) * 1024, :]
                .rearrange("(sc p) f -> p sc f", p=128)), inc=(s_kt[kb % 2], 16))
        for qt in range(NT):
            i = kb * NT + qt
            c.wait("tensor", s_kf[kb % 2], s_kf[kb % 2].n)
            if i >= 2:
                c.wait("tensor", s_exp, B[id(s_exp)] + i - 1)
            for ch in range(NC):
                for hh in range(2):
                    last = (ch == NC - 1 and hh == 1)
                    c.op("tensor", lambda e, ch=ch, hh=hh, qt=qt, kb=kb, i=i:
                         e.matmul(psc[i % 2][:, hh * 512:(hh + 1) * 512],
                                  q_fm[:, ch * TSL + qt * 128:ch * TSL + (qt + 1) * 128],
                                  kfm[kb % 2][:, ch * 1024 + hh * 512:ch * 1024 + (hh + 1) * 512],
                                  start=(ch == 0), stop=(ch == NC - 1)),
                         inc=(s_sc, 1) if last else None)
            emit_maxchain(i)
            emit_exp(i)
            if i >= 1:
                emit_transp(i - 1)
                emit_evac(i - 1)
            if i >= 2:
                emit_av(i - 2)
                emit_update(i - 2)

    last_t = NKB * NT - 1
    emit_transp(last_t)
    emit_evac(last_t)
    emit_av(last_t - 1)
    emit_update(last_t - 1)
    emit_av(last_t)
    emit_update(last_t)

    c.wait("vector", s_ac, B[id(s_ac)] + NKB * NT)
    c.chainop("vector", s_v, lambda e: e.reciprocal(invZ, Zr))
    for qt in range(NT):
        xt = x_t[qt % 2]
        if qt >= 2:
            c.wait("sync", s_xo[qt % 2], B[id(s_xo[qt % 2])] + 16 * (qt // 2))
        c.op("sync", lambda e, qt=qt, xt=xt: e.dma_start(
            out=xt[:, :], in_=x2_in[qt * 128:(qt + 1) * 128, :]),
            inc=(s_xl[qt % 2], 16))
        c.wait("vector", s_xl[qt % 2], s_xl[qt % 2].n)
        c.chainop("vector", s_v, lambda e, xt=xt, qt=qt: e.scalar_tensor_tensor(
            xt[:, :], acc[:, qt * 1024:(qt + 1) * 1024], invZ[:, qt:qt + 1],
            xt[:, :], ALU.mult, ALU.add))
        c.wait("sync", s_v)
        c.op("sync", lambda e, qt=qt, xt=xt: e.dma_start(
            out=x3_out[qt * 128:(qt + 1) * 128, :], in_=xt[:, :]),
            inc=(s_xo[qt % 2], 16))
    c.wait("sync", s_xo[0])
    c.wait("sync", s_xo[1])


def emit_cc_scales(c, name, gmax11, s_gm, cc_in, cc_out, rg, identf,
                   n_vals=1, bc=None):
    amr = c.sbuf(name + "_amr", [1, 4 * n_vals], F32)
    ones_row = c.sbuf(name + "_ones", [1, 128], F32)
    if bc is None:
        bc = c.sbuf(name + "_bc", [128, 3 * n_vals], F32)
    p_b = c.psum(name + "_pb", [128, 3 * n_vals], F32)
    s = c.sem(name + "_s")
    s_ccl = c.sem(name + "_cc")
    c.op("gpsimd", lambda e: e.memset(ones_row[:, :], 1.0), inc=(s, 1))
    c.wait("sync", s_gm, s_gm.n)
    c.op("sync", lambda e: e.dma_start(out=cc_in, in_=gmax11), inc=(s, 16))
    c.wait("gpsimd", s)
    c.op("gpsimd", lambda e: e.collective_compute(
        "AllReduce", ALU.max, replica_groups=rg, ins=[cc_in], outs=[cc_out]),
        inc=(s_ccl, 1))
    c.wait("sync", s_ccl, 1)
    c.op("sync", lambda e: e.dma_start(
        out=amr[:, 0:n_vals], in_=cc_out.rearrange("v o -> o v")),
        inc=(s, 16))
    c.wait("vector", s)
    for v in range(n_vals):
        c.chainop("vector", s, lambda e, v=v: e.tensor_scalar(
            amr[:, n_vals + 3 * v:n_vals + 3 * v + 1], amr[:, v:v + 1],
            1.0 / 32767.0, 1e-12, ALU.mult, ALU.max))
        c.chainop("vector", s, lambda e, v=v: e.reciprocal(
            amr[:, n_vals + 3 * v + 1:n_vals + 3 * v + 2],
            amr[:, n_vals + 3 * v:n_vals + 3 * v + 1]))
        c.chainop("vector", s, lambda e, v=v: e.tensor_scalar_mul(
            amr[:, n_vals + 3 * v + 2:n_vals + 3 * v + 3],
            amr[:, n_vals + 3 * v + 1:n_vals + 3 * v + 2], -1.0))
    c.wait("tensor", s)
    c.op("tensor", lambda e: e.matmul(
        p_b[:, :], ones_row[:, :], amr[:, n_vals:4 * n_vals],
        start=True, stop=True), inc=(s, 1))
    c.wait("vector", s)
    c.op("vector", lambda e: e.tensor_copy(bc[:, :], p_b[:, :]), inc=(s, 1))
    return bc, s


def emit_colmax_to_scalar(c, name, parts, width, identf, p_am):
    pfin = c.sbuf(name + "_pfin", [128, 1], F32)
    out11 = c.sbuf(name + "_o11", [1, 1], F32)
    s = c.sem(name + "_s")
    c.chainop("vector", s, lambda e: e.tensor_reduce(
        pfin[:, :], parts, axis=AX, op=ALU.max, apply_absolute_value=True))
    c.wait("tensor", s)
    c.wait("tensor", identf[1])
    c.op("tensor", lambda e: e.transpose(
        p_am[0:1, :], pfin[:, :], identf[0][:, :]), inc=(s, 1))
    c.wait("vector", s)
    c.chainop("vector", s, lambda e: e.tensor_reduce(
        out11[:, :], p_am[0:1, :], axis=AX, op=ALU.max,
        apply_absolute_value=True))
    return out11, s


def _phase_ffn_up(c, h3_tm, s_h3, bc1, s_bc1, wugT_d, s_wt, s_wt_ev,
                  up_out, gt_out, ident, identf, p_am):
    xq_tm = c.sbuf("g_xq_tm", [128, NT * 1024], BF16)
    xq_fm = c.sbuf("g_xq_fm", [128, NC * TSL], BF16)
    rb = mk_round_bufs(c, "g_rb", 1024)
    wub = [c.sbuf(f"g_wub{i}", [128, NC * 512], BF16) for i in range(4)]
    stg = [c.sbuf(f"g_stg{i}", [128, 512], BF16) for i in range(4)]
    parts = c.sbuf("g_parts", [128, 2 * 96], F32)
    pfin2 = c.sbuf("g_pfin2", [128, 2], F32)
    am2 = c.sbuf("g_am2", [2, 1], F32)

    s_ch = c.sem("f_ch")
    s_a = c.sem("f_a")
    s_xq = c.sem("f_xq")
    s_fm = c.sem("f_fm")
    s_wl = [c.sem(f"f_wl{i}") for i in range(4)]
    s_mm = c.sem("f_mm")
    s_evq = c.sem("f_evq")
    s_pt = c.sem("f_pt")
    s_st = [c.sem(f"f_st{i}") for i in range(4)]
    B = {id(x): x.n for x in [s_mm, s_evq, s_pt, s_xq]}

    c.wait("scalar", s_bc1, s_bc1.n)
    c.wait("vector", s_bc1, s_bc1.n)
    c.wait("scalar", s_h3, s_h3.n)
    for t in range(NT):
        src = h3_tm[:, t * 1024:(t + 1) * 1024]
        emit_round(c, s_a, src, 1024, rb, bc1[:, 2:3], s_ch)
        c.wait("scalar", s_ch)
        c.op("scalar", lambda e, t=t: e.activation(
            xq_tm[:, t * 1024:(t + 1) * 1024], rb["r"][:, 0:1024],
            AF.Identity, scale=bc1[:, 0:1]), inc=(s_xq, 1))
        c.wait("vector", s_xq)

    p_tr_cm = c.nc.psum_tensor("g_p_tr", [128, 2048], BF16)
    p_tr = p_tr_cm.__enter__()
    emit_transpose_tiles(c, NT, xq_tm, xq_fm, p_tr, ident, s_xq, s_fm,
                         name="lnP", s_src_base=B[id(s_xq)])
    p_tr_cm.__exit__(None, None, None)
    psQ = [c.psum(f"g_psQ{i}", [128, 512], F32) for i in range(4)]

    c.wait("tensor", s_fm)
    c.wait("tensor", s_wt_ev, s_wt_ev.n)
    c.wait("sync", s_wt[0], s_wt[0].n)
    c.wait("sync", s_wt[1], s_wt[1].n)
    for ic in range(6):
        for w2 in range(2):
            sl = 2 * (ic % 2) + w2
            if ic >= 2:
                c.wait("sync", s_mm, B[id(s_mm)] + (ic - 1) * 32)
            c.op("sync", lambda e, ic=ic, w2=w2, sl=sl: e.dma_start(
                out=wub[sl].ap().rearrange("p (ch j) -> p ch j", ch=NC),
                in_=wugT_d[w2 * D:(w2 + 1) * D, ic * 512:(ic + 1) * 512]
                    .rearrange("(ch p) j -> p ch j", p=128)),
                inc=(s_wl[sl], 16))
        for t in range(NT):
            for w2 in range(2):
                gi = ic * 32 + t * 2 + w2
                sl = 2 * (ic % 2) + w2
                ps = psQ[2 * (t % 2) + w2]
                c.wait("tensor", s_wl[sl], s_wl[sl].n)
                if gi >= 4:
                    c.wait("tensor", s_evq, B[id(s_evq)] + gi - 3)
                    c.wait("tensor", s_pt, B[id(s_pt)] + gi - 3)
                for ch in range(NC):
                    c.op("tensor", lambda e, ps=ps, ch=ch, t=t, sl=sl:
                         e.matmul(ps[:, :],
                                  xq_fm[:, ch * TSL + t * 128:ch * TSL + (t + 1) * 128],
                                  wub[sl][:, ch * 512:(ch + 1) * 512],
                                  start=(ch == 0), stop=(ch == NC - 1)),
                         inc=(s_mm, 1) if ch == NC - 1 else None)
                c.wait("scalar", s_mm)
                c.wait("scalar", s_st[2 * (t % 2) + w2],
                       s_st[2 * (t % 2) + w2].n)
                c.wait("scalar", s_pt, max(B[id(s_pt)], B[id(s_pt)] + gi - 3))
                c.op("scalar", lambda e, ps=ps, sl2=2 * (t % 2) + w2:
                     e.activation(stg[sl2][:, :], ps[:, :], AF.Identity),
                     inc=(s_evq, 1))
                c.wait("vector", s_evq)
                c.op("vector", lambda e, sl2=2 * (t % 2) + w2,
                     col=w2 * 96 + ic * 16 + t:
                     e.tensor_reduce(parts[:, col:col + 1], stg[sl2][:, :],
                                     axis=AX, op=ALU.max,
                                     apply_absolute_value=True),
                     inc=(s_pt, 1))
                c.wait("sync", s_evq)
                dst = (up_out if w2 == 0 else gt_out)[
                    t * 128:(t + 1) * 128, ic * 512:(ic + 1) * 512]
                c.op("sync", lambda e, dst=dst, sl2=2 * (t % 2) + w2:
                     e.dma_start(out=dst, in_=stg[sl2][:, :]),
                     inc=(s_st[2 * (t % 2) + w2], 16))
    s_am = c.sem("g_am_s")
    c.wait("vector", s_pt)
    c.chainop("vector", s_am, lambda e: e.tensor_reduce(
        pfin2[:, :], parts.ap().rearrange("p (w k) -> p w k", w=2), axis=AX,
        op=ALU.max, apply_absolute_value=True))
    c.wait("tensor", s_am)
    c.wait("tensor", identf[1])
    c.op("tensor", lambda e: e.transpose(
        p_am[0:2, :], pfin2[:, :], identf[0][:, :]), inc=(s_am, 1))
    c.wait("vector", s_am)
    c.chainop("vector", s_am, lambda e: e.tensor_reduce(
        am2[:, :], p_am[0:2, :], axis=AX, op=ALU.max,
        apply_absolute_value=True))
    for ss in s_st:
        c.wait("sync", ss)
    return am2, s_am


def _phase_ffn_h(c, up_in, gt_in, bc2, bc3, s_bc, upq_out, g2_out, identf,
                 p_am):
    ldu = [c.sbuf(f"h_ldu{i}", [128, DI], BF16) for i in range(2)]
    ldg = [c.sbuf(f"h_ldg{i}", [128, DI], BF16) for i in range(2)]
    rb = [mk_round_bufs(c, f"h_rb{i}", DI) for i in range(2)]
    sigb = [c.sbuf(f"h_sigb{i}", [128, DI], F32) for i in range(2)]
    g1b = [c.sbuf(f"h_g1b{i}", [128, DI], BF16) for i in range(2)]
    outb = [c.sbuf(f"h_outb{i}", [128, DI], BF16) for i in range(4)]
    parts = c.sbuf("h_parts", [128, NT], F32)
    s_lu = [c.sem(f"f_wl{i}") for i in range(2)]
    s_lg = [c.sem(f"f_wl{i + 2}") for i in range(2)]
    s_chp = [c.sem("f_ch"), c.sem("f_ch2")]
    s_ap = [c.sem("f_a"), c.sem("f_a2")]
    s_st = [c.sem(f"f_st{i}") for i in range(4)]

    for par in range(2):
        c.wait("scalar", s_bc, s_bc.n)
        c.wait("vector", s_bc, s_bc.n)
    for t in range(NT):
        sl = t % 2
        s_ch, s_a = s_chp[sl], s_ap[sl]
        if t >= 2:
            c.wait("sync", s_chp[sl], s_chp[sl].n)
        c.op("sync", lambda e, t=t, sl=sl: e.dma_start(
            out=ldu[sl][:, :], in_=up_in[t * 128:(t + 1) * 128, :]),
            inc=(s_lu[sl], 16))
        c.op("sync", lambda e, t=t, sl=sl: e.dma_start(
            out=ldg[sl][:, :], in_=gt_in[t * 128:(t + 1) * 128, :]),
            inc=(s_lg[sl], 16))
        c.wait("scalar", s_lg[sl], s_lg[sl].n)
        c.wait("vector", s_lg[sl], s_lg[sl].n)
        emit_round(c, s_a, ldg[sl][:, :], DI, rb[sl], bc3[:, 2:3], s_ch)
        c.wait("scalar", s_ch)
        c.op("scalar", lambda e, sl=sl: e.activation(
            sigb[sl][:, :], rb[sl]["r"][:, :], AF.Sigmoid, scale=bc3[:, 0:1]),
            inc=(s_a, 1))
        c.wait("scalar", s_a)
        c.op("scalar", lambda e, sl=sl: e.activation(
            g1b[sl][:, :], rb[sl]["r"][:, :], AF.Identity, scale=bc3[:, 0:1]),
            inc=(s_a, 1))
        c.wait("scalar", s_a)
        c.wait("scalar", s_ch)
        c.op("scalar", lambda e, sl=sl: e.activation(
            rb[sl]["i"][:, :], sigb[sl][:, :], AF.Identity, scale=-65536.0),
            inc=(s_a, 1))
        c.wait("scalar", s_a)
        c.op("scalar", lambda e, sl=sl: e.activation(
            rb[sl]["cf"][:, :], rb[sl]["i"][:, :], AF.Copy), inc=(s_a, 1))
        c.wait("vector", s_a)
        c.chainop("vector", s_ch, lambda e, sl=sl: e.scalar_tensor_tensor(
            rb[sl]["d"][:, :], sigb[sl][:, :], -65536.0, rb[sl]["cf"][:, :],
            ALU.mult, ALU.subtract))
        c.chainop("vector", s_ch, lambda e, sl=sl: e.scalar_tensor_tensor(
            rb[sl]["a"][:, :], rb[sl]["d"][:, :], 0.5, rb[sl]["cf"][:, :],
            ALU.is_ge, ALU.add))
        c.chainop("vector", s_ch, lambda e, sl=sl: e.scalar_tensor_tensor(
            rb[sl]["r"][:, :], rb[sl]["d"][:, :], -0.5, rb[sl]["a"][:, :],
            ALU.is_le, ALU.subtract))
        c.chainop("vector", s_ch, lambda e, sl=sl: e.tensor_scalar_min(
            rb[sl]["r"][:, :], rb[sl]["r"][:, :], 65535.0))
        ob = outb[2 * sl]
        c.wait("vector", s_st[2 * sl], s_st[2 * sl].n)
        c.chainop("vector", s_ch, lambda e, ob=ob, sl=sl: e.scalar_tensor_tensor(
            ob[:, :], rb[sl]["r"][:, :], 1.0 / 65536.0, g1b[sl][:, :],
            ALU.mult, ALU.mult))
        c.chainop("vector", s_ch, lambda e, ob=ob, t=t: e.tensor_reduce(
            parts[:, t:t + 1], ob[:, :], axis=AX, op=ALU.max,
            apply_absolute_value=True))
        c.wait("sync", s_ch)
        c.op("sync", lambda e, ob=ob, t=t: e.dma_start(
            out=g2_out[t * 128:(t + 1) * 128, :], in_=ob[:, :]),
            inc=(s_st[2 * sl], 16))
        c.wait("scalar", s_lu[sl], s_lu[sl].n)
        emit_round(c, s_a, ldu[sl][:, :], DI, rb[sl], bc2[:, 2:3], s_ch)
        ob2 = outb[2 * sl + 1]
        c.wait("scalar", s_ch)
        c.wait("scalar", s_st[2 * sl + 1], s_st[2 * sl + 1].n)
        c.op("scalar", lambda e, ob2=ob2, sl=sl: e.activation(
            ob2[:, :], rb[sl]["r"][:, :], AF.Identity, scale=bc2[:, 0:1]),
            inc=(s_a, 1))
        c.wait("sync", s_a)
        c.op("sync", lambda e, ob2=ob2, t=t: e.dma_start(
            out=upq_out[t * 128:(t + 1) * 128, :], in_=ob2[:, :]),
            inc=(s_st[2 * sl + 1], 16))
        c.wait("vector", s_a)
    c.wait("vector", s_chp[0])
    c.wait("vector", s_chp[1])
    c.wait("vector", s_ap[0])
    c.wait("vector", s_ap[1])
    am4, s_am = emit_colmax_to_scalar(c, "h_am", parts[:, :], NT, identf, p_am)
    for ss in s_st:
        c.wait("sync", ss)
    return am4, s_am


def _phase_ffn_i(c, g2_in, upq_in, bc4, s_bc, o_out, identf, p_am):
    ldg = [c.sbuf(f"i_ldg{i}", [128, DI], BF16) for i in range(2)]
    ldu = [c.sbuf(f"i_ldu{i}", [128, DI], BF16) for i in range(2)]
    rb = [mk_round_bufs(c, f"i_rb{i}", DI) for i in range(2)]
    outb = [c.sbuf(f"i_outb{i}", [128, DI], BF16) for i in range(2)]
    parts = c.sbuf("i_parts", [128, NT], F32)
    s_lg = [c.sem(f"f_wl{i}") for i in range(2)]
    s_lu = [c.sem(f"f_wl{i + 2}") for i in range(2)]
    s_chp = [c.sem("f_ch"), c.sem("f_ch2")]
    s_ap = [c.sem("f_a"), c.sem("f_a2")]
    s_st = [c.sem(f"f_st{i}") for i in range(2)]

    c.wait("scalar", s_bc, s_bc.n)
    c.wait("vector", s_bc, s_bc.n)
    for t in range(NT):
        sl = t % 2
        s_ch, s_a = s_chp[sl], s_ap[sl]
        if t >= 2:
            c.wait("sync", s_chp[sl], s_chp[sl].n)
        c.op("sync", lambda e, t=t, sl=sl: e.dma_start(
            out=ldg[sl][:, :], in_=g2_in[t * 128:(t + 1) * 128, :]),
            inc=(s_lg[sl], 16))
        c.op("sync", lambda e, t=t, sl=sl: e.dma_start(
            out=ldu[sl][:, :], in_=upq_in[t * 128:(t + 1) * 128, :]),
            inc=(s_lu[sl], 16))
        c.wait("scalar", s_lg[sl], s_lg[sl].n)
        c.wait("vector", s_lg[sl], s_lg[sl].n)
        emit_round(c, s_a, ldg[sl][:, :], DI, rb[sl], bc4[:, 2:3], s_ch)
        c.wait("vector", s_lu[sl], s_lu[sl].n)
        ob = outb[sl]
        c.wait("vector", s_st[sl], s_st[sl].n)
        c.chainop("vector", s_ch, lambda e, ob=ob, sl=sl:
                  e.scalar_tensor_tensor(ob[:, :], rb[sl]["r"][:, :],
                                         bc4[:, 0:1], ldu[sl][:, :],
                                         ALU.mult, ALU.mult))
        c.chainop("vector", s_ch, lambda e, ob=ob, t=t: e.tensor_reduce(
            parts[:, t:t + 1], ob[:, :], axis=AX, op=ALU.max,
            apply_absolute_value=True))
        c.wait("sync", s_ch)
        c.op("sync", lambda e, ob=ob, t=t: e.dma_start(
            out=o_out[t * 128:(t + 1) * 128, :], in_=ob[:, :]),
            inc=(s_st[sl], 16))
    c.wait("vector", s_chp[0])
    c.wait("vector", s_chp[1])
    am5, s_am = emit_colmax_to_scalar(c, "i_am", parts[:, :], NT, identf, p_am)
    c.wait("sync", s_st[0])
    c.wait("sync", s_st[1])
    return am5, s_am


def _phase_ffn_j(c, o_in, bc5, s_bc, x3_in, wdnT_d, s_wt, s_wt_ev, y_out,
                 ident, rb):
    ldo = [c.sbuf(f"j_ldo{i}", [128, DI], BF16) for i in range(2)]
    o_b = c.sbuf("j_o_b", [128, DI], BF16)
    o_fm = [c.sbuf(f"j_o_fm{i}", [128, DI], BF16) for i in range(2)]
    wdT = c.sbuf("j_wdT", [128, (DI // 128) * 1024], BF16)
    x_t = [c.sbuf(f"j_x_t{i}", [128, 1024], F32) for i in range(2)]
    s_lo = [c.sem(f"f_wl{i}") for i in range(2)]
    s_ch = c.sem("f_ch")
    s_a = c.sem("f_a")
    s_ob = c.sem("j_ob")
    s_tp = c.sem("a_tp")
    s_ofm = c.sem("j_ofm")
    s_mm = c.sem("f_mm")
    s_wl = c.sem("j_wl")
    s_xl = [c.sem(f"a_xl{i}") for i in range(2)]
    s_xo = [c.sem(f"a_xo{i}") for i in range(2)]
    s_fin = c.sem("a_fin")
    B = {id(x): x.n for x in [s_ob, s_tp, s_ofm, s_mm, s_fin] + s_xl + s_xo}

    c.wait("sync", s_wt[0], s_wt[0].n)
    c.wait("sync", s_wt[1], s_wt[1].n)
    c.op("sync", lambda e: e.dma_start(
        out=wdT.ap().rearrange("p (ic j) -> p ic j", ic=DI // 128),
        in_=wdnT_d.rearrange("(ic p) j -> p ic j", p=128)), inc=(s_wl, 16))
    p_tr_cm = c.nc.psum_tensor("j_p_tr", [128, 2048], BF16)
    p_tr = p_tr_cm.__enter__()
    p_dn_cm = [c.nc.psum_tensor(f"j_p_dn{i}", [128, 1024], F32)
               for i in range(2)]
    p_dn = [p.__enter__() for p in p_dn_cm]

    c.wait("scalar", s_bc, s_bc.n)
    c.wait("vector", s_bc, s_bc.n)
    c.wait("tensor", s_wt_ev, s_wt_ev.n)
    c.wait("tensor", ident[1])
    for t in range(NT):
        sl = t % 2
        if t >= 2:
            c.wait("sync", s_ch, s_ch.n)
        c.op("sync", lambda e, t=t, sl=sl: e.dma_start(
            out=ldo[sl][:, :], in_=o_in[t * 128:(t + 1) * 128, :]),
            inc=(s_lo[sl], 16))
        c.wait("scalar", s_lo[sl], s_lo[sl].n)
        c.wait("vector", s_lo[sl], s_lo[sl].n)
        emit_round(c, s_a, ldo[sl][:, :], DI, rb, bc5[:, 2:3], s_ch)
        c.wait("scalar", s_ch)
        if t >= 1:
            c.wait("scalar", s_tp, B[id(s_tp)] + t * 3)
        c.op("scalar", lambda e: e.activation(
            o_b[:, :], rb["r"][:, :], AF.Identity, scale=bc5[:, 0:1]),
            inc=(s_ob, 1))
        c.wait("vector", s_ob)
        c.wait("tensor", s_ob)
        for g in range(3):
            gg = t * 3 + g
            half = (gg % 2) * 1024
            if gg >= 2:
                c.wait("tensor", s_ofm, B[id(s_ofm)] + gg - 1)
            for icc in range(8):
                c.op("tensor", lambda e, g=g, icc=icc, half=half: e.transpose(
                    p_tr[:, half + icc * 128:half + (icc + 1) * 128],
                    o_b[:, (g * 8 + icc) * 128:(g * 8 + icc + 1) * 128],
                    ident[0][:, :]), inc=(s_tp, 1) if icc == 7 else None)
            c.wait("vector", s_tp)
            if t >= 2:
                c.wait("vector", s_mm, B[id(s_mm)] + t - 1)
            c.op("vector", lambda e, sl=sl, g=g, half=half: e.tensor_copy(
                o_fm[sl][:, g * 1024:(g + 1) * 1024], p_tr[:, half:half + 1024]),
                inc=(s_ofm, 1))
        c.wait("tensor", s_ofm)
        c.wait("tensor", s_wl)
        pd = p_dn[t % 2]
        if t >= 2:
            c.wait("tensor", s_fin, B[id(s_fin)] + 2 * (t - 1))
        for icc in range(24):
            for hh in range(2):
                last = (icc == 23 and hh == 1)
                c.op("tensor", lambda e, pd=pd, icc=icc, hh=hh, sl=sl:
                     e.matmul(pd[:, hh * 512:(hh + 1) * 512],
                              o_fm[sl][:, icc * 128:(icc + 1) * 128],
                              wdT[:, icc * 1024 + hh * 512:icc * 1024 + (hh + 1) * 512],
                              start=(icc == 0), stop=(icc == 23)),
                     inc=(s_mm, 1) if last else None)
        xt = x_t[t % 2]
        if t >= 2:
            c.wait("sync", s_xo[t % 2], B[id(s_xo[t % 2])] + 16 * (t // 2))
        c.op("sync", lambda e, t=t, xt=xt: e.dma_start(
            out=xt[:, :], in_=x3_in[t * 128:(t + 1) * 128, :]),
            inc=(s_xl[t % 2], 16))
        c.wait("vector", s_xl[t % 2], s_xl[t % 2].n)
        c.wait("vector", s_mm)
        c.op("vector", lambda e, xt=xt, pd=pd: e.scalar_tensor_tensor(
            xt[:, :], pd[:, :], 1.0, xt[:, :], ALU.mult, ALU.add),
            inc=(s_fin, 2))
        c.wait("sync", s_fin)
        c.op("sync", lambda e, t=t, xt=xt: e.dma_start(
            out=y_out[t * 128:(t + 1) * 128, :], in_=xt[:, :]),
            inc=(s_xo[t % 2], 16))
    c.wait("sync", s_xo[0])
    c.wait("sync", s_xo[1])
    p_dn_cm[1].__exit__(None, None, None)
    p_dn_cm[0].__exit__(None, None, None)
    p_tr_cm.__exit__(None, None, None)


def build_fused():
    nc = bass.Bass("TRN2", target_bir_lowering=False)
    RG = [list(range(NCORES))]
    x_in = nc.dram_tensor("x", [TSL, D], F32, kind="ExternalInput").ap()
    W_in = nc.dram_tensor("W", [D, D], F32, kind="ExternalInput").ap()
    gw_in = nc.dram_tensor("gate_w", [8, D], F32, kind="ExternalInput").ap()
    ew_in = nc.dram_tensor("expert_w", [8 * D, D], F32, kind="ExternalInput").ap()
    attw_in = nc.dram_tensor("attw", [5 * D, D], F32, kind="ExternalInput").ap()
    wug_in = nc.dram_tensor("wug", [2 * DI, D], F32, kind="ExternalInput").ap()
    wd_in = nc.dram_tensor("w_down", [D, DI], F32, kind="ExternalInput").ap()
    y_out = nc.dram_tensor("y", [TSL, D], F32, kind="ExternalOutput").ap()

    dt_ = nc.dram_tensor
    x1_d = dt_("x1_d", [TSL, D], F32).ap()
    x2_d = dt_("x2_d", [TSL, D], F32).ap()
    x3_d = dt_("x3_d", [TSL, D], F32).ap()
    E_dram = dt_("E_scr", [TSL, T], BF16).ap()
    wsum_d = dt_("wsum_d", [1, D], F32).ap()
    h0tm_in = dt_("h0tm_in", [TSL, D], BF16).ap()
    h0fm_in = dt_("h0fm_in", [D, TSL], BF16).ap()
    h0tm_g = dt_("h0tm_g", [T, D], BF16, addr_space="Shared").ap()
    h0fm_g = dt_("h0fm_g", [NCORES * D, TSL], BF16, addr_space="Shared").ap()
    ewT_d = dt_("ewT", [8 * D, D], BF16).ap()
    attwT_d = dt_("attwT", [5 * D, D], BF16).ap()
    qfm_d = dt_("qfm_d", [D, TSL], BF16).ap()
    akfm_in = dt_("akfm_in", [D, TSL], BF16).ap()
    utm_in = dt_("utm_in", [TSL, D], BF16).ap()
    akfm_g = dt_("akfm_g", [NCORES * D, TSL], BF16, addr_space="Shared").ap()
    utm_g = dt_("utm_g", [T, D], BF16, addr_space="Shared").ap()
    up_d = dt_("up_d", [TSL, DI], BF16).ap()
    gt_d = dt_("gt_d", [TSL, DI], BF16).ap()
    upq_d = dt_("upq_d", [TSL, DI], BF16).ap()
    g2_d = dt_("g2_d", [TSL, DI], BF16).ap()
    o_d = dt_("o_d", [TSL, DI], BF16).ap()
    wugdq_d = dt_("wugdq", [2 * DI, D], BF16).ap()
    wugT_d = dt_("wugT", [2 * D, DI], BF16).ap()
    wdndq_d = dt_("wdndq", [D, DI], BF16).ap()
    wdnT_d = dt_("wdnT", [DI, D], BF16).ap()
    cc1_in = dt_("cc1_in", [1, 1], F32).ap()
    cc1_out = dt_("cc1_out", [1, 1], F32).ap()
    cc2_in = dt_("cc2_in", [2, 1], F32).ap()
    cc2_out = dt_("cc2_out", [2, 1], F32).ap()
    cc4_in = dt_("cc4_in", [1, 1], F32).ap()
    cc4_out = dt_("cc4_out", [1, 1], F32).ap()
    cc5_in = dt_("cc5_in", [1, 1], F32).ap()
    cc5_out = dt_("cc5_out", [1, 1], F32).ap()

    c = Ctx(nc)
    ident = make_identity(c, "ident", BF16)
    identf = make_identity(c, "identf", F32)
    consts = Consts(c, [EPS, -32.0])
    s_cc = c.sem("s_cc")
    bc1_t = c.sbuf("bc1_t", [128, 3], F32)
    bc23_t = c.sbuf("bc23_t", [128, 6], F32)
    bc4_t = c.sbuf("bc4_t", [128, 3], F32)
    bc5_t = c.sbuf("bc5_t", [128, 3], F32)

    c.push_scope()
    h0_fm = c.sbuf("p1_h0fm", [128, NC * TSL], BF16)
    c.push_scope()
    h0_tm = c.sbuf("p1_h0tm", [128, NT * 1024], BF16)
    p_tr1_cm = nc.psum_tensor("p1_ptr", [128, 2048], BF16)
    p_tr1 = p_tr1_cm.__enter__()
    c.wait("scalar", consts.sem)
    s_h, s_fm, _xsb = _phase_ln(c, "p1", x_in, None, ident, consts[EPS],
                                h0_tm, h0_fm, p_tr1, None)
    c.wait("sync", s_h)
    c.op("sync", lambda e: e.dma_start(
        out=h0tm_in.rearrange("(t p) d -> p t d", p=128),
        in_=h0_tm.ap().rearrange("p (t d) -> p t d", t=NT)), inc=(s_cc, 16))
    c.wait("sync", s_fm)
    c.op("sync", lambda e: e.dma_start(
        out=h0fm_in.rearrange("(ch p) s2 -> p ch s2", p=128),
        in_=h0_fm.ap().rearrange("p (ch s2) -> p ch s2", ch=NC)), inc=(s_cc, 16))
    c.wait("gpsimd", s_cc)
    s_g1 = c.sem("s_g1")
    s_g2 = c.sem("s_g2")
    c.op("gpsimd", lambda e: e.collective_compute(
        "AllGather", ALU.bypass, replica_groups=RG, ins=[h0tm_in],
        outs=[h0tm_g]), inc=(s_g1, 1))
    c.wait("gpsimd", s_g1, 1)
    c.op("gpsimd", lambda e: e.collective_compute(
        "AllGather", ALU.bypass, replica_groups=RG, ins=[h0fm_in],
        outs=[h0fm_g]), inc=(s_g2, 1))
    p_tr1_cm.__exit__(None, None, None)
    c.pop_scope()
    c.barrier()

    c.push_scope()
    wsum_pack = emit_wsum_rep(c, W_in, wsum_d, p_bf="defer",
                              ld_sem_names=("a_eld0", "a_eld1"))
    c.wait("sync", s_g1, 1)
    c.wait("sync", s_g2, 1)
    p_w = c.psum("p2_pw", [128, 2048], BF16)
    wt_ldb = [c.sbuf(f"wt_ldb{i}", [128, D], F32) for i in range(3)]
    wt_cab = [c.sbuf(f"wt_cab{i}", [128, D], BF16) for i in range(3)]
    wt_evb = [c.sbuf(f"wt_evb{i}", [128, D], BF16) for i in range(2)]
    wtb = (wt_ldb, wt_cab, wt_evb)
    s_ewt, EWT_NOC, s_ewt_ev, ewt_oc = emit_weight_transpose(
        c, "p3ewt", ew_in, ewT_d, 8 * D, D, ident, p_w, o_block=D, sp="wt",
        bufs=wtb, defer=True)
    s_awt, AWT_NOC, s_awt_ev, awt_oc = emit_weight_transpose(
        c, "p4awt", attw_in, attwT_d, 5 * D, D, ident, p_w, o_block=D,
        sp="wt", bufs=wtb, defer=True)

    def wt_cb(i):
        if i < 8 or (i - 8) % 2:
            return
        n = (i - 8) // 2
        if n < EWT_NOC:
            ewt_oc(n)
        elif n < EWT_NOC + AWT_NOC:
            if n == EWT_NOC:
                c.wait("tensor", s_ewt_ev, s_ewt_ev.n)
                c.wait("sync", s_ewt_ev, s_ewt_ev.n)
                c.wait("vector", s_ewt[0], s_ewt[0].n)
                c.wait("vector", s_ewt[1], s_ewt[1].n)
                c.wait("vector", s_ewt_ev, s_ewt_ev.n)
            awt_oc(n - EWT_NOC)

    _phase_attn0(c, x_in, h0fm_g, h0tm_g, W_in, x1_d, E_dram, wsum_d,
                 h0_fm, ident, consts, s_fm, tile_cb=wt_cb,
                 wsum_pack=wsum_pack)
    c.pop_scope()
    c.pop_scope()
    c.barrier()

    c.push_scope()
    _phase_moe(c, x1_d, gw_in, ewT_d, x2_d, ident, consts, s_ewt, s_ewt_ev)
    c.pop_scope()
    c.barrier()

    c.push_scope()
    h2_fm = c.sbuf("p4_h2fm", [128, NC * TSL], BF16)
    c.push_scope()
    h2_tm = c.sbuf("p4_h2tm", [128, NT * 1024], BF16)
    p_tr4_cm = nc.psum_tensor("p4_ptr", [128, 2048], BF16)
    p_tr4 = p_tr4_cm.__enter__()
    c.wait("tensor", s_awt_ev, s_awt_ev.n)
    s_h2, s_fm2, _ = _phase_ln(c, "p4", x2_d, None, ident, consts[EPS],
                               h2_tm, h2_fm, p_tr4, None)
    p_tr4_cm.__exit__(None, None, None)
    c.pop_scope()
    c.wait("sync", s_h2, s_h2.n)
    c.wait("sync", s_fm2, s_fm2.n)
    c.wait("vector", s_fm2, s_fm2.n)
    c.wait("tensor", s_fm2, s_fm2.n)
    c.wait("scalar", s_h2, s_h2.n)
    s_pst = _phase_proj(c, h2_fm, s_fm2, attwT_d, s_awt, s_awt_ev,
                        qfm_d, akfm_in, utm_in, None)
    c.wait("gpsimd", s_pst[0], s_pst[0].n)
    c.wait("gpsimd", s_pst[1], s_pst[1].n)
    s_g3 = c.sem("s_g3")
    s_g4 = c.sem("s_g4")
    c.op("gpsimd", lambda e: e.collective_compute(
        "AllGather", ALU.bypass, replica_groups=RG, ins=[akfm_in],
        outs=[akfm_g]), inc=(s_g3, 1))
    c.wait("gpsimd", s_g3, 1)
    c.op("gpsimd", lambda e: e.collective_compute(
        "AllGather", ALU.bypass, replica_groups=RG, ins=[utm_in],
        outs=[utm_g]), inc=(s_g4, 1))
    c.pop_scope()
    c.barrier()
    c.wait("tensor", s_g3, 1)
    c.wait("tensor", s_g4, 1)
    c.wait("sync", s_g3, 1)
    c.wait("sync", s_g4, 1)

    c.push_scope()
    _phase_attn1(c, x2_d, qfm_d, akfm_g, utm_g, x3_d, ident)
    c.pop_scope()
    c.barrier()

    c.push_scope()
    h3_tm = c.sbuf("p6_h3tm", [128, NT * 1024], BF16)
    s_dq, s_dqch, s_dqa = emit_dequant4(c, "p7dq", wug_in, wugdq_d, 2 * DI, D,
                                        sp="dq")
    c.wait("sync", s_dq[0], s_dq[0].n)
    c.wait("sync", s_dq[1], s_dq[1].n)
    p_w_cm = nc.psum_tensor("p7_pw", [128, 2048], BF16)
    p_w = p_w_cm.__enter__()
    s_wt7, _, s_wt7_ev = emit_weight_transpose(
        c, "p7wt", wugdq_d, wugT_d, 2 * DI, D, ident, p_w, o_block=DI,
        in_f32=False, sp="wt")
    c.push_scope()
    p_tr6_cm = nc.psum_tensor("p6_ptr", [128, 2048], BF16)
    p_tr6 = p_tr6_cm.__enter__()
    s_h3, _, _ = _phase_ln(c, "p6", x3_d, None, ident, consts[EPS],
                           h3_tm, None, p_tr6, None)
    p_tr6_cm.__exit__(None, None, None)
    p_am = c.psum("p6_pam", [2, 128], F32)
    c.wait("vector", s_h3)
    am1_sb = c.sbuf("p6_am_keep", [1, 1], F32)
    am1, s_am1 = emit_colmax_to_scalar(c, "p6_am", h3_tm[:, :], NT * 1024,
                                       identf, p_am)
    c.wait("sync", s_am1, s_am1.n)
    bc1, s_bc1 = emit_cc_scales(c, "p6_sc", am1[:, :], s_am1, cc1_in, cc1_out,
                                RG, identf, bc=bc1_t)
    c.pop_scope()
    c.barrier()

    p_w_cm.__exit__(None, None, None)
    c.push_scope()
    p_am = c.psum("p7_pam", [2, 128], F32)
    am2, s_am2 = _phase_ffn_up(c, h3_tm, s_h3, bc1, s_bc1, wugT_d, s_wt7,
                               s_wt7_ev, up_d, gt_d, ident, identf, p_am)
    c.wait("sync", s_am2, s_am2.n)
    bc23, s_bc23 = emit_cc_scales(c, "p7_sc", am2[:, :], s_am2, cc2_in,
                                  cc2_out, RG, identf, n_vals=2, bc=bc23_t)
    c.pop_scope()
    c.pop_scope()
    c.barrier()

    c.push_scope()
    p_am = c.psum("p8_pam", [2, 128], F32)
    bc2 = bc23[:, 0:3]
    bc3 = bc23[:, 3:6]
    am4, s_am4 = _phase_ffn_h(c, up_d, gt_d, bc2, bc3, s_bc23, upq_d, g2_d,
                              identf, p_am)
    c.wait("sync", s_am4, s_am4.n)
    bc4, s_bc4 = emit_cc_scales(c, "p8_sc", am4[:, :], s_am4, cc4_in, cc4_out,
                                RG, identf, bc=bc4_t)
    c.pop_scope()
    c.barrier()

    c.push_scope()
    p_am = c.psum("p9_pam", [2, 128], F32)
    am5, s_am5 = _phase_ffn_i(c, g2_d, upq_d, bc4, s_bc4, o_d, identf, p_am)
    c.wait("sync", s_am5, s_am5.n)
    bc5, s_bc5 = emit_cc_scales(c, "p9_sc", am5[:, :], s_am5, cc5_in, cc5_out,
                                RG, identf, bc=bc5_t)
    c.pop_scope()
    c.barrier()

    c.push_scope()
    rbJ = mk_round_bufs(c, "p10_rb", DI)
    s_dqd, s_dqdch, s_dqda = emit_dequant4(c, "p10dq", wd_in, wdndq_d, D, DI,
                                           rbufs=rbJ, nld=1, sp="dq")
    c.wait("sync", s_dqd[0], s_dqd[0].n)
    c.wait("sync", s_dqd[1], s_dqd[1].n)
    c.wait("scalar", s_dqdch, s_dqdch.n)
    c.wait("vector", s_dqdch, s_dqdch.n)
    c.wait("scalar", s_dqda, s_dqda.n)
    c.wait("vector", s_dqda, s_dqda.n)
    p_w_cm = nc.psum_tensor("p10_pw", [128, 2048], BF16)
    p_w = p_w_cm.__enter__()
    s_wtd, _, s_wtd_ev = emit_weight_transpose(
        c, "p10wt", wdndq_d, wdnT_d, D, DI, ident, p_w, in_f32=False, sp="wt")
    p_w_cm.__exit__(None, None, None)
    _phase_ffn_j(c, o_d, bc5, s_bc5, x3_d, wdnT_d, s_wtd, s_wtd_ev, y_out,
                 ident, rbJ)
    c.pop_scope()

    c.build_block()
    c.close()
    return nc


def _run(nc, im):
    return run_bass_kernel_spmd(nc, im, core_ids=list(range(NCORES))).results


def kernel(L0_attn_x, L0_attn_W, L0_ffn_gate_w, L0_ffn_expert_w,
           L1_attn_Wk, L1_attn_Wq, L1_attn_Wv, L1_attn_Wa, L1_attn_Wb,
           L1_ffn_w_gate, L1_ffn_w_up, L1_ffn_w_down):
    f32 = np.float32
    x = np.ascontiguousarray(np.asarray(L0_attn_x, f32).reshape(T, D))
    W = np.ascontiguousarray(np.asarray(L0_attn_W, f32))
    gw = np.ascontiguousarray(np.asarray(L0_ffn_gate_w, f32))
    ew = np.ascontiguousarray(np.asarray(L0_ffn_expert_w, f32).reshape(8 * D, D))
    attw = np.ascontiguousarray(np.concatenate([
        np.asarray(L1_attn_Wk, f32), np.asarray(L1_attn_Wq, f32),
        np.asarray(L1_attn_Wv, f32), np.asarray(L1_attn_Wa, f32),
        np.asarray(L1_attn_Wb, f32)], 0))
    wug = np.ascontiguousarray(np.concatenate([
        np.asarray(L1_ffn_w_up, f32), np.asarray(L1_ffn_w_gate, f32)], 0))
    wd = np.ascontiguousarray(np.asarray(L1_ffn_w_down, f32))
    xs = _split_tokens(x)
    nc = build_fused()
    im = [{"x": xs[i], "W": W, "gate_w": gw, "expert_w": ew, "attw": attw,
           "wug": wug, "w_down": wd} for i in range(NCORES)]
    r = _run(nc, im)
    y = np.concatenate([q["y"] for q in r], 0)
    return y.reshape(1, T, D)


def kernel_unfused(L0_attn_x, L0_attn_W, L0_ffn_gate_w, L0_ffn_expert_w,
                   L1_attn_Wk, L1_attn_Wq, L1_attn_Wv, L1_attn_Wa, L1_attn_Wb,
                   L1_ffn_w_gate, L1_ffn_w_up, L1_ffn_w_down):
    f32 = np.float32
    x = np.ascontiguousarray(np.asarray(L0_attn_x, f32).reshape(T, D))
    W = np.ascontiguousarray(np.asarray(L0_attn_W, f32))
    gw = np.ascontiguousarray(np.asarray(L0_ffn_gate_w, f32))
    ew = np.ascontiguousarray(np.asarray(L0_ffn_expert_w, f32).reshape(8 * D, D))
    attw = np.ascontiguousarray(np.concatenate([
        np.asarray(L1_attn_Wk, f32), np.asarray(L1_attn_Wq, f32),
        np.asarray(L1_attn_Wv, f32), np.asarray(L1_attn_Wa, f32),
        np.asarray(L1_attn_Wb, f32)], 0))
    wug = np.ascontiguousarray(np.concatenate([
        np.asarray(L1_ffn_w_up, f32), np.asarray(L1_ffn_w_gate, f32)], 0))
    wd = np.ascontiguousarray(np.asarray(L1_ffn_w_down, f32))

    xs = _split_tokens(x)
    r = _run(build_launch_A(), [{"x": xs[i]} for i in range(NCORES)])
    h0tm = np.ascontiguousarray(np.concatenate([q["h_tm"] for q in r], 0))
    h0fm = np.ascontiguousarray(np.concatenate([q["h_fm"] for q in r], 1))
    r = _run(build_launch_B(), [
        {"x": xs[i], "h_fm": h0fm, "h_tm": h0tm, "W": W,
         "hq": np.ascontiguousarray(h0fm[:, i * TSL:(i + 1) * TSL])}
        for i in range(NCORES)])
    x1 = [np.ascontiguousarray(q["x1"]) for q in r]
    r = _run(build_launch_C(), [
        {"x1": x1[i], "gate_w": gw, "expert_w": ew} for i in range(NCORES)])
    x2 = [np.ascontiguousarray(q["x2"]) for q in r]
    r = _run(build_launch_A(), [{"x": x2[i]} for i in range(NCORES)])
    h2fm = [np.ascontiguousarray(q["h_fm"]) for q in r]
    r = _run(build_launch_D(), [
        {"h_fm": h2fm[i], "attw": attw} for i in range(NCORES)])
    qfm = [np.ascontiguousarray(q["q_fm"]) for q in r]
    akfm = np.ascontiguousarray(np.concatenate([q["ak_fm"] for q in r], 1))
    utm = np.ascontiguousarray(np.concatenate([q["u_tm"] for q in r], 0))
    r = _run(build_launch_E(), [
        {"x2": x2[i], "q_fm": qfm[i], "ak_fm": akfm, "u_tm": utm}
        for i in range(NCORES)])
    x3 = [np.ascontiguousarray(q["x3"]) for q in r]
    r = _run(build_launch_A(with_fm=False, with_absmax=True),
             [{"x": x3[i]} for i in range(NCORES)])
    h3 = [np.ascontiguousarray(q["h_tm"]) for q in r]
    am1 = np.ascontiguousarray(np.stack([q["amax"][0] for q in r]))
    r = _run(build_launch_G(), [
        {"h3": h3[i], "amax1": am1, "wug": wug} for i in range(NCORES)])
    upr = [np.ascontiguousarray(q["up_raw"]) for q in r]
    gtr = [np.ascontiguousarray(q["gate_raw"]) for q in r]
    am23 = np.ascontiguousarray(np.stack([q["amax23"][:, 0] for q in r]))
    r = _run(build_launch_H(), [
        {"up_raw": upr[i], "gate_raw": gtr[i], "amax23": am23}
        for i in range(NCORES)])
    upq = [np.ascontiguousarray(q["up_q"]) for q in r]
    g2r = [np.ascontiguousarray(q["gate2raw"]) for q in r]
    am4 = np.ascontiguousarray(np.stack([q["amax4"][0] for q in r]))
    r = _run(build_launch_I(), [
        {"gate2raw": g2r[i], "up_q": upq[i], "amax4": am4}
        for i in range(NCORES)])
    orw = [np.ascontiguousarray(q["o_raw"]) for q in r]
    am5 = np.ascontiguousarray(np.stack([q["amax5"][0] for q in r]))
    r = _run(build_launch_J(), [
        {"o_raw": orw[i], "amax5": am5, "x3": x3[i], "w_down": wd}
        for i in range(NCORES)])
    x4 = np.concatenate([q["x4"] for q in r], 0)
    return x4.reshape(1, T, D)


# version 07
# speedup vs baseline: 1.4728x; 1.0573x over previous
import numpy as np
import ml_dtypes
from contextlib import ExitStack

import concourse.bass as bass
import concourse.mybir as mybir
from concourse.bass_utils import run_bass_kernel_spmd

F32 = mybir.dt.float32
F32R = mybir.dt.float32r
BF16 = mybir.dt.bfloat16
I32 = mybir.dt.int32
AF = mybir.ActivationFunctionType
ALU = mybir.AluOpType
AX = mybir.AxisListType.X
BF16NP = ml_dtypes.bfloat16

NCORES = 8
T = 16384
D = 1024
TSL = T // NCORES
NT = TSL // 128
NC = D // 128
DI = 3072
EPS = 1e-5

ENGINES = ("sync", "vector", "scalar", "tensor", "gpsimd")


class Sem:
    __slots__ = ("h", "n")
    def __init__(self, h):
        self.h = h
        self.n = 0


class Ctx:
    def __init__(self, nc):
        self.nc = nc
        self.st = ExitStack()
        self.q = {k: [] for k in ENGINES}
        self._sems = {}
        self.scopes = [self.st]

    def sem(self, name):
        if name not in self._sems:
            self._sems[name] = Sem(self.st.enter_context(self.nc.semaphore(name)))
        return self._sems[name]

    def sbuf(self, name, shape, dt):
        return self.scopes[-1].enter_context(self.nc.sbuf_tensor(name, shape, dt))

    def psum(self, name, shape, dt):
        return self.scopes[-1].enter_context(self.nc.psum_tensor(name, shape, dt))

    def push_scope(self):
        self.scopes.append(ExitStack())

    def pop_scope(self):
        self.scopes.pop().close()

    def barrier(self):
        for eng in ENGINES:
            for s in self._sems.values():
                if s.n > 0:
                    self.wait(eng, s)

    def op(self, eng, fn, inc=None):
        if inc is not None:
            sem, k = inc
            sem.n += k
            self.q[eng].append(lambda e, fn=fn, s=sem.h, k=k: fn(e).then_inc(s, k))
        else:
            self.q[eng].append(fn)

    def wait(self, eng, sem, val=None):
        v = sem.n if val is None else val
        self.q[eng].append(lambda e, s=sem.h, v=v: e.wait_ge(s, v))

    def chainop(self, eng, sem, fn):
        self.wait(eng, sem)
        self.op(eng, fn, inc=(sem, 1))

    def build_block(self):
        with self.nc.Block() as block:
            @block.sync
            def _(e):
                for fn in self.q["sync"]:
                    fn(e)

            @block.vector
            def _(e):
                for fn in self.q["vector"]:
                    fn(e)

            @block.scalar
            def _(e):
                for fn in self.q["scalar"]:
                    fn(e)

            @block.tensor
            def _(e):
                for fn in self.q["tensor"]:
                    fn(e)

            @block.gpsimd
            def _(e):
                for fn in self.q["gpsimd"]:
                    fn(e)

    def close(self):
        self.st.close()


class Consts:
    def __init__(self, c, values):
        self.vals = list(values)
        self.t = c.sbuf("consts", [128, max(1, len(self.vals))], F32)
        self.sem = c.sem("s_consts")
        for i, v in enumerate(self.vals):
            c.op("gpsimd", lambda e, i=i, v=v: e.memset(self.t[:, i:i + 1], v),
                 inc=(self.sem, 1) if i == len(self.vals) - 1 else None)

    def __getitem__(self, v):
        return self.t[:, self.vals.index(v):self.vals.index(v) + 1]


def make_identity(c, name, dt):
    ident = c.sbuf(name, [128, 128], dt)
    s = c.sem(name + "_s")
    c.op("gpsimd", lambda e, i=ident: e.memset(i[:, :], 0.0), inc=(s, 1))
    c.wait("gpsimd", s)
    c.op("gpsimd", lambda e, i=ident: e.affine_select(
        out=i[:, :], in_=i[:, :], compare_op=ALU.not_equal, fill=1.0,
        base=0, pattern=[[-1, 128]], channel_multiplier=1), inc=(s, 1))
    return ident, s


def emit_ln(c, name, x_sb, nt, h_tm, scr, stats, ident, s_h, sem_xready,
            eps_ap, h_fm=None, p_tr=None, s_fm=None):
    sx, sxv = sem_xready
    ssum = lambda t: stats[:, t:t + 1]
    ssq = lambda t: stats[:, nt + t:nt + t + 1]
    mean = stats[:, 2 * nt:3 * nt]
    msq = stats[:, 3 * nt:4 * nt]
    var = stats[:, 4 * nt:5 * nt]
    rstd = stats[:, 5 * nt:6 * nt]
    negmr = stats[:, 6 * nt:7 * nt]
    sd = stats[:, 7 * nt:8 * nt]

    s_stat = c.sem(name + "_stat")
    s_sq = c.sem(name + "_sq")

    c.wait("vector", sx, sxv)
    c.wait("scalar", sx, sxv)
    for t in range(nt):
        xt = x_sb[:, t * 1024:(t + 1) * 1024]
        c.op("vector", lambda e, xt=xt, o=ssum(t): e.reduce_sum(o, xt, axis=AX),
             inc=(s_stat, 1))
        if t >= 2:
            c.wait("scalar", s_sq, s_sq.n - 1)
        half = (t % 2) * 1024
        c.op("scalar", lambda e, xt=xt, s=scr[:, half:half + 1024], o=ssq(t):
             e.activation(s, xt, AF.Square, accum_out=o), inc=(s_sq, 1))
    c.wait("vector", s_stat)
    c.wait("vector", s_sq)
    c.op("vector", lambda e: e.tensor_scalar_mul(mean, stats[:, 0:nt], 1.0 / 1024.0),
         inc=(s_stat, 1))
    c.op("vector", lambda e: e.tensor_scalar_mul(msq, stats[:, nt:2 * nt], 1.0 / 1024.0),
         inc=(s_stat, 1))
    c.chainop("vector", s_stat,
              lambda e: e.tensor_tensor(var, mean, mean, ALU.mult))
    c.chainop("vector", s_stat,
              lambda e: e.tensor_tensor(var, msq, var, ALU.subtract))
    c.wait("scalar", s_stat)
    c.op("scalar", lambda e: e.activation(sd, var, AF.Sqrt, bias=eps_ap),
         inc=(s_stat, 1))
    c.wait("vector", s_stat)
    c.op("vector", lambda e: e.reciprocal(rstd, sd), inc=(s_stat, 1))
    c.chainop("vector", s_stat, lambda e: e.scalar_tensor_tensor(
        negmr, mean, -1.0, rstd, ALU.mult, ALU.mult))
    c.wait("scalar", s_stat)
    for t in range(nt):
        xt = x_sb[:, t * 1024:(t + 1) * 1024]
        c.op("scalar", lambda e, xt=xt, o=h_tm[:, t * 1024:(t + 1) * 1024], t=t:
             e.activation(o, xt, AF.Identity, scale=rstd[:, t:t + 1],
                          bias=negmr[:, t:t + 1]), inc=(s_h, 1))

    if h_fm is not None:
        emit_transpose_tiles(c, nt, h_tm, h_fm, p_tr, ident, s_h, s_fm,
                             name=name)


def emit_transpose_tiles(c, nt, src_tm, dst_fm, p_tr, ident, s_src, s_dst,
                         name, tokens_total=None, s_src_base=0):
    if tokens_total is None:
        tokens_total = nt * 128
    if s_src_base == 0:
        s_src_base = s_src.n - nt
    ident, s_id = ident
    c.wait("tensor", s_id)
    s_tp = c.sem(name + "_tp")
    for t in range(nt):
        half = (t % 2) * 1024
        c.wait("tensor", s_src, s_src_base + t + 1)
        if t >= 2:
            c.wait("tensor", s_dst, s_dst.n - 1)
        for ch in range(NC):
            c.op("tensor", lambda e, t=t, ch=ch, half=half: e.transpose(
                p_tr[:, half + ch * 128:half + (ch + 1) * 128],
                src_tm[:, t * 1024 + ch * 128:t * 1024 + (ch + 1) * 128],
                ident[:, :]),
                inc=(s_tp, 1) if ch == NC - 1 else None)
        c.wait("vector", s_tp)
        dst = dst_fm.ap().rearrange("p (c w) -> p c w", c=NC)[
            :, :, t * 128:(t + 1) * 128]
        c.op("vector", lambda e, dst=dst, half=half: e.tensor_copy(
            dst, p_tr.ap().rearrange("p (c w) -> p c w", c=16)[
                :, half // 128:half // 128 + 8, :]),
            inc=(s_dst, 1))


def build_launch_A(with_fm=True, with_absmax=False):
    nc = bass.Bass("TRN2", target_bir_lowering=False)
    x_in = nc.dram_tensor("x", [TSL, D], F32, kind="ExternalInput").ap()
    htm_out = nc.dram_tensor("h_tm", [TSL, D], BF16, kind="ExternalOutput").ap()
    if with_fm:
        hfm_out = nc.dram_tensor("h_fm", [D, TSL], BF16, kind="ExternalOutput").ap()
    if with_absmax:
        amax_out = nc.dram_tensor("amax", [1, 1], F32, kind="ExternalOutput").ap()

    c = Ctx(nc)
    x_sb = c.sbuf("x_sb", [128, NT * 1024], F32)
    h_tm = c.sbuf("h_tm_sb", [128, NT * 1024], BF16)
    h_fm = c.sbuf("h_fm_sb", [128, NC * TSL], BF16) if with_fm else None
    scr = c.sbuf("scr", [128, 2048], BF16)
    stats = c.sbuf("stats", [128, 8 * NT], F32)
    p_tr = c.psum("p_tr", [128, 2048], BF16)
    ident = make_identity(c, "ident", BF16)
    consts = Consts(c, [EPS])

    s_x = c.sem("s_x")
    s_h = c.sem("s_h")
    s_fm = c.sem("s_fm")
    s_out = c.sem("s_out")

    c.op("sync", lambda e: e.dma_start(
        out=x_sb.ap().rearrange("p (t d) -> p t d", t=NT),
        in_=x_in.rearrange("(t p) d -> p t d", p=128)), inc=(s_x, 16))

    c.wait("scalar", consts.sem)
    emit_ln(c, "ln0", x_sb, NT, h_tm, scr, stats, ident, s_h, (s_x, 16),
            consts[EPS], h_fm=h_fm, p_tr=p_tr, s_fm=s_fm)

    c.wait("sync", s_h)
    c.op("sync", lambda e: e.dma_start(
        out=htm_out.rearrange("(t p) d -> p t d", p=128),
        in_=h_tm.ap().rearrange("p (t d) -> p t d", t=NT)), inc=(s_out, 16))
    if with_fm:
        c.wait("sync", s_fm)
        c.op("sync", lambda e: e.dma_start(
            out=hfm_out.rearrange("(ch p) s -> p ch s", p=128),
            in_=h_fm.ap().rearrange("p (ch s) -> p ch s", ch=NC)), inc=(s_out, 16))
    if with_absmax:
        s_am = c.sem("s_am")
        amax_pp = c.sbuf("amax_pp", [128, 1], F32)
        amax_r = c.sbuf("amax_r", [1, 1], F32)
        p_am = c.psum("p_am", [1, 128], F32)
        identf = make_identity(c, "identf", F32)
        c.wait("vector", s_h)
        c.op("vector", lambda e: e.tensor_reduce(
            amax_pp[:, :], h_tm[:, :], axis=AX, op=ALU.max,
            apply_absolute_value=True), inc=(s_am, 1))
        c.wait("tensor", s_am)
        c.wait("tensor", identf[1])
        c.op("tensor", lambda e: e.transpose(
            p_am[:, :], amax_pp[:, :], identf[0][:, :]), inc=(s_am, 1))
        c.wait("vector", s_am)
        c.op("vector", lambda e: e.tensor_reduce(
            amax_r[:, :], p_am[:, :], axis=AX, op=ALU.max,
            apply_absolute_value=True), inc=(s_am, 1))
        c.wait("sync", s_am)
        c.op("sync", lambda e: e.dma_start(out=amax_out, in_=amax_r[:, :]),
             inc=(s_out, 16))
    c.wait("sync", s_out)

    c.build_block()
    c.close()
    return nc


def build_launch_D():
    nc = bass.Bass("TRN2", target_bir_lowering=False)
    hfm_in = nc.dram_tensor("h_fm", [D, TSL], BF16, kind="ExternalInput").ap()
    attw_in = nc.dram_tensor("attw", [5 * D, D], F32, kind="ExternalInput").ap()
    qfm_out = nc.dram_tensor("q_fm", [D, TSL], BF16, kind="ExternalOutput").ap()
    akfm_out = nc.dram_tensor("ak_fm", [D, TSL], BF16, kind="ExternalOutput").ap()
    utm_out = nc.dram_tensor("u_tm", [TSL, D], BF16, kind="ExternalOutput").ap()
    wT_d = nc.dram_tensor("attwT", [5 * D, D], BF16, kind="Internal").ap()

    c = Ctx(nc)
    h_fm = c.sbuf("h_fm_sb", [128, NC * TSL], BF16)
    wb = [c.sbuf(f"wb{i}", [128, NC * 1024], BF16) for i in range(2)]
    src2 = c.sbuf("src2", [128, NC * TSL], BF16)
    stg = [c.sbuf(f"stg{i}", [128, 2048], BF16) for i in range(2)]
    ident = make_identity(c, "ident", BF16)
    p_w_cm = nc.psum_tensor("p_w", [128, 2048], BF16)
    p_w = p_w_cm.__enter__()

    s_h = c.sem("s_h")
    s_wl = [c.sem(f"s_wl{i}") for i in range(2)]
    s_mm = c.sem("s_mm")
    s_ev = c.sem("s_ev")
    s_st = [c.sem(f"s_st{i}") for i in range(2)]

    c.op("sync", lambda e: e.dma_start(
        out=h_fm.ap().rearrange("p (ch s) -> p ch s", ch=NC),
        in_=hfm_in.rearrange("(ch p) s -> p ch s", p=128)), inc=(s_h, 16))

    s_wt, _, s_wt_ev = emit_weight_transpose(
        c, "awt", attw_in, wT_d, 5 * D, D, ident, p_w, o_block=D)
    p_w_cm.__exit__(None, None, None)
    psP = [c.psum(f"psP{i}", [128, 2048], F32) for i in range(2)]

    def proj(wi, slot, src, src_sem, src_val, dst_dram, dst_sbuf, kind,
             prev_mm):
        sl = slot % 2
        if prev_mm is not None:
            c.wait("sync", s_mm, prev_mm)
        c.wait("sync", s_wt[0])
        c.wait("sync", s_wt[1])
        c.op("sync", lambda e, wi=wi, sl=sl: e.dma_start(
            out=wb[sl].ap().rearrange("p (ch m) -> p ch m", ch=NC),
            in_=wT_d[wi * D:(wi + 1) * D, :].rearrange(
                "(ch p) m -> p ch m", p=128)), inc=(s_wl[sl], 16))
        c.wait("tensor", s_wl[sl])
        c.wait("tensor", s_wt[0])
        c.wait("tensor", s_wt[1])
        c.wait("tensor", src_sem, src_val)
        for mc in range(8):
            ps = psP[mc % 2]
            if s_ev.n >= 1:
                c.wait("tensor", s_ev, s_ev.n - 1)
            if kind == "fm":
                for ch in range(NC):
                    for hh in range(4):
                        last = (ch == NC - 1 and hh == 3)
                        c.op("tensor", lambda e, ps=ps, ch=ch, hh=hh, mc=mc, sl=sl, src=src:
                             e.matmul(ps[:, hh * 512:(hh + 1) * 512],
                                      wb[sl][:, ch * 1024 + mc * 128:ch * 1024 + (mc + 1) * 128],
                                      src[:, ch * TSL + hh * 512:ch * TSL + (hh + 1) * 512],
                                      start=(ch == 0), stop=(ch == NC - 1)),
                             inc=(s_mm, 1) if last else None)
            else:
                for tt in range(2):
                    t = mc * 2 + tt
                    for ch in range(NC):
                        for hh in range(2):
                            last = (tt == 1 and ch == NC - 1 and hh == 1)
                            c.op("tensor", lambda e, ps=ps, ch=ch, hh=hh, t=t, tt=tt, sl=sl, src=src:
                                 e.matmul(ps[:, tt * 1024 + hh * 512:tt * 1024 + (hh + 1) * 512],
                                          src[:, ch * TSL + t * 128:ch * TSL + (t + 1) * 128],
                                          wb[sl][:, ch * 1024 + hh * 512:ch * 1024 + (hh + 1) * 512],
                                          start=(ch == 0), stop=(ch == NC - 1)),
                                 inc=(s_mm, 1) if last else None)
            c.wait("vector", s_mm)
            if dst_sbuf is not None:
                c.op("vector", lambda e, ps=ps, mc=mc, dst_sbuf=dst_sbuf:
                     e.tensor_copy(dst_sbuf[:, mc * TSL:mc * TSL + 2048], ps[:, :]),
                     inc=(s_ev, 1))
            else:
                c.wait("vector", s_st[mc % 2])
                c.op("vector", lambda e, ps=ps, mc=mc: e.tensor_copy(
                    stg[mc % 2][:, :], ps[:, :]), inc=(s_ev, 1))
                c.wait("sync", s_ev)
                if kind == "fm":
                    c.op("sync", lambda e, mc=mc, dst=dst_dram: e.dma_start(
                        out=dst[mc * 128:(mc + 1) * 128, :],
                        in_=stg[mc % 2][:, :]), inc=(s_st[mc % 2], 16))
                else:
                    c.op("sync", lambda e, mc=mc, dst=dst_dram: e.dma_start(
                        out=dst[mc * 256:(mc + 1) * 256, :].rearrange(
                            "(tt p) d -> p tt d", p=128),
                        in_=stg[mc % 2].ap().rearrange(
                            "p (tt d) -> p tt d", tt=2)),
                        inc=(s_st[mc % 2], 16))
        return s_mm.n

    n0 = proj(0, 0, h_fm[:, :], s_h, 16, None, src2, "fm", None)
    n1 = proj(3, 1, src2[:, :], s_ev, s_ev.n, akfm_out, None, "fm", None)
    n2 = proj(1, 0, h_fm[:, :], s_h, 16, qfm_out, None, "fm", n0)
    n3 = proj(2, 1, h_fm[:, :], s_h, 16, None, src2, "fm", n1)
    n4 = proj(4, 0, src2[:, :], s_ev, s_ev.n, utm_out, None, "tm", n2)

    c.wait("sync", s_st[0])
    c.wait("sync", s_st[1])

    c.build_block()
    c.close()
    return nc


def emit_wsum_rep(c, W_dram, wsum_d, p_bf=None,
                  ld_sem_names=("s_wl0", "s_wl1")):
    wbuf = [c.sbuf(f"wbuf{i}", [128, 1024], F32) for i in range(2)]
    wsum_pc = c.sbuf("wsum_pc", [128, 8], F32)
    wrow = c.sbuf("wrow", [1, 1024], F32)
    ones_row = c.sbuf("ones_row", [1, 128], F32)
    wsum_rep = c.sbuf("wsum_rep", [128, 1024], F32)
    if p_bf is None:
        p_rep = c.psum("p_rep", [128, 512], F32)
    s_w = c.sem("s_w")
    s_wr = c.sem("s_wr")

    s_wl = [c.sem(n) for n in ld_sem_names]
    c.op("gpsimd", lambda e: e.memset(ones_row[:, :], 1.0), inc=(s_w, 1))
    for ch in range(8):
        sl = ch % 2
        if ch >= 2:
            c.wait("sync", s_w, s_w.n - 1)
        c.op("sync", lambda e, ch=ch, sl=sl: e.dma_start(
            out=wbuf[sl][:, :],
            in_=W_dram[ch * 128:(ch + 1) * 128, :]), inc=(s_wl[sl], 16))
        c.wait("vector", s_wl[sl], s_wl[sl].n)
        c.op("vector", lambda e, ch=ch, sl=sl: e.reduce_sum(
            wsum_pc[:, ch:ch + 1], wbuf[sl][:, :], axis=AX),
            inc=(s_w, 1))
    c.wait("sync", s_w, s_w.n)
    for ch in range(8):
        c.wait("sync", s_wr)
        c.op("sync", lambda e, ch=ch: e.dma_start(
            out=wsum_d[0:1, ch * 128:(ch + 1) * 128]
                .rearrange("r (p o) -> p (r o)", p=128),
            in_=wsum_pc[:, ch:ch + 1]), inc=(s_wr, 16))
    c.wait("sync", s_wr)
    c.op("sync", lambda e: e.dma_start(out=wrow[:, :], in_=wsum_d),
         inc=(s_wr, 16))
    if p_bf is None:
        c.wait("tensor", s_wr)
        c.wait("tensor", s_w, s_w.n)
        for hh in range(2):
            if hh == 1:
                c.wait("tensor", s_wr)
            c.op("tensor", lambda e, hh=hh: e.matmul(
                p_rep[:, :], ones_row[:, :], wrow[:, hh * 512:(hh + 1) * 512],
                start=True, stop=True), inc=(s_wr, 1))
            c.wait("vector", s_wr)
            c.op("vector", lambda e, hh=hh: e.tensor_copy(
                wsum_rep[:, hh * 512:(hh + 1) * 512], p_rep[:, :]),
                inc=(s_wr, 1))
        return wsum_rep, s_wr
    def mm_cb(p_ap):
        c.wait("tensor", s_wr, s_wr.n)
        c.wait("tensor", s_w, s_w.n)
        for hh in range(2):
            c.op("tensor", lambda e, hh=hh: e.matmul(
                p_ap[:, hh * 512:(hh + 1) * 512], ones_row[:, :],
                wrow[:, hh * 512:(hh + 1) * 512],
                start=True, stop=True), inc=(s_wr, 1) if hh == 1 else None)
        c.wait("vector", s_wr)
        c.op("vector", lambda e: e.tensor_copy(
            wsum_rep[:, :], p_ap[:, 0:1024]), inc=(s_wr, 1))
    return wsum_rep, s_wr, mm_cb


def build_launch_B():
    nc = bass.Bass("TRN2", target_bir_lowering=False)
    x_in = nc.dram_tensor("x", [TSL, D], F32, kind="ExternalInput").ap()
    hfm_in = nc.dram_tensor("h_fm", [D, T], BF16, kind="ExternalInput").ap()
    htm_in = nc.dram_tensor("h_tm", [T, D], BF16, kind="ExternalInput").ap()
    W_in = nc.dram_tensor("W", [D, D], F32, kind="ExternalInput").ap()
    x1_out = nc.dram_tensor("x1", [TSL, D], F32, kind="ExternalOutput").ap()
    E_dram = nc.dram_tensor("E_scr", [TSL, T], BF16, kind="Internal").ap()
    wsum_d = nc.dram_tensor("wsum_d", [1, D], F32, kind="Internal").ap()

    c = Ctx(nc)
    NKB = T // 1024

    x_t = [c.sbuf(f"x_t{i}", [128, 1024], F32) for i in range(2)]
    hq_fm = c.sbuf("hq_fm", [128, NC * TSL], BF16)
    kfm = [c.sbuf(f"kfm{i}", [128, NC * 1024], BF16) for i in range(2)]
    ktm = [c.sbuf(f"ktm{i}", [128, 8 * 1024], BF16) for i in range(2)]
    Escr = [c.sbuf(f"Escr{i}", [128, 1024], BF16) for i in range(2)]
    Embuf = [c.sbuf(f"Embuf{i}", [128, 1024], BF16) for i in range(2)]
    Emm = [c.sbuf(f"Emm{i}", [128, 1024], BF16) for i in range(2)]
    EmT = [c.sbuf(f"EmT{i}", [128, 1024], BF16) for i in range(2)]
    acc = c.sbuf("acc", [128, 8 * 1024], F32)
    Zparts = c.sbuf("Zparts", [128, NT * NKB], F32)
    SpParts = c.sbuf("SpParts", [128, NT * NKB], F32)
    zstat = c.sbuf("zstat", [128, 6 * NT], F32)
    psA = c.psum("psA", [128, 1024], F32)
    psB = c.psum("psB", [128, 1024], F32)
    p_tr = c.psum("p_tr", [128, 2048], BF16)
    ident = make_identity(c, "ident", BF16)
    consts = Consts(c, [-32.0])

    Z = zstat[:, 0:NT]
    theta = zstat[:, NT:2 * NT]
    invZ = zstat[:, 2 * NT:3 * NT]
    sp = zstat[:, 3 * NT:4 * NT]
    dinv = zstat[:, 4 * NT:5 * NT]
    coef2 = zstat[:, 5 * NT:6 * NT]

    s_x = c.sem("s_x")
    s_xl = [c.sem(f"s_xl{i}") for i in range(2)]
    s_xo = [c.sem(f"s_xo{i}") for i in range(2)]
    s_kf = [c.sem(f"s_kf{i}") for i in range(2)]
    s_kt = [c.sem(f"s_kt{i}") for i in range(2)]
    s_est = [c.sem(f"s_est{i}") for i in range(2)]
    s_eld = [c.sem(f"s_eld{i}") for i in range(2)]
    s_sc = c.sem("s_sc")
    s_exp = c.sem("s_exp")
    s_msk = c.sem("s_msk")
    s_tp = c.sem("s_tp")
    s_ev = c.sem("s_ev")
    s_av = c.sem("s_av")
    s_ac = c.sem("s_ac")
    s_fin = c.sem("s_fin")
    s_out = c.sem("s_out")

    hq_in = nc.dram_tensor("hq", [D, TSL], BF16, kind="ExternalInput").ap()
    c.op("sync", lambda e: e.dma_start(
        out=hq_fm.ap().rearrange("p (ch s) -> p ch s", ch=NC),
        in_=hq_in.rearrange("(ch p) s -> p ch s", p=128)), inc=(s_x, 16))

    wsum_rep, s_wr = emit_wsum_rep(c, W_in, wsum_d)

    c.wait("scalar", consts.sem)
    c.wait("tensor", s_x)
    for kb in range(NKB):
        if kb >= 2:
            c.wait("sync", s_sc, 16 * (kb - 1))
        c.op("sync", lambda e, kb=kb: e.dma_start(
            out=kfm[kb % 2].ap().rearrange("p (ch s) -> p ch s", ch=NC),
            in_=hfm_in[:, kb * 1024:(kb + 1) * 1024]
                .rearrange("(ch p) s -> p ch s", p=128)), inc=(s_kf[kb % 2], 16))
        for qt in range(NT):
            i = kb * NT + qt
            ps = psA if i % 2 == 0 else psB
            c.wait("tensor", s_kf[kb % 2], 16 * (kb // 2 + 1))
            if i >= 2:
                c.wait("tensor", s_exp, i - 1)
            for ch in range(NC):
                for hh in range(2):
                    last = (ch == NC - 1 and hh == 1)
                    c.op("tensor", lambda e, ps=ps, ch=ch, hh=hh, qt=qt, kb=kb:
                         e.matmul(ps[:, hh * 512:(hh + 1) * 512],
                                  hq_fm[:, ch * TSL + qt * 128:ch * TSL + (qt + 1) * 128],
                                  kfm[kb % 2][:, ch * 1024 + hh * 512:ch * 1024 + (hh + 1) * 512],
                                  start=(ch == 0), stop=(ch == NC - 1)),
                         inc=(s_sc, 1) if last else None)
            c.wait("scalar", s_sc)
            if i >= 2:
                c.wait("scalar", s_est[i % 2], 16 * (i // 2))
            c.op("scalar", lambda e, ps=ps, i=i, qt=qt, kb=kb: e.activation(
                Escr[i % 2][:, :], ps[:, :], AF.Exp, bias=consts[-32.0],
                scale=1.0 / 32.0, accum_out=Zparts[:, i:i + 1]),
                inc=(s_exp, 1))
            c.wait("gpsimd", s_exp)
            c.op("gpsimd", lambda e, i=i, qt=qt, kb=kb: e.dma_start(
                out=E_dram[qt * 128:(qt + 1) * 128, kb * 1024:(kb + 1) * 1024],
                in_=Escr[i % 2][:, :]), inc=(s_est[i % 2], 16))

    c.wait("vector", s_exp)
    c.op("vector", lambda e: e.reduce_sum(
        Z, Zparts.ap().rearrange("p (kb qt) -> p qt kb", qt=NT), axis=AX),
        inc=(s_fin, 1))
    c.chainop("vector", s_fin, lambda e: e.tensor_scalar_mul(theta, Z, 0.01))
    c.chainop("vector", s_fin, lambda e: e.reciprocal(invZ, Z))

    c.wait("tensor", s_exp)
    c.wait("tensor", ident[1])
    for g in range(2):
        for kb in range(NKB):
            j = g * NKB + kb
            if j >= 2:
                c.wait("sync", s_av, (j - 1) * 8)
            c.op("sync", lambda e, kb=kb, j=j: e.dma_start(
                out=ktm[j % 2].ap().rearrange("p (sc f) -> p sc f", sc=8),
                in_=htm_in[kb * 1024:(kb + 1) * 1024, :]
                    .rearrange("(sc p) f -> p sc f", p=128)),
                inc=(s_kt[j % 2], 16))
            for ti in range(8):
                qt = g * 8 + ti
                i = j * 8 + ti
                ist = kb * NT + qt
                c.wait("sync", s_est[ist % 2], 16 * (ist // 2 + 1))
                if i >= 2:
                    c.wait("sync", s_msk, i - 1)
                c.op("sync", lambda e, qt=qt, kb=kb, i=i: e.dma_start(
                    out=Embuf[i % 2][:, :],
                    in_=E_dram[qt * 128:(qt + 1) * 128, kb * 1024:(kb + 1) * 1024]),
                    inc=(s_eld[i % 2], 16))
                c.wait("vector", s_eld[i % 2])
                c.wait("vector", s_fin, 3)
                if i >= 2:
                    c.wait("vector", s_tp, i - 1)
                c.op("vector", lambda e, i=i, qt=qt, kb=kb: e.scalar_tensor_tensor(
                    Emm[i % 2][:, :], Embuf[i % 4][:, :], theta[:, qt:qt + 1],
                    Embuf[i % 4][:, :], ALU.is_ge, ALU.mult,
                    accum_out=SpParts[:, kb * NT + qt:kb * NT + qt + 1]),
                    inc=(s_msk, 1))
                c.wait("tensor", s_msk)
                if i >= 2:
                    c.wait("tensor", s_ev, i - 1)
                half = (i % 2) * 1024
                for ch in range(8):
                    c.op("tensor", lambda e, i=i, ch=ch, half=half: e.transpose(
                        p_tr[:, half + ch * 128:half + (ch + 1) * 128],
                        Emm[i % 2][:, ch * 128:(ch + 1) * 128], ident[0][:, :]),
                        inc=(s_tp, 1) if ch == 7 else None)
                c.wait("vector", s_tp)
                if i >= 2:
                    c.wait("vector", s_av, i - 1)
                c.op("vector", lambda e, i=i, half=half: e.tensor_copy(
                    EmT[i % 2][:, :], p_tr[:, half:half + 1024]), inc=(s_ev, 1))
                c.wait("tensor", s_ev)
                c.wait("tensor", s_kt[j % 2], 16 * (j // 2 + 1))
                ps = psA if i % 2 == 0 else psB
                if i >= 2:
                    c.wait("tensor", s_ac, i - 1)
                for sc in range(8):
                    for hh in range(2):
                        last = (sc == 7 and hh == 1)
                        c.op("tensor", lambda e, ps=ps, sc=sc, hh=hh, i=i, j=j:
                             e.matmul(ps[:, hh * 512:(hh + 1) * 512],
                                      EmT[i % 2][:, sc * 128:(sc + 1) * 128],
                                      ktm[j % 2][:, sc * 1024 + hh * 512:sc * 1024 + (hh + 1) * 512],
                                      start=(sc == 0), stop=(sc == 7)),
                             inc=(s_av, 1) if last else None)
                c.wait("vector", s_av)
                if kb == 0:
                    c.wait("vector", s_fin)
                else:
                    c.wait("vector", s_ac, i - 7)
                c.op("vector", lambda e, ps=ps, ti=ti, kb=kb: (
                    e.tensor_copy(acc[:, ti * 1024:(ti + 1) * 1024], ps[:, :])
                    if kb == 0 else
                    e.tensor_add(acc[:, ti * 1024:(ti + 1) * 1024],
                                 acc[:, ti * 1024:(ti + 1) * 1024], ps[:, :])),
                    inc=(s_ac, 1))

        c.wait("vector", s_msk)
        c.wait("vector", s_ac)
        c.op("vector", lambda e, g=g: e.reduce_sum(
            sp[:, g * 8:(g + 1) * 8],
            SpParts.ap().rearrange("p (kb qt) -> p qt kb", qt=NT)[:, g * 8:(g + 1) * 8, :],
            axis=AX), inc=(s_fin, 1))
        c.chainop("vector", s_fin, lambda e, g=g: e.scalar_tensor_tensor(
            dinv[:, g * 8:(g + 1) * 8], Z[:, g * 8:(g + 1) * 8], 1e-9,
            sp[:, g * 8:(g + 1) * 8], ALU.mult, ALU.add))
        c.chainop("vector", s_fin, lambda e, g=g: e.reciprocal(
            dinv[:, g * 8:(g + 1) * 8], dinv[:, g * 8:(g + 1) * 8]))
        c.chainop("vector", s_fin, lambda e, g=g: e.tensor_sub(
            coef2[:, g * 8:(g + 1) * 8], Z[:, g * 8:(g + 1) * 8],
            sp[:, g * 8:(g + 1) * 8]))
        c.chainop("vector", s_fin, lambda e, g=g: e.tensor_scalar_mul(
            coef2[:, g * 8:(g + 1) * 8], coef2[:, g * 8:(g + 1) * 8], 0.01))
        c.chainop("vector", s_fin, lambda e, g=g: e.tensor_tensor(
            coef2[:, g * 8:(g + 1) * 8], coef2[:, g * 8:(g + 1) * 8],
            invZ[:, g * 8:(g + 1) * 8], ALU.mult))
        c.wait("vector", s_wr)
        for ti in range(8):
            qt = g * 8 + ti
            xt = x_t[qt % 2]
            if qt >= 2:
                c.wait("sync", s_xo[qt % 2], 16 * (qt // 2))
            c.op("sync", lambda e, qt=qt, xt=xt: e.dma_start(
                out=xt[:, :], in_=x_in[qt * 128:(qt + 1) * 128, :]),
                inc=(s_xl[qt % 2], 16))
            c.wait("vector", s_xl[qt % 2])
            c.chainop("vector", s_fin, lambda e, xt=xt, ti=ti, qt=qt:
                      e.scalar_tensor_tensor(
                          xt[:, :], acc[:, ti * 1024:(ti + 1) * 1024],
                          dinv[:, qt:qt + 1], xt[:, :], ALU.mult, ALU.add))
            c.chainop("vector", s_fin, lambda e, xt=xt, qt=qt:
                      e.scalar_tensor_tensor(
                          xt[:, :], wsum_rep[:, :], coef2[:, qt:qt + 1], xt[:, :],
                          ALU.mult, ALU.add))
            c.wait("sync", s_fin)
            c.op("sync", lambda e, qt=qt, xt=xt: e.dma_start(
                out=x1_out[qt * 128:(qt + 1) * 128, :], in_=xt[:, :]),
                inc=(s_xo[qt % 2], 16))

    c.wait("sync", s_xo[0])
    c.wait("sync", s_xo[1])

    c.build_block()
    c.close()
    return nc


def emit_weight_transpose(c, name, w_in, wT_out, O, I, ident, p_w,
                          o_block=None, in_f32=True, sp=None, bufs=None,
                          defer=False):
    if o_block is None:
        o_block = O
    if sp is None:
        sp = name
    ldt = F32 if in_f32 else BF16
    if bufs is not None:
        ldb, cab, evb = bufs
    else:
        ldb = [c.sbuf(f"{name}_ld{i}", [128, I], ldt) for i in range(2)]
        cab = [c.sbuf(f"{name}_ca{i}", [128, I], BF16) for i in range(len(ldb))] \
            if in_f32 else ldb
        evb = [c.sbuf(f"{name}_ev{i}", [128, min(I, 1024)], BF16)
               for i in range(2)]
    nld = len(ldb)
    s_ld = [c.sem(f"{sp}_ld{i}") for i in range(nld)]
    s_ca = c.sem(f"{sp}_ca")
    s_tp = c.sem(f"{sp}_tp")
    s_ev = c.sem(f"{sp}_ev")
    s_st = [c.sem(f"{sp}_st{i}") for i in range(2)]
    NIC = I // 128
    NOC = O // 128
    NG = (NIC + 7) // 8
    c.wait("tensor", ident[1])

    def emit_oc(oc):
        sl = oc % nld
        if oc >= nld:
            if in_f32:
                c.wait("sync", s_ca, s_ca.n - (nld - 1))
            else:
                c.wait("sync", s_tp, s_tp.n - (nld - 1) * NG)
        c.op("sync", lambda e, oc=oc, sl=sl: e.dma_start(
            out=ldb[sl][:, :], in_=w_in[oc * 128:(oc + 1) * 128, :]),
            inc=(s_ld[sl], 16))
        if in_f32:
            c.wait("vector", s_ld[sl], s_ld[sl].n)
            if oc >= nld:
                c.wait("vector", s_tp, s_tp.n - (nld - 1) * NG)
            c.op("vector", lambda e, sl=sl: e.tensor_copy(
                cab[sl][:, :], ldb[sl][:, :]), inc=(s_ca, 1))
            c.wait("tensor", s_ca)
        else:
            c.wait("tensor", s_ld[sl], s_ld[sl].n)
        nfc = o_block // 128
        blk, fc = oc // nfc, oc % nfc
        nh = p_w.shape[1] // 1024
        for g in range(NG):
            nic_g = min(8, NIC - g * 8)
            gg = oc * NG + g
            half = (gg % nh) * 1024
            if gg >= nh:
                c.wait("tensor", s_ev, s_ev.n - (nh - 1))
            for ic in range(nic_g):
                c.op("tensor", lambda e, sl=sl, g=g, ic=ic, half=half:
                     e.transpose(p_w[:, half + ic * 128:half + (ic + 1) * 128],
                                 cab[sl][:, (g * 8 + ic) * 128:(g * 8 + ic + 1) * 128],
                                 ident[0][:, :]),
                     inc=(s_tp, 1) if ic == nic_g - 1 else None)
            c.wait("vector", s_tp)
            if gg >= 2:
                c.wait("vector", s_st[gg % 2], s_st[gg % 2].n)
            c.op("vector", lambda e, gg=gg, half=half, nic_g=nic_g:
                 e.tensor_copy(evb[gg % 2][:, 0:nic_g * 128],
                               p_w[:, half:half + nic_g * 128]), inc=(s_ev, 1))
            c.wait("sync", s_ev)
            dst = wT_out[blk * I + g * 1024:blk * I + g * 1024 + nic_g * 128, :]\
                .rearrange("(ic p) o -> p ic o", p=128)[
                    :, :, fc * 128:(fc + 1) * 128]
            c.op("sync", lambda e, dst=dst, gg=gg, nic_g=nic_g: e.dma_start(
                out=dst, in_=evb[gg % 2].ap().rearrange(
                    "p (ic j) -> p ic j", ic=nic_g)), inc=(s_st[gg % 2], 16))

    if defer:
        return s_st, NOC, s_ev, emit_oc
    for oc in range(NOC):
        emit_oc(oc)
    return s_st, NOC, s_ev


def build_launch_E():
    nc = bass.Bass("TRN2", target_bir_lowering=False)
    x_in = nc.dram_tensor("x2", [TSL, D], F32, kind="ExternalInput").ap()
    qfm_in = nc.dram_tensor("q_fm", [D, TSL], BF16, kind="ExternalInput").ap()
    akfm_in = nc.dram_tensor("ak_fm", [D, T], BF16, kind="ExternalInput").ap()
    utm_in = nc.dram_tensor("u_tm", [T, D], BF16, kind="ExternalInput").ap()
    x3_out = nc.dram_tensor("x3", [TSL, D], F32, kind="ExternalOutput").ap()

    c = Ctx(nc)
    NKB = T // 1024

    q_fm = c.sbuf("q_fm_sb", [128, NC * TSL], BF16)
    kfm = [c.sbuf(f"kfm{i}", [128, NC * 1024], BF16) for i in range(2)]
    ktm = [c.sbuf(f"ktm{i}", [128, 8 * 1024], BF16) for i in range(2)]
    Escr = [c.sbuf(f"Escr{i}", [128, 1024], BF16) for i in range(2)]
    EmT = [c.sbuf(f"EmT{i}", [128, 1024], BF16) for i in range(2)]
    acc = c.sbuf("acc", [128, NT * 1024], F32)
    Zparts = c.sbuf("Zparts", [128, NT * NKB], F32)
    fstat = c.sbuf("fstat", [128, 4 * NT + 4], F32)
    x_t = [c.sbuf(f"x_t{i}", [128, 1024], F32) for i in range(2)]
    psA = c.psum("psA", [128, 1024], F32)
    psB = c.psum("psB", [128, 1024], F32)
    p_tr = c.psum("p_tr", [128, 2048], BF16)
    ps_av = c.psum("ps_av", [128, 1024], F32)
    ident = make_identity(c, "ident", BF16)

    mr = fstat[:, 0:NT]
    negm = fstat[:, NT:2 * NT]
    Zr = fstat[:, 2 * NT:3 * NT]
    invZ = fstat[:, 3 * NT:4 * NT]
    bm = fstat[:, 4 * NT:4 * NT + 1]
    dcol = fstat[:, 4 * NT + 1:4 * NT + 2]
    alph = fstat[:, 4 * NT + 2:4 * NT + 3]

    s_q = c.sem("s_q")
    s_kf = [c.sem(f"s_kf{i}") for i in range(2)]
    s_kt = [c.sem(f"s_kt{i}") for i in range(2)]
    s_xl = [c.sem(f"s_xl{i}") for i in range(2)]
    s_xo = [c.sem(f"s_xo{i}") for i in range(2)]
    s_sc = c.sem("s_sc")
    s_v = c.sem("s_v")
    s_al = c.sem("s_al")
    s_exp = c.sem("s_exp")
    s_tp = c.sem("s_tp")
    s_av = c.sem("s_av")

    ev_cnt = {}
    upd_cnt = {}
    al_cnt = {}

    c.op("sync", lambda e: e.dma_start(
        out=q_fm.ap().rearrange("p (ch s) -> p ch s", ch=NC),
        in_=qfm_in.rearrange("(ch p) s -> p ch s", p=128)), inc=(s_q, 16))
    c.wait("tensor", s_q)
    c.wait("tensor", ident[1])

    for kb in range(NKB):
        if kb >= 2:
            c.wait("sync", s_sc, 16 * (kb - 1))
        c.op("sync", lambda e, kb=kb: e.dma_start(
            out=kfm[kb % 2].ap().rearrange("p (ch s) -> p ch s", ch=NC),
            in_=akfm_in[:, kb * 1024:(kb + 1) * 1024]
                .rearrange("(ch p) s -> p ch s", p=128)), inc=(s_kf[kb % 2], 16))
        if kb >= 2:
            c.wait("sync", s_av, (kb - 1) * 16)
        c.op("sync", lambda e, kb=kb: e.dma_start(
            out=ktm[kb % 2].ap().rearrange("p (sc f) -> p sc f", sc=8),
            in_=utm_in[kb * 1024:(kb + 1) * 1024, :]
                .rearrange("(sc p) f -> p sc f", p=128)), inc=(s_kt[kb % 2], 16))
        for qt in range(NT):
            i = kb * NT + qt
            ps = psA if i % 2 == 0 else psB
            c.wait("tensor", s_kf[kb % 2], 16 * (kb // 2 + 1))
            if i >= 2:
                c.wait("tensor", s_exp, i - 1)
            for ch in range(NC):
                for hh in range(2):
                    last = (ch == NC - 1 and hh == 1)
                    c.op("tensor", lambda e, ps=ps, ch=ch, hh=hh, qt=qt, kb=kb:
                         e.matmul(ps[:, hh * 512:(hh + 1) * 512],
                                  q_fm[:, ch * TSL + qt * 128:ch * TSL + (qt + 1) * 128],
                                  kfm[kb % 2][:, ch * 1024 + hh * 512:ch * 1024 + (hh + 1) * 512],
                                  start=(ch == 0), stop=(ch == NC - 1)),
                         inc=(s_sc, 1) if last else None)
            c.wait("vector", s_sc)
            if kb == 0:
                c.chainop("vector", s_v, lambda e, ps=ps, qt=qt: e.reduce_max(
                    mr[:, qt:qt + 1], ps[:, :], axis=AX))
                c.chainop("vector", s_v, lambda e, qt=qt: e.tensor_scalar_mul(
                    negm[:, qt:qt + 1], mr[:, qt:qt + 1], -1.0))
            else:
                if i - 1 in al_cnt:
                    c.wait("vector", s_al, al_cnt[i - 1])
                c.chainop("vector", s_v, lambda e, ps=ps: e.reduce_max(
                    bm, ps[:, :], axis=AX))
                c.chainop("vector", s_v, lambda e, qt=qt: e.tensor_sub(
                    dcol, mr[:, qt:qt + 1], bm))
                c.chainop("vector", s_v, lambda e: e.tensor_scalar_min(
                    dcol, dcol, 0.0))
                c.chainop("vector", s_v, lambda e, qt=qt: e.tensor_max(
                    mr[:, qt:qt + 1], mr[:, qt:qt + 1], bm))
                c.chainop("vector", s_v, lambda e, qt=qt: e.tensor_scalar_mul(
                    negm[:, qt:qt + 1], mr[:, qt:qt + 1], -1.0))
            v_after_chain = s_v.n
            c.wait("scalar", s_v, v_after_chain)
            if kb > 0:
                c.op("scalar", lambda e: e.activation(alph, dcol, AF.Exp),
                     inc=(s_al, 1))
                al_cnt[i] = s_al.n
            if i >= 2:
                c.wait("scalar", s_tp, i - 1)
            c.op("scalar", lambda e, ps=ps, i=i, qt=qt: e.activation(
                Escr[i % 2][:, :], ps[:, :], AF.Exp, bias=negm[:, qt:qt + 1],
                accum_out=Zparts[:, i:i + 1]), inc=(s_exp, 1))
            c.wait("tensor", s_exp)
            if i >= 2:
                c.wait("tensor", s_v, ev_cnt[i - 2])
            half = (i % 2) * 1024
            for ch in range(8):
                c.op("tensor", lambda e, i=i, ch=ch, half=half: e.transpose(
                    p_tr[:, half + ch * 128:half + (ch + 1) * 128],
                    Escr[i % 2][:, ch * 128:(ch + 1) * 128], ident[0][:, :]),
                    inc=(s_tp, 1) if ch == 7 else None)
            c.wait("vector", s_tp)
            if i >= 2:
                c.wait("vector", s_av, i - 1)
            c.chainop("vector", s_v, lambda e, i=i, half=half: e.tensor_copy(
                EmT[i % 2][:, :], p_tr[:, half:half + 1024]))
            ev_cnt[i] = s_v.n
            c.wait("tensor", s_v, ev_cnt[i])
            c.wait("tensor", s_kt[kb % 2], 16 * (kb // 2 + 1))
            if i >= 1:
                c.wait("tensor", s_v, upd_cnt[i - 1])
            for sc in range(8):
                for hh in range(2):
                    last = (sc == 7 and hh == 1)
                    c.op("tensor", lambda e, sc=sc, hh=hh, i=i, kb=kb:
                         e.matmul(ps_av[:, hh * 512:(hh + 1) * 512],
                                  EmT[i % 2][:, sc * 128:(sc + 1) * 128],
                                  ktm[kb % 2][:, sc * 1024 + hh * 512:sc * 1024 + (hh + 1) * 512],
                                  start=(sc == 0), stop=(sc == 7)),
                         inc=(s_av, 1) if last else None)
            at = acc[:, qt * 1024:(qt + 1) * 1024]
            c.wait("vector", s_av)
            if kb > 0:
                c.wait("vector", s_al, al_cnt[i])
            if kb == 0:
                c.chainop("vector", s_v, lambda e, at=at: e.tensor_copy(
                    at, ps_av[:, :]))
                c.chainop("vector", s_v, lambda e, qt=qt, i=i: e.tensor_copy(
                    Zr[:, qt:qt + 1], Zparts[:, i:i + 1]))
            else:
                c.chainop("vector", s_v, lambda e, at=at: e.scalar_tensor_tensor(
                    at, at, alph, ps_av[:, :], ALU.mult, ALU.add))
                c.chainop("vector", s_v, lambda e, qt=qt, i=i:
                          e.scalar_tensor_tensor(
                              Zr[:, qt:qt + 1], Zr[:, qt:qt + 1], alph,
                              Zparts[:, i:i + 1], ALU.mult, ALU.add))
            upd_cnt[i] = s_v.n

    c.chainop("vector", s_v, lambda e: e.reciprocal(invZ, Zr))
    for qt in range(NT):
        xt = x_t[qt % 2]
        if qt >= 2:
            c.wait("sync", s_xo[qt % 2], 16 * (qt // 2))
        c.op("sync", lambda e, qt=qt, xt=xt: e.dma_start(
            out=xt[:, :], in_=x_in[qt * 128:(qt + 1) * 128, :]),
            inc=(s_xl[qt % 2], 16))
        c.wait("vector", s_xl[qt % 2], 16 * (qt // 2 + 1))
        c.chainop("vector", s_v, lambda e, xt=xt, qt=qt: e.scalar_tensor_tensor(
            xt[:, :], acc[:, qt * 1024:(qt + 1) * 1024], invZ[:, qt:qt + 1],
            xt[:, :], ALU.mult, ALU.add))
        c.wait("sync", s_v)
        c.op("sync", lambda e, qt=qt, xt=xt: e.dma_start(
            out=x3_out[qt * 128:(qt + 1) * 128, :], in_=xt[:, :]),
            inc=(s_xo[qt % 2], 16))

    c.wait("sync", s_xo[0])
    c.wait("sync", s_xo[1])

    c.build_block()
    c.close()
    return nc


def build_launch_C():
    nc = bass.Bass("TRN2", target_bir_lowering=False)
    x_in = nc.dram_tensor("x1", [TSL, D], F32, kind="ExternalInput").ap()
    gw_in = nc.dram_tensor("gate_w", [8, D], F32, kind="ExternalInput").ap()
    ew_in = nc.dram_tensor("expert_w", [8 * D, D], F32, kind="ExternalInput").ap()
    x2_out = nc.dram_tensor("x2", [TSL, D], F32, kind="ExternalOutput").ap()
    ewT_d = nc.dram_tensor("ewT", [8 * D, D], BF16, kind="Internal").ap()

    c = Ctx(nc)
    x_sb = c.sbuf("x_sb", [128, NT * 1024], F32)
    h_tm = c.sbuf("h_tm", [128, NT * 1024], BF16)
    h_fm = c.sbuf("h_fm", [128, NC * TSL], BF16)
    scr = c.sbuf("scr", [128, 2048], BF16)
    stats = c.sbuf("stats", [128, 8 * NT], F32)
    gwT = c.sbuf("gwT", [128, NC * 8], BF16)
    gwb = c.sbuf("gwb", [8, D], F32)
    gwb16 = c.sbuf("gwb16", [8, D], BF16)
    mask8 = c.sbuf("mask8", [128, NT * 8], F32)
    gmax = c.sbuf("gmax", [128, NT], F32)
    wq = [c.sbuf(f"wq{e}", [128, NC * 256], BF16) for e in range(8)]
    accq = c.sbuf("accq", [128, NT * 256], BF16)
    ident = make_identity(c, "ident", BF16)
    consts = Consts(c, [EPS])

    s_x = c.sem("s_x")
    s_h = c.sem("s_h")
    s_fm = c.sem("s_fm")
    s_gw = c.sem("s_gw")
    s_gt = c.sem("s_gt")
    s_wq = [c.sem(f"s_wq{e}") for e in range(8)]
    s_pe = c.sem("s_pe")
    s_evq = c.sem("s_evq")
    s_fin = c.sem("s_fin")
    s_out = c.sem("s_out")

    p_tr_cm = nc.psum_tensor("p_tr", [128, 2048], BF16)
    p_tr = p_tr_cm.__enter__()
    p_g_cm = nc.psum_tensor("p_g", [128, 8], F32)
    p_g = p_g_cm.__enter__()
    p_w_cm = nc.psum_tensor("p_w", [128, 2048], BF16)
    p_w = p_w_cm.__enter__()

    c.op("sync", lambda e: e.dma_start(
        out=x_sb.ap().rearrange("p (t d) -> p t d", t=NT),
        in_=x_in.rearrange("(t p) d -> p t d", p=128)), inc=(s_x, 16))
    c.wait("scalar", consts.sem)
    emit_ln(c, "ln1", x_sb, NT, h_tm, scr, stats, ident, s_h, (s_x, 16),
            consts[EPS], h_fm=h_fm, p_tr=p_tr, s_fm=s_fm)

    s_wt, _, s_ewt_ev = emit_weight_transpose(c, "ewt", ew_in, ewT_d, 8 * D, D,
                                              ident, p_w, o_block=D)

    c.op("sync", lambda e: e.dma_start(out=gwb[:, :], in_=gw_in),
         inc=(s_gw, 16))
    c.wait("vector", s_gw)
    c.op("vector", lambda e: e.tensor_copy(gwb16[:, :], gwb[:, :]),
         inc=(s_gw, 1))
    c.wait("tensor", s_gw)
    c.wait("tensor", ident[1])
    c.wait("tensor", s_fm)
    for ch in range(NC):
        c.op("tensor", lambda e, ch=ch: e.transpose(
            p_tr[:, ch * 128:ch * 128 + 8],
            gwb16[:, ch * 128:(ch + 1) * 128], ident[0][0:8, 0:8]),
            inc=(s_gw, 1) if ch == NC - 1 else None)
    c.wait("vector", s_gw)
    c.op("vector", lambda e: e.tensor_copy(
        gwT.ap().rearrange("p (ch j) -> p ch j", ch=NC),
        p_tr.ap().rearrange("p (ch j) -> p ch j", ch=16)[:, 0:NC, 0:8]),
        inc=(s_gw, 1))

    for t in range(NT):
        c.wait("tensor", s_gw)
        c.wait("tensor", s_fm, t + 1)
        if t >= 1:
            c.wait("tensor", s_gt, 2 * t)
        for ch in range(NC):
            c.op("tensor", lambda e, ch=ch, t=t: e.matmul(
                p_g[:, :], h_fm[:, ch * TSL + t * 128:ch * TSL + (t + 1) * 128],
                gwT[:, ch * 8:(ch + 1) * 8],
                start=(ch == 0), stop=(ch == NC - 1)),
                inc=(s_pe, 1) if ch == NC - 1 else None)
        c.wait("vector", s_pe)
        c.op("vector", lambda e, t=t: e.reduce_max(
            gmax[:, t:t + 1], p_g[:, :], axis=AX), inc=(s_gt, 1))
        c.wait("vector", s_gt)
        c.op("vector", lambda e, t=t: e.tensor_single_scalar(
            mask8[:, t * 8:(t + 1) * 8], p_g[:, :], gmax[:, t:t + 1],
            ALU.is_ge), inc=(s_gt, 1))

    p_w_cm.__exit__(None, None, None)
    p_g_cm.__exit__(None, None, None)
    p_tr_cm.__exit__(None, None, None)
    psE = [c.psum(f"psE{i}", [128, 8 * 256], F32) for i in range(2)]

    c.wait("tensor", s_gt)
    c.wait("tensor", s_gw)
    c.wait("tensor", s_fm)
    c.wait("tensor", s_ewt_ev)

    for fq in range(4):
        for e8 in range(8):
            if fq >= 1:
                c.wait("sync", s_pe, NT + fq * NT)
            c.op("sync", lambda e, e8=e8, fq=fq: e.dma_start(
                out=wq[e8].ap().rearrange("p (ch j) -> p ch j", ch=NC),
                in_=ewT_d[e8 * D:(e8 + 1) * D, fq * 256:(fq + 1) * 256]
                    .rearrange("(ch p) j -> p ch j", p=128)),
                inc=(s_wq[e8], 16))
        for t in range(NT):
            gi = fq * NT + t
            grp = psE[gi % 2]
            for e8 in range(8):
                c.wait("tensor", s_wq[e8], 16 * (fq + 1))
            c.wait("tensor", s_wt[0], 16 * 32)
            c.wait("tensor", s_wt[1], 16 * 32)
            if gi >= 2:
                c.wait("tensor", s_evq, (gi - 1) * 8)
            for e8 in range(8):
                for ch in range(NC):
                    last = (ch == NC - 1 and e8 == 7)
                    c.op("tensor", lambda e, grp=grp, ch=ch, e8=e8, t=t:
                         e.matmul(grp[:, e8 * 256:(e8 + 1) * 256],
                                  h_fm[:, ch * TSL + t * 128:ch * TSL + (t + 1) * 128],
                                  wq[e8][:, ch * 256:(ch + 1) * 256],
                                  start=(ch == 0), stop=(ch == NC - 1)),
                         inc=(s_pe, 1) if last else None)
            c.wait("vector", s_pe)
            for e8 in range(8):
                aq = accq[:, t * 256:(t + 1) * 256]
                if e8 == 0:
                    if fq >= 1:
                        c.wait("vector", s_fin, 2 * ((fq - 1) * NT + t + 1))
                    c.op("vector", lambda e, grp=grp, aq=aq, t=t: e.tensor_single_scalar(
                        aq, grp[:, 0:256], mask8[:, t * 8:t * 8 + 1], ALU.mult),
                        inc=(s_evq, 1))
                else:
                    c.wait("vector", s_evq)
                    c.op("vector", lambda e, grp=grp, aq=aq, t=t, e8=e8:
                         e.scalar_tensor_tensor(
                             aq, grp[:, e8 * 256:(e8 + 1) * 256],
                             mask8[:, t * 8 + e8:t * 8 + e8 + 1], aq,
                             ALU.mult, ALU.add), inc=(s_evq, 1))
            xq = x_sb[:, t * 1024 + fq * 256:t * 1024 + (fq + 1) * 256]
            hq = h_tm[:, t * 1024 + fq * 256:t * 1024 + (fq + 1) * 256]
            c.wait("vector", s_evq)
            c.op("vector", lambda e, xq=xq, hq=hq: e.tensor_add(xq, xq, hq),
                 inc=(s_fin, 1))
            c.chainop("vector", s_fin, lambda e, xq=xq, t=t: e.tensor_add(
                xq, xq, accq[:, t * 256:(t + 1) * 256]))

    c.wait("sync", s_fin)
    c.op("sync", lambda e: e.dma_start(
        out=x2_out.rearrange("(t p) d -> p t d", p=128),
        in_=x_sb.ap().rearrange("p (t d) -> p t d", t=NT)), inc=(s_out, 16))
    c.wait("sync", s_out)

    c.build_block()
    c.close()
    return nc


def emit_bcast_scales(c, name, am_in, nrows, qmax, identf):
    amb = c.sbuf(name + "_amb", [nrows, 1], F32)
    amr = c.sbuf(name + "_amr", [1, 4], F32)
    ones_row = c.sbuf(name + "_ones", [1, 128], F32)
    bc = c.sbuf(name + "_bc", [128, 3], F32)
    p_t = c.psum(name + "_pt", [1, 128], F32)
    p_b = c.psum(name + "_pb", [128, 3], F32)
    s = c.sem("bcs_s")
    c.op("gpsimd", lambda e: e.memset(ones_row[:, :], 1.0), inc=(s, 1))
    c.op("sync", lambda e: e.dma_start(out=amb[:, :], in_=am_in,
                                       allow_slow_non_contiguous=True),
         inc=(s, 16))
    c.wait("tensor", s)
    c.wait("tensor", identf[1])
    c.op("tensor", lambda e: e.transpose(
        p_t[:, 0:nrows], amb[:, :], identf[0][0:nrows, 0:nrows]), inc=(s, 1))
    c.wait("vector", s)
    c.op("vector", lambda e: e.reduce_max(
        amr[:, 0:1], p_t[:, 0:nrows], axis=AX), inc=(s, 1))
    c.chainop("vector", s, lambda e: e.tensor_scalar(
        amr[:, 1:2], amr[:, 0:1], 1.0 / qmax, 1e-12, ALU.mult, ALU.max))
    c.chainop("vector", s, lambda e: e.reciprocal(amr[:, 2:3], amr[:, 1:2]))
    c.chainop("vector", s, lambda e: e.tensor_scalar_mul(
        amr[:, 3:4], amr[:, 2:3], -1.0))
    c.wait("tensor", s)
    c.op("tensor", lambda e: e.matmul(
        p_b[:, :], ones_row[:, :], amr[:, 1:4], start=True, stop=True),
        inc=(s, 1))
    c.wait("vector", s)
    c.op("vector", lambda e: e.tensor_copy(bc[:, :], p_b[:, :]), inc=(s, 1))
    return bc, s


def emit_round(c, eng_s, src, w, bufs, inv_neg_scale, s_ch):
    ib, cfb, db, ab, rb = bufs["i"], bufs["cf"], bufs["d"], bufs["a"], bufs["r"]
    c.wait("scalar", s_ch)
    c.op("scalar", lambda e: e.activation(
        ib[:, 0:w], src, AF.Identity, scale=inv_neg_scale), inc=(eng_s, 1))
    c.wait("scalar", eng_s)
    c.op("scalar", lambda e: e.activation(cfb[:, 0:w], ib[:, 0:w], AF.Copy),
         inc=(eng_s, 1))
    c.wait("vector", eng_s)
    c.chainop("vector", s_ch, lambda e: e.scalar_tensor_tensor(
        db[:, 0:w], src, inv_neg_scale, cfb[:, 0:w], ALU.mult, ALU.subtract))
    c.chainop("vector", s_ch, lambda e: e.scalar_tensor_tensor(
        ab[:, 0:w], db[:, 0:w], 0.5, cfb[:, 0:w], ALU.is_ge, ALU.add))
    c.chainop("vector", s_ch, lambda e: e.scalar_tensor_tensor(
        rb[:, 0:w], db[:, 0:w], -0.5, ab[:, 0:w], ALU.is_le, ALU.subtract))


def mk_round_bufs(c, name, w):
    cf = c.sbuf(name + "_cf", [128, w], F32)
    return {"i": c.sbuf(name + "_i", [128, w], I32),
            "cf": cf,
            "d": c.sbuf(name + "_d", [128, w], F32),
            "a": cf,
            "r": c.sbuf(name + "_r", [128, w], F32)}


def emit_dequant4(c, name, w_in, wdq_out, O, I, rbufs=None, nld=2, sp=None):
    if sp is None:
        sp = name
    nb = I // 32
    ld = [c.sbuf(f"{name}_ld{i}", [128, I], F32) for i in range(nld)]
    if nld == 1:
        ld = ld * 2
    pm = c.sbuf(f"{name}_pm", [128, nb], F32)
    sb = c.sbuf(f"{name}_sb", [128, nb], F32)
    iv = c.sbuf(f"{name}_iv", [128, nb], F32)
    yb = c.sbuf(f"{name}_yb", [128, I], F32)
    ob = [c.sbuf(f"{name}_ob{i}", [128, I], BF16) for i in range(2)]
    if rbufs is None:
        rbufs = mk_round_bufs(c, name + "_rb", I)
    s_l = [c.sem(f"{sp}_l{i}") for i in range(2)]
    s_ch = c.sem(f"{sp}_ch")
    s_a = c.sem(f"{sp}_a")
    s_st = [c.sem(f"{sp}_st{i}") for i in range(2)]
    for oc in range(O // 128):
        sl = (oc % 2) if nld == 2 else 0
        if oc >= (2 if nld == 2 else 1):
            c.wait("sync", s_ch, s_ch.n)
        c.op("sync", lambda e, oc=oc, sl=sl: e.dma_start(
            out=ld[sl][:, :], in_=w_in[oc * 128:(oc + 1) * 128, :]),
            inc=(s_l[sl], 16))
        c.wait("vector", s_l[sl], s_l[sl].n)
        c.chainop("vector", s_ch, lambda e, sl=sl: e.tensor_reduce(
            pm[:, :], ld[sl].ap().rearrange("p (b j) -> p b j", b=nb),
            axis=AX, op=ALU.max, apply_absolute_value=True))
        c.chainop("vector", s_ch, lambda e: e.tensor_scalar(
            sb[:, :], pm[:, :], 1.0 / 7.0, 1e-12, ALU.mult, ALU.max))
        c.chainop("vector", s_ch, lambda e: e.reciprocal(iv[:, :], sb[:, :]))
        c.chainop("vector", s_ch, lambda e: e.tensor_scalar_mul(
            iv[:, :], iv[:, :], -1.0))
        c.chainop("vector", s_ch, lambda e, sl=sl: e.tensor_tensor(
            yb.ap().rearrange("p (b j) -> p b j", b=nb), ld[sl].ap().rearrange(
                "p (b j) -> p b j", b=nb),
            iv.ap().unsqueeze(2).broadcast_to([128, nb, 32]), ALU.mult))
        c.wait("scalar", s_ch)
        c.op("scalar", lambda e: e.activation(
            rbufs["i"][:, :], yb[:, :], AF.Identity), inc=(s_a, 1))
        c.wait("scalar", s_a)
        c.op("scalar", lambda e: e.activation(
            rbufs["cf"][:, :], rbufs["i"][:, :], AF.Copy), inc=(s_a, 1))
        c.wait("vector", s_a)
        c.chainop("vector", s_ch, lambda e: e.tensor_sub(
            rbufs["d"][:, :], yb[:, :], rbufs["cf"][:, :]))
        c.chainop("vector", s_ch, lambda e: e.scalar_tensor_tensor(
            rbufs["a"][:, :], rbufs["d"][:, :], 0.5, rbufs["cf"][:, :],
            ALU.is_ge, ALU.add))
        c.chainop("vector", s_ch, lambda e: e.scalar_tensor_tensor(
            rbufs["r"][:, :], rbufs["d"][:, :], -0.5, rbufs["a"][:, :],
            ALU.is_le, ALU.subtract))
        c.chainop("vector", s_ch, lambda e: e.tensor_scalar(
            rbufs["r"][:, :], rbufs["r"][:, :], 7.0, -8.0, ALU.min, ALU.max))
        c.wait("vector", s_st[sl], s_st[sl].n)
        c.chainop("vector", s_ch, lambda e, sl=sl: e.tensor_tensor(
            ob[sl].ap().rearrange("p (b j) -> p b j", b=nb),
            rbufs["r"].ap().rearrange("p (b j) -> p b j", b=nb),
            sb.ap().unsqueeze(2).broadcast_to([128, nb, 32]), ALU.mult))
        c.wait("sync", s_ch)
        c.op("sync", lambda e, oc=oc, sl=sl: e.dma_start(
            out=wdq_out[oc * 128:(oc + 1) * 128, :], in_=ob[sl][:, :]),
            inc=(s_st[sl], 16))
    return s_st, s_ch, s_a


def build_launch_G():
    nc = bass.Bass("TRN2", target_bir_lowering=False)
    h3_in = nc.dram_tensor("h3", [TSL, D], BF16, kind="ExternalInput").ap()
    am1_in = nc.dram_tensor("amax1", [8, 1], F32, kind="ExternalInput").ap()
    wug_in = nc.dram_tensor("wug", [2 * DI, D], F32, kind="ExternalInput").ap()
    up_out = nc.dram_tensor("up_raw", [TSL, DI], BF16, kind="ExternalOutput").ap()
    gt_out = nc.dram_tensor("gate_raw", [TSL, DI], BF16, kind="ExternalOutput").ap()
    am23_out = nc.dram_tensor("amax23", [2, 1], F32, kind="ExternalOutput").ap()
    wdq_d = nc.dram_tensor("wdq", [2 * DI, D], BF16, kind="Internal").ap()
    wT_d = nc.dram_tensor("wT", [2 * D, DI], BF16, kind="Internal").ap()

    c = Ctx(nc)
    identf = make_identity(c, "identf", F32)
    ident = make_identity(c, "ident", BF16)
    bc1, s_bc1 = emit_bcast_scales(c, "sc1", am1_in, 8, 32767.0, identf)
    h3_sb = c.sbuf("h3_sb", [128, NT * 1024], BF16)
    xq_tm = c.sbuf("xq_tm", [128, NT * 1024], BF16)
    xq_fm = c.sbuf("xq_fm", [128, NC * TSL], BF16)
    rb = mk_round_bufs(c, "rb", 1024)
    wub = [c.sbuf(f"wub{i}", [128, NC * 512], BF16) for i in range(4)]
    stg = [c.sbuf(f"stg{i}", [128, 512], BF16) for i in range(4)]
    parts = c.sbuf("parts", [128, 2 * 96], F32)
    pfin = c.sbuf("pfin", [128, 2], F32)
    amr = c.sbuf("amr", [2, 1], F32)
    p_am = c.psum("p_am", [2, 128], F32)

    s_h3 = c.sem("s_h3")
    s_ch = c.sem("s_ch")
    s_a = c.sem("s_a")
    s_xq = c.sem("s_xq")
    s_fm = c.sem("s_fm")
    s_wl = [c.sem(f"s_wl{i}") for i in range(4)]
    s_mm = c.sem("s_mm")
    s_evq = c.sem("s_evq")
    s_pt = c.sem("s_pt")
    s_st = [c.sem(f"s_stq{i}") for i in range(4)]
    s_out = c.sem("s_out")

    c.op("sync", lambda e: e.dma_start(
        out=h3_sb.ap().rearrange("p (t d) -> p t d", t=NT),
        in_=h3_in.rearrange("(t p) d -> p t d", p=128)), inc=(s_h3, 16))

    c.wait("scalar", s_bc1)
    c.wait("vector", s_bc1)
    c.wait("scalar", s_h3)
    for t in range(NT):
        src = h3_sb[:, t * 1024:(t + 1) * 1024]
        emit_round(c, s_a, src, 1024, rb, bc1[:, 2:3], s_ch)
        c.wait("scalar", s_ch)
        c.op("scalar", lambda e, t=t: e.activation(
            xq_tm[:, t * 1024:(t + 1) * 1024], rb["r"][:, 0:1024],
            AF.Identity, scale=bc1[:, 0:1]), inc=(s_xq, 1))
        c.wait("vector", s_xq)

    p_tr_cm = nc.psum_tensor("p_tr", [128, 2048], BF16)
    p_tr = p_tr_cm.__enter__()
    emit_transpose_tiles(c, NT, xq_tm, xq_fm, p_tr, ident, s_xq, s_fm,
                         name="xqt")

    s_dq, s_dqch, s_dqa = emit_dequant4(c, "du", wug_in, wdq_d, 2 * DI, D)
    c.wait("sync", s_dq[0])
    c.wait("sync", s_dq[1])
    p_w_cm = nc.psum_tensor("p_w", [128, 2048], BF16)
    p_w = p_w_cm.__enter__()
    s_wt, _, s_wt_ev = emit_weight_transpose(
        c, "wt", wdq_d, wT_d, 2 * DI, D, ident, p_w, o_block=DI, in_f32=False)
    p_w_cm.__exit__(None, None, None)
    p_tr_cm.__exit__(None, None, None)
    psQ = [c.psum(f"psQ{i}", [128, 512], F32) for i in range(4)]

    c.wait("tensor", s_fm)
    c.wait("tensor", s_wt_ev)
    c.wait("sync", s_wt[0])
    c.wait("sync", s_wt[1])
    for ic in range(6):
        for w2 in range(2):
            sl = 2 * (ic % 2) + w2
            if ic >= 2:
                c.wait("sync", s_mm, (ic - 1) * 32)
            c.op("sync", lambda e, ic=ic, w2=w2, sl=sl: e.dma_start(
                out=wub[sl].ap().rearrange("p (ch j) -> p ch j", ch=NC),
                in_=wT_d[w2 * D:(w2 + 1) * D, ic * 512:(ic + 1) * 512]
                    .rearrange("(ch p) j -> p ch j", p=128)),
                inc=(s_wl[sl], 16))
        for t in range(NT):
            for w2 in range(2):
                gi = ic * 32 + t * 2 + w2
                sl = 2 * (ic % 2) + w2
                ps = psQ[2 * (t % 2) + w2]
                c.wait("tensor", s_wl[sl], 16 * (ic // 2 + 1))
                if gi >= 4:
                    c.wait("tensor", s_evq, gi - 3)
                    c.wait("tensor", s_pt, gi - 3)
                for ch in range(NC):
                    c.op("tensor", lambda e, ps=ps, ch=ch, t=t, sl=sl:
                         e.matmul(ps[:, :],
                                  xq_fm[:, ch * TSL + t * 128:ch * TSL + (t + 1) * 128],
                                  wub[sl][:, ch * 512:(ch + 1) * 512],
                                  start=(ch == 0), stop=(ch == NC - 1)),
                         inc=(s_mm, 1) if ch == NC - 1 else None)
                c.wait("scalar", s_mm)
                c.wait("scalar", s_st[2 * (t % 2) + w2],
                       s_st[2 * (t % 2) + w2].n)
                c.wait("scalar", s_pt, max(0, gi - 3))
                c.op("scalar", lambda e, ps=ps, sl2=2 * (t % 2) + w2:
                     e.activation(stg[sl2][:, :], ps[:, :], AF.Identity),
                     inc=(s_evq, 1))
                c.wait("vector", s_evq)
                c.op("vector", lambda e, sl2=2 * (t % 2) + w2,
                     col=w2 * 96 + ic * 16 + t:
                     e.tensor_reduce(parts[:, col:col + 1], stg[sl2][:, :],
                                     axis=AX, op=ALU.max,
                                     apply_absolute_value=True),
                     inc=(s_pt, 1))
                c.wait("sync", s_evq)
                dst = (up_out if w2 == 0 else gt_out)[
                    t * 128:(t + 1) * 128, ic * 512:(ic + 1) * 512]
                c.op("sync", lambda e, dst=dst, sl2=2 * (t % 2) + w2:
                     e.dma_start(out=dst, in_=stg[sl2][:, :]),
                     inc=(s_st[2 * (t % 2) + w2], 16))

    c.wait("vector", s_pt)
    c.chainop("vector", s_ch, lambda e: e.tensor_reduce(
        pfin[:, :], parts.ap().rearrange("p (w k) -> p w k", w=2), axis=AX,
        op=ALU.max, apply_absolute_value=True))
    c.wait("tensor", s_ch)
    c.op("tensor", lambda e: e.transpose(
        p_am[:, :], pfin[:, :], identf[0][:, :]), inc=(s_ch, 1))
    c.wait("vector", s_ch)
    c.chainop("vector", s_ch, lambda e: e.tensor_reduce(
        amr[:, :], p_am[:, :], axis=AX, op=ALU.max, apply_absolute_value=True))
    c.wait("sync", s_ch)
    c.op("sync", lambda e: e.dma_start(out=am23_out, in_=amr[:, :]),
         inc=(s_out, 16))
    c.wait("sync", s_st[0])
    c.wait("sync", s_st[1])
    c.wait("sync", s_st[2])
    c.wait("sync", s_st[3])
    c.wait("sync", s_out)

    c.build_block()
    c.close()
    return nc


def build_launch_H():
    nc = bass.Bass("TRN2", target_bir_lowering=False)
    up_in = nc.dram_tensor("up_raw", [TSL, DI], BF16, kind="ExternalInput").ap()
    gt_in = nc.dram_tensor("gate_raw", [TSL, DI], BF16, kind="ExternalInput").ap()
    am_in = nc.dram_tensor("amax23", [8, 2], F32, kind="ExternalInput").ap()
    upq_out = nc.dram_tensor("up_q", [TSL, DI], BF16, kind="ExternalOutput").ap()
    g2_out = nc.dram_tensor("gate2raw", [TSL, DI], BF16, kind="ExternalOutput").ap()
    am4_out = nc.dram_tensor("amax4", [1, 1], F32, kind="ExternalOutput").ap()

    c = Ctx(nc)
    identf = make_identity(c, "identf", F32)
    bc2, s_bc2 = emit_bcast_scales(c, "sc2", am_in[:, 0:1], 8, 32767.0, identf)
    bc3, s_bc3 = emit_bcast_scales(c, "sc3", am_in[:, 1:2], 8, 32767.0, identf)
    ldu = [c.sbuf(f"ldu{i}", [128, DI], BF16) for i in range(2)]
    ldg = [c.sbuf(f"ldg{i}", [128, DI], BF16) for i in range(2)]
    rb = mk_round_bufs(c, "rb", DI)
    sigb = c.sbuf("sigb", [128, DI], F32)
    g1b = c.sbuf("g1b", [128, DI], BF16)
    outb = [c.sbuf(f"outb{i}", [128, DI], BF16) for i in range(4)]
    parts = c.sbuf("parts", [128, NT], F32)
    pfin = c.sbuf("pfin", [128, 1], F32)
    amr = c.sbuf("amr", [1, 1], F32)
    p_am = c.psum("p_am", [1, 128], F32)

    s_lu = [c.sem(f"s_lu{i}") for i in range(2)]
    s_lg = [c.sem(f"s_lg{i}") for i in range(2)]
    s_ch = c.sem("s_ch")
    s_a = c.sem("s_a")
    s_st = [c.sem(f"s_sth{i}") for i in range(4)]
    s_out = c.sem("s_out")

    c.wait("scalar", s_bc2)
    c.wait("scalar", s_bc3)
    c.wait("vector", s_bc2)
    c.wait("vector", s_bc3)
    for t in range(NT):
        slu, slg = t % 2, t % 2
        if t >= 2:
            c.wait("sync", s_ch, s_ch.n)
        c.op("sync", lambda e, t=t, slu=slu: e.dma_start(
            out=ldu[slu][:, :], in_=up_in[t * 128:(t + 1) * 128, :]),
            inc=(s_lu[slu], 16))
        c.op("sync", lambda e, t=t, slg=slg: e.dma_start(
            out=ldg[slg][:, :], in_=gt_in[t * 128:(t + 1) * 128, :]),
            inc=(s_lg[slg], 16))
        c.wait("scalar", s_lg[slg], 16 * (t // 2 + 1))
        c.wait("vector", s_lg[slg], 16 * (t // 2 + 1))
        emit_round(c, s_a, ldg[slg][:, :], DI, rb, bc3[:, 2:3], s_ch)
        c.wait("scalar", s_ch)
        c.op("scalar", lambda e: e.activation(
            sigb[:, :], rb["r"][:, :], AF.Sigmoid, scale=bc3[:, 0:1]),
            inc=(s_a, 1))
        c.wait("scalar", s_a)
        c.op("scalar", lambda e: e.activation(
            g1b[:, :], rb["r"][:, :], AF.Identity, scale=bc3[:, 0:1]),
            inc=(s_a, 1))
        c.wait("scalar", s_a)
        c.op("scalar", lambda e: e.activation(
            rb["i"][:, :], sigb[:, :], AF.Identity, scale=-65536.0),
            inc=(s_a, 1))
        c.wait("scalar", s_a)
        c.op("scalar", lambda e: e.activation(
            rb["cf"][:, :], rb["i"][:, :], AF.Copy), inc=(s_a, 1))
        c.wait("vector", s_a)
        c.chainop("vector", s_ch, lambda e: e.scalar_tensor_tensor(
            rb["d"][:, :], sigb[:, :], -65536.0, rb["cf"][:, :],
            ALU.mult, ALU.subtract))
        c.chainop("vector", s_ch, lambda e: e.scalar_tensor_tensor(
            rb["a"][:, :], rb["d"][:, :], 0.5, rb["cf"][:, :],
            ALU.is_ge, ALU.add))
        c.chainop("vector", s_ch, lambda e: e.scalar_tensor_tensor(
            rb["r"][:, :], rb["d"][:, :], -0.5, rb["a"][:, :],
            ALU.is_le, ALU.subtract))
        c.chainop("vector", s_ch, lambda e: e.tensor_scalar_min(
            rb["r"][:, :], rb["r"][:, :], 65535.0))
        ob = outb[2 * (t % 2)]
        c.wait("vector", s_st[2 * (t % 2)], s_st[2 * (t % 2)].n)
        c.chainop("vector", s_ch, lambda e, ob=ob: e.scalar_tensor_tensor(
            ob[:, :], rb["r"][:, :], 1.0 / 65536.0, g1b[:, :],
            ALU.mult, ALU.mult))
        c.chainop("vector", s_ch, lambda e, ob=ob, t=t: e.tensor_reduce(
            parts[:, t:t + 1], ob[:, :], axis=AX, op=ALU.max,
            apply_absolute_value=True))
        c.wait("sync", s_ch)
        c.op("sync", lambda e, ob=ob, t=t: e.dma_start(
            out=g2_out[t * 128:(t + 1) * 128, :], in_=ob[:, :]),
            inc=(s_st[2 * (t % 2)], 16))
        c.wait("scalar", s_lu[slu], 16 * (t // 2 + 1))
        emit_round(c, s_a, ldu[slu][:, :], DI, rb, bc2[:, 2:3], s_ch)
        ob2 = outb[2 * (t % 2) + 1]
        c.wait("scalar", s_ch)
        c.wait("scalar", s_st[2 * (t % 2) + 1], s_st[2 * (t % 2) + 1].n)
        c.op("scalar", lambda e, ob2=ob2: e.activation(
            ob2[:, :], rb["r"][:, :], AF.Identity, scale=bc2[:, 0:1]),
            inc=(s_a, 1))
        c.wait("sync", s_a)
        c.op("sync", lambda e, ob2=ob2, t=t: e.dma_start(
            out=upq_out[t * 128:(t + 1) * 128, :], in_=ob2[:, :]),
            inc=(s_st[2 * (t % 2) + 1], 16))
        c.wait("vector", s_a)

    c.chainop("vector", s_ch, lambda e: e.tensor_reduce(
        pfin[:, :], parts[:, :], axis=AX, op=ALU.max,
        apply_absolute_value=True))
    c.wait("tensor", s_ch)
    c.wait("tensor", identf[1])
    c.op("tensor", lambda e: e.transpose(
        p_am[:, :], pfin[:, :], identf[0][:, :]), inc=(s_ch, 1))
    c.wait("vector", s_ch)
    c.chainop("vector", s_ch, lambda e: e.tensor_reduce(
        amr[:, :], p_am[:, :], axis=AX, op=ALU.max, apply_absolute_value=True))
    c.wait("sync", s_ch)
    c.op("sync", lambda e: e.dma_start(out=am4_out, in_=amr[:, :]),
         inc=(s_out, 16))
    for ss in s_st:
        c.wait("sync", ss)
    c.wait("sync", s_out)

    c.build_block()
    c.close()
    return nc


def build_launch_I():
    nc = bass.Bass("TRN2", target_bir_lowering=False)
    g2_in = nc.dram_tensor("gate2raw", [TSL, DI], BF16, kind="ExternalInput").ap()
    up_in = nc.dram_tensor("up_q", [TSL, DI], BF16, kind="ExternalInput").ap()
    am_in = nc.dram_tensor("amax4", [8, 1], F32, kind="ExternalInput").ap()
    o_out = nc.dram_tensor("o_raw", [TSL, DI], BF16, kind="ExternalOutput").ap()
    am5_out = nc.dram_tensor("amax5", [1, 1], F32, kind="ExternalOutput").ap()

    c = Ctx(nc)
    identf = make_identity(c, "identf", F32)
    bc4, s_bc4 = emit_bcast_scales(c, "sc4", am_in, 8, 32767.0, identf)
    ldg = [c.sbuf(f"ldg{i}", [128, DI], BF16) for i in range(2)]
    ldu = [c.sbuf(f"ldu{i}", [128, DI], BF16) for i in range(2)]
    rb = mk_round_bufs(c, "rb", DI)
    outb = [c.sbuf(f"outb{i}", [128, DI], BF16) for i in range(2)]
    parts = c.sbuf("parts", [128, NT], F32)
    pfin = c.sbuf("pfin", [128, 1], F32)
    amr = c.sbuf("amr", [1, 1], F32)
    p_am = c.psum("p_am", [1, 128], F32)

    s_lg = [c.sem(f"s_lg{i}") for i in range(2)]
    s_lu = [c.sem(f"s_lu{i}") for i in range(2)]
    s_ch = c.sem("s_ch")
    s_a = c.sem("s_a")
    s_st = [c.sem(f"s_sti{i}") for i in range(2)]
    s_out = c.sem("s_out")

    c.wait("scalar", s_bc4)
    c.wait("vector", s_bc4)
    for t in range(NT):
        sl = t % 2
        if t >= 2:
            c.wait("sync", s_ch, s_ch.n)
        c.op("sync", lambda e, t=t, sl=sl: e.dma_start(
            out=ldg[sl][:, :], in_=g2_in[t * 128:(t + 1) * 128, :]),
            inc=(s_lg[sl], 16))
        c.op("sync", lambda e, t=t, sl=sl: e.dma_start(
            out=ldu[sl][:, :], in_=up_in[t * 128:(t + 1) * 128, :]),
            inc=(s_lu[sl], 16))
        c.wait("scalar", s_lg[sl], 16 * (t // 2 + 1))
        c.wait("vector", s_lg[sl], 16 * (t // 2 + 1))
        emit_round(c, s_a, ldg[sl][:, :], DI, rb, bc4[:, 2:3], s_ch)
        c.wait("vector", s_lu[sl], 16 * (t // 2 + 1))
        ob = outb[t % 2]
        c.wait("vector", s_st[t % 2], s_st[t % 2].n)
        c.chainop("vector", s_ch, lambda e, ob=ob, sl=sl:
                  e.scalar_tensor_tensor(ob[:, :], rb["r"][:, :], bc4[:, 0:1],
                                         ldu[sl][:, :], ALU.mult, ALU.mult))
        c.chainop("vector", s_ch, lambda e, ob=ob, t=t: e.tensor_reduce(
            parts[:, t:t + 1], ob[:, :], axis=AX, op=ALU.max,
            apply_absolute_value=True))
        c.wait("sync", s_ch)
        c.op("sync", lambda e, ob=ob, t=t: e.dma_start(
            out=o_out[t * 128:(t + 1) * 128, :], in_=ob[:, :]),
            inc=(s_st[t % 2], 16))

    c.chainop("vector", s_ch, lambda e: e.tensor_reduce(
        pfin[:, :], parts[:, :], axis=AX, op=ALU.max,
        apply_absolute_value=True))
    c.wait("tensor", s_ch)
    c.wait("tensor", identf[1])
    c.op("tensor", lambda e: e.transpose(
        p_am[:, :], pfin[:, :], identf[0][:, :]), inc=(s_ch, 1))
    c.wait("vector", s_ch)
    c.chainop("vector", s_ch, lambda e: e.tensor_reduce(
        amr[:, :], p_am[:, :], axis=AX, op=ALU.max, apply_absolute_value=True))
    c.wait("sync", s_ch)
    c.op("sync", lambda e: e.dma_start(out=am5_out, in_=amr[:, :]),
         inc=(s_out, 16))
    c.wait("sync", s_st[0])
    c.wait("sync", s_st[1])
    c.wait("sync", s_out)

    c.build_block()
    c.close()
    return nc


def build_launch_J():
    nc = bass.Bass("TRN2", target_bir_lowering=False)
    o_in = nc.dram_tensor("o_raw", [TSL, DI], BF16, kind="ExternalInput").ap()
    am_in = nc.dram_tensor("amax5", [8, 1], F32, kind="ExternalInput").ap()
    x_in = nc.dram_tensor("x3", [TSL, D], F32, kind="ExternalInput").ap()
    wd_in = nc.dram_tensor("w_down", [D, DI], F32, kind="ExternalInput").ap()
    x4_out = nc.dram_tensor("x4", [TSL, D], F32, kind="ExternalOutput").ap()
    wdq_d = nc.dram_tensor("wdq", [D, DI], BF16, kind="Internal").ap()
    wT_d = nc.dram_tensor("wT", [DI, D], BF16, kind="Internal").ap()

    c = Ctx(nc)
    identf = make_identity(c, "identf", F32)
    ident = make_identity(c, "ident", BF16)
    bc5, s_bc5 = emit_bcast_scales(c, "sc5", am_in, 8, 32767.0, identf)
    ldo = [c.sbuf(f"ldo{i}", [128, DI], BF16) for i in range(2)]
    rb = mk_round_bufs(c, "rb", DI)
    o_b = c.sbuf("o_b", [128, DI], BF16)
    o_fm = [c.sbuf(f"o_fm{i}", [128, DI // 128 * 128], BF16) for i in range(2)]
    wdT = c.sbuf("wdT", [128, (DI // 128) * 1024], BF16)
    x_t = [c.sbuf(f"x_t{i}", [128, 1024], F32) for i in range(2)]

    s_lo = [c.sem(f"s_lo{i}") for i in range(2)]
    s_ch = c.sem("s_ch")
    s_a = c.sem("s_a")
    s_ob = c.sem("s_ob")
    s_tp = c.sem("s_tp")
    s_ofm = c.sem("s_ofm")
    s_mm = c.sem("s_mm")
    s_wl = c.sem("s_wl")
    s_xl = [c.sem(f"s_xl{i}") for i in range(2)]
    s_xo = [c.sem(f"s_xo{i}") for i in range(2)]
    s_fin = c.sem("s_fin")

    s_dq, s_dqch, s_dqa = emit_dequant4(c, "dd", wd_in, wdq_d, D, DI,
                                        rbufs=rb, nld=1)
    c.wait("sync", s_dq[0])
    c.wait("sync", s_dq[1])
    c.wait("scalar", s_dqch)
    c.wait("vector", s_dqch)
    c.wait("scalar", s_dqa)
    c.wait("vector", s_dqa)
    p_w_cm = nc.psum_tensor("p_w", [128, 2048], BF16)
    p_w = p_w_cm.__enter__()
    s_wt, _, s_wt_ev = emit_weight_transpose(
        c, "wt", wdq_d, wT_d, D, DI, ident, p_w, in_f32=False)
    c.wait("sync", s_wt[0])
    c.wait("sync", s_wt[1])
    c.op("sync", lambda e: e.dma_start(
        out=wdT.ap().rearrange("p (ic j) -> p ic j", ic=DI // 128),
        in_=wT_d.rearrange("(ic p) j -> p ic j", p=128)), inc=(s_wl, 16))
    p_w_cm.__exit__(None, None, None)
    p_tr_cm = nc.psum_tensor("p_tr", [128, 2048], BF16)
    p_tr = p_tr_cm.__enter__()
    p_dn_cm = [nc.psum_tensor(f"p_dn{i}", [128, 1024], F32) for i in range(2)]
    p_dn = [p.__enter__() for p in p_dn_cm]

    c.wait("scalar", s_bc5)
    c.wait("vector", s_bc5)
    c.wait("tensor", s_wt_ev)
    c.wait("tensor", ident[1])
    for t in range(NT):
        sl = t % 2
        if t >= 2:
            c.wait("sync", s_ch, s_ch.n)
        c.op("sync", lambda e, t=t, sl=sl: e.dma_start(
            out=ldo[sl][:, :], in_=o_in[t * 128:(t + 1) * 128, :]),
            inc=(s_lo[sl], 16))
        c.wait("scalar", s_lo[sl], 16 * (t // 2 + 1))
        c.wait("vector", s_lo[sl], 16 * (t // 2 + 1))
        emit_round(c, s_a, ldo[sl][:, :], DI, rb, bc5[:, 2:3], s_ch)
        c.wait("scalar", s_ch)
        if t >= 1:
            c.wait("scalar", s_tp, t)
        c.op("scalar", lambda e: e.activation(
            o_b[:, :], rb["r"][:, :], AF.Identity, scale=bc5[:, 0:1]),
            inc=(s_ob, 1))
        c.wait("vector", s_ob)
        c.wait("tensor", s_ob)
        for g in range(3):
            gg = t * 3 + g
            half = (gg % 2) * 1024
            if gg >= 2:
                c.wait("tensor", s_ofm, gg - 1)
            for icc in range(8):
                c.op("tensor", lambda e, g=g, icc=icc, half=half: e.transpose(
                    p_tr[:, half + icc * 128:half + (icc + 1) * 128],
                    o_b[:, (g * 8 + icc) * 128:(g * 8 + icc + 1) * 128],
                    ident[0][:, :]), inc=(s_tp, 1) if icc == 7 else None)
            c.wait("vector", s_tp)
            if t >= 2:
                c.wait("vector", s_mm, t - 1)
            c.op("vector", lambda e, sl=sl, g=g, half=half: e.tensor_copy(
                o_fm[sl][:, g * 1024:(g + 1) * 1024], p_tr[:, half:half + 1024]),
                inc=(s_ofm, 1))
        c.wait("tensor", s_ofm)
        c.wait("tensor", s_wl)
        pd = p_dn[t % 2]
        if t >= 2:
            c.wait("tensor", s_fin, 2 * (t - 1))
        for icc in range(24):
            for hh in range(2):
                last = (icc == 23 and hh == 1)
                c.op("tensor", lambda e, pd=pd, icc=icc, hh=hh, sl=sl:
                     e.matmul(pd[:, hh * 512:(hh + 1) * 512],
                              o_fm[sl][:, icc * 128:(icc + 1) * 128],
                              wdT[:, icc * 1024 + hh * 512:icc * 1024 + (hh + 1) * 512],
                              start=(icc == 0), stop=(icc == 23)),
                     inc=(s_mm, 1) if last else None)
        xt = x_t[t % 2]
        if t >= 2:
            c.wait("sync", s_xo[t % 2], 16 * (t // 2))
        c.op("sync", lambda e, t=t, xt=xt: e.dma_start(
            out=xt[:, :], in_=x_in[t * 128:(t + 1) * 128, :]),
            inc=(s_xl[t % 2], 16))
        c.wait("vector", s_xl[t % 2], 16 * (t // 2 + 1))
        c.wait("vector", s_mm)
        c.op("vector", lambda e, xt=xt, pd=pd: e.scalar_tensor_tensor(
            xt[:, :], pd[:, :], 1.0, xt[:, :], ALU.mult, ALU.add),
            inc=(s_fin, 2))
        c.wait("sync", s_fin)
        c.op("sync", lambda e, t=t, xt=xt: e.dma_start(
            out=x4_out[t * 128:(t + 1) * 128, :], in_=xt[:, :]),
            inc=(s_xo[t % 2], 16))

    c.wait("sync", s_xo[0])
    c.wait("sync", s_xo[1])

    p_dn_cm[1].__exit__(None, None, None)
    p_dn_cm[0].__exit__(None, None, None)
    p_tr_cm.__exit__(None, None, None)
    c.build_block()
    c.close()
    return nc


def _split_tokens(x):
    return [np.ascontiguousarray(x[i * TSL:(i + 1) * TSL]) for i in range(NCORES)]


def _phase_ln(c, pfx, x_dram, sems_unused, ident, eps_ap, h_tm, h_fm, p_tr,
              scope):
    x_sb = scope.__self__ if False else c.sbuf(pfx + "_x_sb", [128, NT * 1024], F32)
    scr = c.sbuf(pfx + "_scr", [128, 2048], BF16)
    stats = c.sbuf(pfx + "_stats", [128, 8 * NT], F32)
    s_x = c.sem("ln_x")
    s_h = c.sem("ln_h")
    s_fm = c.sem("ln_fm")
    c.op("sync", lambda e: e.dma_start(
        out=x_sb.ap().rearrange("p (t d) -> p t d", t=NT),
        in_=x_dram.rearrange("(t p) d -> p t d", p=128)), inc=(s_x, 16))
    emit_ln(c, "lnP", x_sb, NT, h_tm, scr, stats, ident, s_h, (s_x, s_x.n),
            eps_ap, h_fm=h_fm, p_tr=p_tr, s_fm=s_fm)
    return s_h, s_fm, x_sb


def _phase_attn0(c, x_in, hfm_g, htm_g, W_in, x1_out, E_dram, wsum_d, hq_fm,
                 ident, consts, s_hq, tile_cb=None, wsum_pack=None):
    NKB = T // 1024
    x_t = [c.sbuf(f"b_x_t{i}", [128, 1024], F32) for i in range(2)]
    kfm = [c.sbuf(f"b_kfm{i}", [128, NC * 1024], BF16) for i in range(2)]
    ktm = [c.sbuf(f"b_ktm{i}", [128, 8 * 1024], BF16) for i in range(2)]
    Escr = [c.sbuf(f"b_Escr{i}", [128, 1024], BF16) for i in range(2)]
    Embuf = [c.sbuf(f"b_Embuf{i}", [128, 1024], BF16) for i in range(4)]
    Emm = [c.sbuf(f"b_Emm{i}", [128, 1024], BF16) for i in range(2)]
    EmT = [c.sbuf(f"b_EmT{i}", [128, 1024], BF16) for i in range(2)]
    acc = c.sbuf("b_acc", [128, 8 * 1024], F32)
    Zparts = c.sbuf("b_Zparts", [128, NT * NKB], F32)
    SpParts = c.sbuf("b_SpParts", [128, NT * NKB], F32)
    zstat = c.sbuf("b_zstat", [128, 6 * NT], F32)
    psA = c.psum("b_psA", [128, 1024], F32)
    psB = c.psum("b_psB", [128, 1024], F32)
    p_tr = c.psum("b_p_tr", [128, 2048], BF16)

    Z = zstat[:, 0:NT]
    theta = zstat[:, NT:2 * NT]
    invZ = zstat[:, 2 * NT:3 * NT]
    sp = zstat[:, 3 * NT:4 * NT]
    dinv = zstat[:, 4 * NT:5 * NT]
    coef2 = zstat[:, 5 * NT:6 * NT]

    s_xl = [c.sem(f"a_xl{i}") for i in range(2)]
    s_xo = [c.sem(f"a_xo{i}") for i in range(2)]
    s_kf = [c.sem(f"a_kf{i}") for i in range(2)]
    s_kt = [c.sem(f"a_kt{i}") for i in range(2)]
    s_est = [c.sem(f"a_est{i}") for i in range(2)]
    s_eld = [c.sem(f"a_eld{i}") for i in range(4)]
    s_sc = c.sem("a_sc")
    s_exp = c.sem("a_exp")
    s_msk = c.sem("a_msk")
    s_tp = c.sem("a_tp")
    s_ev = c.sem("a_ev")
    s_av = c.sem("a_av")
    s_ac = c.sem("a_ac")
    s_fin = c.sem("a_fin")
    B = {id(x): x.n for x in [s_sc, s_exp, s_msk, s_tp, s_ev, s_av, s_ac,
                              s_fin] + s_xl + s_xo + s_kf + s_kt + s_est + s_eld}

    if wsum_pack is None:
        wsum_rep, s_wr, wsum_mm_cb = emit_wsum_rep(
            c, W_in, wsum_d, p_bf="defer", ld_sem_names=("a_eld0", "a_eld1"))
    else:
        wsum_rep, s_wr, wsum_mm_cb = wsum_pack

    c.wait("scalar", consts.sem)
    c.wait("tensor", s_hq, s_hq.n)
    for kb in range(NKB):
        if kb >= 2:
            c.wait("sync", s_sc, B[id(s_sc)] + 16 * (kb - 1))
        c.op("sync", lambda e, kb=kb: e.dma_start(
            out=kfm[kb % 2].ap().rearrange("p (ch s) -> p ch s", ch=NC),
            in_=hfm_g[(kb // 2) * 1024:(kb // 2 + 1) * 1024,
                      (kb % 2) * 1024:(kb % 2 + 1) * 1024]
                .rearrange("(ch p) s -> p ch s", p=128)), inc=(s_kf[kb % 2], 16))
        for qt in range(NT):
            i = kb * NT + qt
            ps = psA if i % 2 == 0 else psB
            c.wait("tensor", s_kf[kb % 2], s_kf[kb % 2].n)
            if i >= 2:
                c.wait("tensor", s_exp, B[id(s_exp)] + i - 1)
            for ch in range(NC):
                for hh in range(2):
                    last = (ch == NC - 1 and hh == 1)
                    c.op("tensor", lambda e, ps=ps, ch=ch, hh=hh, qt=qt, kb=kb:
                         e.matmul(ps[:, hh * 512:(hh + 1) * 512],
                                  hq_fm[:, ch * TSL + qt * 128:ch * TSL + (qt + 1) * 128],
                                  kfm[kb % 2][:, ch * 1024 + hh * 512:ch * 1024 + (hh + 1) * 512],
                                  start=(ch == 0), stop=(ch == NC - 1)),
                         inc=(s_sc, 1) if last else None)
            c.wait("scalar", s_sc)
            if i >= 2:
                c.wait("scalar", s_est[i % 2], B[id(s_est[i % 2])] + 16 * (i // 2))
            c.op("scalar", lambda e, ps=ps, i=i: e.activation(
                Escr[i % 2][:, :], ps[:, :], AF.Exp, bias=consts[-32.0],
                scale=1.0 / 32.0, accum_out=Zparts[:, i:i + 1]),
                inc=(s_exp, 1))
            c.wait("gpsimd", s_exp)
            c.op("gpsimd", lambda e, i=i, qt=qt, kb=kb: e.dma_start(
                out=E_dram[qt * 128:(qt + 1) * 128, kb * 1024:(kb + 1) * 1024],
                in_=Escr[i % 2][:, :]), inc=(s_est[i % 2], 16))
            if tile_cb is not None:
                tile_cb(i)

    c.wait("tensor", s_exp)
    wsum_mm_cb(psA)
    c.wait("vector", s_exp)
    c.op("vector", lambda e: e.reduce_sum(
        Z, Zparts.ap().rearrange("p (kb qt) -> p qt kb", qt=NT), axis=AX),
        inc=(s_fin, 1))
    c.chainop("vector", s_fin, lambda e: e.tensor_scalar_mul(theta, Z, 0.01))
    c.chainop("vector", s_fin, lambda e: e.reciprocal(invZ, Z))
    bfin3 = s_fin.n

    c.wait("tensor", s_exp)
    c.wait("tensor", ident[1])
    c.wait("tensor", s_wr, s_wr.n)

    def emit_av(a):
        j1 = a // 8
        ps = psA if a % 2 == 0 else psB
        c.wait("tensor", s_ev, B[id(s_ev)] + a + 1)
        c.wait("tensor", s_kt[j1 % 2], s_kt[j1 % 2].n)
        if a >= 2:
            c.wait("tensor", s_ac, B[id(s_ac)] + a - 1)
        for sc in range(8):
            for hh in range(2):
                last = (sc == 7 and hh == 1)
                c.op("tensor", lambda e, ps=ps, sc=sc, hh=hh, a=a, j1=j1:
                     e.matmul(ps[:, hh * 512:(hh + 1) * 512],
                              EmT[a % 2][:, sc * 128:(sc + 1) * 128],
                              ktm[j1 % 2][:, sc * 1024 + hh * 512:sc * 1024 + (hh + 1) * 512],
                              start=(sc == 0), stop=(sc == 7)),
                     inc=(s_av, 1) if last else None)

    def emit_add(a):
        ps = psA if a % 2 == 0 else psB
        ti_a = a % 8
        kb_a = (a // 8) % NKB
        c.wait("vector", s_av, B[id(s_av)] + a + 1)
        if kb_a == 0:
            c.wait("vector", s_fin, s_fin.n)
        c.op("vector", lambda e, ps=ps, ti_a=ti_a, kb_a=kb_a: (
            e.tensor_copy(acc[:, ti_a * 1024:(ti_a + 1) * 1024], ps[:, :])
            if kb_a == 0 else
            e.tensor_add(acc[:, ti_a * 1024:(ti_a + 1) * 1024],
                         acc[:, ti_a * 1024:(ti_a + 1) * 1024], ps[:, :])),
            inc=(s_ac, 1))

    for g in range(2):
        for kb in range(NKB):
            j = g * NKB + kb
            if j >= 2:
                c.wait("sync", s_av, B[id(s_av)] + (j - 1) * 8)
            c.op("sync", lambda e, kb=kb, j=j: e.dma_start(
                out=ktm[j % 2].ap().rearrange("p (sc f) -> p sc f", sc=8),
                in_=htm_g[kb * 1024:(kb + 1) * 1024, :]
                    .rearrange("(sc p) f -> p sc f", p=128)),
                inc=(s_kt[j % 2], 16))
            for ti in range(8):
                qt = g * 8 + ti
                i = j * 8 + ti
                ist = kb * NT + qt
                c.wait("sync", s_est[ist % 2],
                       B[id(s_est[ist % 2])] + 16 * (ist // 2 + 1))
                if i >= 4:
                    c.wait("sync", s_msk, B[id(s_msk)] + i - 3)
                c.op("sync", lambda e, qt=qt, kb=kb, i=i: e.dma_start(
                    out=Embuf[i % 4][:, :],
                    in_=E_dram[qt * 128:(qt + 1) * 128, kb * 1024:(kb + 1) * 1024]),
                    inc=(s_eld[i % 4], 16))
                c.wait("vector", s_eld[i % 4])
                c.wait("vector", s_fin, bfin3)
                if i >= 2:
                    c.wait("vector", s_tp, B[id(s_tp)] + i - 1)
                c.op("vector", lambda e, i=i, qt=qt, kb=kb: e.scalar_tensor_tensor(
                    Emm[i % 2][:, :], Embuf[i % 4][:, :], theta[:, qt:qt + 1],
                    Embuf[i % 4][:, :], ALU.is_ge, ALU.mult,
                    accum_out=SpParts[:, kb * NT + qt:kb * NT + qt + 1]),
                    inc=(s_msk, 1))
                c.wait("tensor", s_msk)
                if i >= 2:
                    c.wait("tensor", s_ev, B[id(s_ev)] + i - 1)
                for ch in range(8):
                    c.op("tensor", lambda e, i=i, ch=ch: e.transpose(
                        p_tr[:, (i % 2) * 1024 + ch * 128:(i % 2) * 1024 + (ch + 1) * 128],
                        Emm[i % 2][:, ch * 128:(ch + 1) * 128], ident[0][:, :]),
                        inc=(s_tp, 1) if ch == 7 else None)
                c.wait("scalar", s_tp, B[id(s_tp)] + i + 1)
                if i >= 2:
                    c.wait("scalar", s_av, B[id(s_av)] + i - 1)
                c.op("scalar", lambda e, i=i: e.activation(
                    EmT[i % 2][:, :], p_tr[:, (i % 2) * 1024:(i % 2 + 1) * 1024],
                    AF.Identity), inc=(s_ev, 1))
                if i % 128 >= 1:
                    emit_av(i - 1)
                if i % 128 >= 2:
                    emit_add(i - 2)
        emit_av(g * 128 + 127)
        emit_add(g * 128 + 126)
        emit_add(g * 128 + 127)

        c.wait("vector", s_msk)
        c.wait("vector", s_ac)
        c.op("vector", lambda e, g=g: e.reduce_sum(
            sp[:, g * 8:(g + 1) * 8],
            SpParts.ap().rearrange("p (kb qt) -> p qt kb", qt=NT)[:, g * 8:(g + 1) * 8, :],
            axis=AX), inc=(s_fin, 1))
        c.chainop("vector", s_fin, lambda e, g=g: e.scalar_tensor_tensor(
            dinv[:, g * 8:(g + 1) * 8], Z[:, g * 8:(g + 1) * 8], 1e-9,
            sp[:, g * 8:(g + 1) * 8], ALU.mult, ALU.add))
        c.chainop("vector", s_fin, lambda e, g=g: e.reciprocal(
            dinv[:, g * 8:(g + 1) * 8], dinv[:, g * 8:(g + 1) * 8]))
        c.chainop("vector", s_fin, lambda e, g=g: e.tensor_sub(
            coef2[:, g * 8:(g + 1) * 8], Z[:, g * 8:(g + 1) * 8],
            sp[:, g * 8:(g + 1) * 8]))
        c.chainop("vector", s_fin, lambda e, g=g: e.tensor_scalar_mul(
            coef2[:, g * 8:(g + 1) * 8], coef2[:, g * 8:(g + 1) * 8], 0.01))
        c.chainop("vector", s_fin, lambda e, g=g: e.tensor_tensor(
            coef2[:, g * 8:(g + 1) * 8], coef2[:, g * 8:(g + 1) * 8],
            invZ[:, g * 8:(g + 1) * 8], ALU.mult))
        c.wait("vector", s_wr, s_wr.n)
        for ti in range(8):
            qt = g * 8 + ti
            xt = x_t[qt % 2]
            if qt >= 2:
                c.wait("sync", s_xo[qt % 2], B[id(s_xo[qt % 2])] + 16 * (qt // 2))
            c.op("sync", lambda e, qt=qt, xt=xt: e.dma_start(
                out=xt[:, :], in_=x_in[qt * 128:(qt + 1) * 128, :]),
                inc=(s_xl[qt % 2], 16))
            c.wait("vector", s_xl[qt % 2])
            c.chainop("vector", s_fin, lambda e, xt=xt, ti=ti, qt=qt:
                      e.scalar_tensor_tensor(
                          xt[:, :], acc[:, ti * 1024:(ti + 1) * 1024],
                          dinv[:, qt:qt + 1], xt[:, :], ALU.mult, ALU.add))
            c.chainop("vector", s_fin, lambda e, xt=xt, qt=qt:
                      e.scalar_tensor_tensor(
                          xt[:, :], wsum_rep[:, :], coef2[:, qt:qt + 1], xt[:, :],
                          ALU.mult, ALU.add))
            c.wait("sync", s_fin)
            c.op("sync", lambda e, qt=qt, xt=xt: e.dma_start(
                out=x1_out[qt * 128:(qt + 1) * 128, :], in_=xt[:, :]),
                inc=(s_xo[qt % 2], 16))
    c.wait("sync", s_xo[0])
    c.wait("sync", s_xo[1])


def _phase_moe(c, x1_in, gw_in, ewT_d, x2_out, ident, consts, s_ewt_st,
               s_ewt_ev):
    x_sb = c.sbuf("c_x_sb", [128, NT * 1024], F32)
    h_tm = c.sbuf("c_h_tm", [128, NT * 1024], BF16)
    h_fm = c.sbuf("c_h_fm", [128, NC * TSL], BF16)
    scr = c.sbuf("c_scr", [128, 2048], BF16)
    stats = c.sbuf("c_stats", [128, 8 * NT], F32)
    gwT = c.sbuf("c_gwT", [128, NC * 8], BF16)
    gwb = c.sbuf("c_gwb", [8, D], F32)
    gwb16 = c.sbuf("c_gwb16", [8, D], BF16)
    mask8 = c.sbuf("c_mask8", [128, NT * 8], F32)
    gmax = c.sbuf("c_gmax", [128, NT], F32)
    wq = [c.sbuf(f"c_wq{e}", [128, NC * 256], BF16) for e in range(8)]
    accq = c.sbuf("c_accq", [128, NT * 256], BF16)

    s_x = c.sem("ln_x")
    s_h = c.sem("ln_h")
    s_fm = c.sem("ln_fm")
    s_gw = c.sem("c_gw")
    s_gt = c.sem("c_gt")
    s_wq1 = c.sem("c_wq")
    s_wq = [s_wq1] * 8
    s_pe = c.sem("c_pe")
    s_evq = c.sem("c_evq")
    s_fin = c.sem("c_fin")
    s_out = c.sem("c_out")
    B = {id(x): x.n for x in [s_fm, s_gt, s_pe, s_evq, s_fin]}

    p_tr_cm = c.nc.psum_tensor("c_p_tr", [128, 2048], BF16)
    p_tr = p_tr_cm.__enter__()
    p_g_cm = c.nc.psum_tensor("c_p_g", [128, 8], F32)
    p_g = p_g_cm.__enter__()

    c.op("sync", lambda e: e.dma_start(
        out=x_sb.ap().rearrange("p (t d) -> p t d", t=NT),
        in_=x1_in.rearrange("(t p) d -> p t d", p=128)), inc=(s_x, 16))
    c.wait("scalar", consts.sem)
    emit_ln(c, "lnP", x_sb, NT, h_tm, scr, stats, ident, s_h, (s_x, s_x.n),
            consts[EPS], h_fm=h_fm, p_tr=p_tr, s_fm=s_fm)

    c.op("sync", lambda e: e.dma_start(out=gwb[:, :], in_=gw_in),
         inc=(s_gw, 16))
    c.wait("vector", s_gw)
    c.op("vector", lambda e: e.tensor_copy(gwb16[:, :], gwb[:, :]),
         inc=(s_gw, 1))
    c.wait("tensor", s_gw)
    c.wait("tensor", ident[1])
    c.wait("tensor", s_fm)
    for ch in range(NC):
        c.op("tensor", lambda e, ch=ch: e.transpose(
            p_tr[:, ch * 128:ch * 128 + 8],
            gwb16[:, ch * 128:(ch + 1) * 128], ident[0][0:8, 0:8]),
            inc=(s_gw, 1) if ch == NC - 1 else None)
    c.wait("vector", s_gw)
    c.op("vector", lambda e: e.tensor_copy(
        gwT.ap().rearrange("p (ch j) -> p ch j", ch=NC),
        p_tr.ap().rearrange("p (ch j) -> p ch j", ch=16)[:, 0:NC, 0:8]),
        inc=(s_gw, 1))

    for t in range(NT):
        c.wait("tensor", s_gw)
        c.wait("tensor", s_fm, B[id(s_fm)] + t + 1)
        if t >= 1:
            c.wait("tensor", s_gt, B[id(s_gt)] + 2 * t)
        for ch in range(NC):
            c.op("tensor", lambda e, ch=ch, t=t: e.matmul(
                p_g[:, :], h_fm[:, ch * TSL + t * 128:ch * TSL + (t + 1) * 128],
                gwT[:, ch * 8:(ch + 1) * 8],
                start=(ch == 0), stop=(ch == NC - 1)),
                inc=(s_pe, 1) if ch == NC - 1 else None)
        c.wait("vector", s_pe)
        c.op("vector", lambda e, t=t: e.reduce_max(
            gmax[:, t:t + 1], p_g[:, :], axis=AX), inc=(s_gt, 1))
        c.wait("vector", s_gt)
        c.op("vector", lambda e, t=t: e.tensor_single_scalar(
            mask8[:, t * 8:(t + 1) * 8], p_g[:, :], gmax[:, t:t + 1],
            ALU.is_ge), inc=(s_gt, 1))

    p_g_cm.__exit__(None, None, None)
    p_tr_cm.__exit__(None, None, None)
    psE = [c.psum(f"c_psE{i}", [128, 8 * 256], F32) for i in range(2)]
    c.wait("tensor", s_gt)
    c.wait("tensor", s_gw)
    c.wait("tensor", s_fm)
    c.wait("tensor", s_ewt_ev)

    for fq in range(4):
        for e8 in range(8):
            if fq >= 1:
                c.wait("sync", s_pe, B[id(s_pe)] + NT + fq * NT)
            c.wait("sync", s_ewt_st[0], s_ewt_st[0].n)
            c.wait("sync", s_ewt_st[1], s_ewt_st[1].n)
            c.wait("sync", s_wq1, s_wq1.n)
            c.op("sync", lambda e, e8=e8, fq=fq: e.dma_start(
                out=wq[e8].ap().rearrange("p (ch j) -> p ch j", ch=NC),
                in_=ewT_d[e8 * D:(e8 + 1) * D, fq * 256:(fq + 1) * 256]
                    .rearrange("(ch p) j -> p ch j", p=128)),
                inc=(s_wq[e8], 16))
        for t in range(NT):
            gi = fq * NT + t
            grp = psE[gi % 2]
            c.wait("tensor", s_wq1, s_wq1.n)
            c.wait("tensor", s_ewt_st[0], s_ewt_st[0].n)
            c.wait("tensor", s_ewt_st[1], s_ewt_st[1].n)
            if gi >= 2:
                c.wait("tensor", s_evq, B[id(s_evq)] + (gi - 1) * 8)
            for e8 in range(8):
                for ch in range(NC):
                    last = (ch == NC - 1 and e8 == 7)
                    c.op("tensor", lambda e, grp=grp, ch=ch, e8=e8, t=t:
                         e.matmul(grp[:, e8 * 256:(e8 + 1) * 256],
                                  h_fm[:, ch * TSL + t * 128:ch * TSL + (t + 1) * 128],
                                  wq[e8][:, ch * 256:(ch + 1) * 256],
                                  start=(ch == 0), stop=(ch == NC - 1)),
                         inc=(s_pe, 1) if last else None)
            c.wait("vector", s_pe)
            for e8 in range(8):
                aq = accq[:, t * 256:(t + 1) * 256]
                if e8 == 0:
                    if fq >= 1:
                        c.wait("vector", s_fin,
                               B[id(s_fin)] + 2 * ((fq - 1) * NT + t + 1))
                    c.op("vector", lambda e, grp=grp, aq=aq, t=t:
                         e.tensor_single_scalar(
                             aq, grp[:, 0:256], mask8[:, t * 8:t * 8 + 1],
                             ALU.mult), inc=(s_evq, 1))
                else:
                    c.wait("vector", s_evq)
                    c.op("vector", lambda e, grp=grp, aq=aq, t=t, e8=e8:
                         e.scalar_tensor_tensor(
                             aq, grp[:, e8 * 256:(e8 + 1) * 256],
                             mask8[:, t * 8 + e8:t * 8 + e8 + 1], aq,
                             ALU.mult, ALU.add), inc=(s_evq, 1))
            xq = x_sb[:, t * 1024 + fq * 256:t * 1024 + (fq + 1) * 256]
            hq = h_tm[:, t * 1024 + fq * 256:t * 1024 + (fq + 1) * 256]
            c.wait("vector", s_evq)
            c.op("vector", lambda e, xq=xq, hq=hq: e.tensor_add(xq, xq, hq),
                 inc=(s_fin, 1))
            c.chainop("vector", s_fin, lambda e, xq=xq, t=t, fq=fq: e.tensor_add(
                xq, xq, accq[:, t * 256:(t + 1) * 256]))

    c.wait("sync", s_fin)
    c.op("sync", lambda e: e.dma_start(
        out=x2_out.rearrange("(t p) d -> p t d", p=128),
        in_=x_sb.ap().rearrange("p (t d) -> p t d", t=NT)), inc=(s_out, 16))
    c.wait("sync", s_out)
    for cm in [psE[1], psE[0]]:
        pass


def _phase_proj(c, h_fm, s_fm_done, wT_d, s_wt, s_wt_ev, qfm_out, akfm_out,
                utm_out, kscr):
    wb = [c.sbuf(f"d_wb{i}", [128, NC * 1024], BF16) for i in range(2)]
    src2 = c.sbuf("d_src2", [128, NC * TSL], BF16)
    stg = [c.sbuf(f"d_stg{i}", [128, 2048], BF16) for i in range(2)]
    psP = [c.psum(f"d_psP{i}", [128, 2048], F32) for i in range(2)]
    s_wl = [c.sem(f"d_wl{i}") for i in range(2)]
    s_mm = c.sem("d_mm")
    s_ev = c.sem("d_ev")
    s_st = [c.sem(f"d_st{i}") for i in range(2)]
    B = {id(x): x.n for x in [s_mm, s_ev]}

    def proj(wi, slot, src, src_sem, src_val, dst_dram, dst_sbuf, kind,
             prev_mm):
        sl = slot % 2
        if prev_mm is not None:
            c.wait("sync", s_mm, prev_mm)
        c.wait("sync", s_wt[0], s_wt[0].n)
        c.wait("sync", s_wt[1], s_wt[1].n)
        c.op("sync", lambda e, wi=wi, sl=sl: e.dma_start(
            out=wb[sl].ap().rearrange("p (ch m) -> p ch m", ch=NC),
            in_=wT_d[wi * D:(wi + 1) * D, :].rearrange(
                "(ch p) m -> p ch m", p=128)), inc=(s_wl[sl], 16))
        c.wait("tensor", s_wl[sl])
        c.wait("tensor", s_wt[0], s_wt[0].n)
        c.wait("tensor", s_wt[1], s_wt[1].n)
        c.wait("tensor", src_sem, src_val)
        for mc in range(8):
            ps = psP[mc % 2]
            if s_ev.n - B[id(s_ev)] >= 1:
                c.wait("tensor", s_ev, s_ev.n - 1)
            if kind == "fm":
                for ch in range(NC):
                    for hh in range(4):
                        last = (ch == NC - 1 and hh == 3)
                        c.op("tensor", lambda e, ps=ps, ch=ch, hh=hh, mc=mc, sl=sl, src=src:
                             e.matmul(ps[:, hh * 512:(hh + 1) * 512],
                                      wb[sl][:, ch * 1024 + mc * 128:ch * 1024 + (mc + 1) * 128],
                                      src[:, ch * TSL + hh * 512:ch * TSL + (hh + 1) * 512],
                                      start=(ch == 0), stop=(ch == NC - 1)),
                             inc=(s_mm, 1) if last else None)
            else:
                for tt in range(2):
                    t = mc * 2 + tt
                    for ch in range(NC):
                        for hh in range(2):
                            last = (tt == 1 and ch == NC - 1 and hh == 1)
                            c.op("tensor", lambda e, ps=ps, ch=ch, hh=hh, t=t, tt=tt, sl=sl, src=src:
                                 e.matmul(ps[:, tt * 1024 + hh * 512:tt * 1024 + (hh + 1) * 512],
                                          src[:, ch * TSL + t * 128:ch * TSL + (t + 1) * 128],
                                          wb[sl][:, ch * 1024 + hh * 512:ch * 1024 + (hh + 1) * 512],
                                          start=(ch == 0), stop=(ch == NC - 1)),
                                 inc=(s_mm, 1) if last else None)
            c.wait("vector", s_mm)
            if dst_sbuf is not None:
                c.op("vector", lambda e, ps=ps, mc=mc, dst_sbuf=dst_sbuf:
                     e.tensor_copy(dst_sbuf[:, mc * TSL:mc * TSL + 2048], ps[:, :]),
                     inc=(s_ev, 1))
            else:
                c.wait("vector", s_st[mc % 2], s_st[mc % 2].n)
                c.op("vector", lambda e, ps=ps, mc=mc: e.tensor_copy(
                    stg[mc % 2][:, :], ps[:, :]), inc=(s_ev, 1))
                c.wait("sync", s_ev)
                if kind == "fm":
                    c.op("sync", lambda e, mc=mc, dst=dst_dram: e.dma_start(
                        out=dst[mc * 128:(mc + 1) * 128, :],
                        in_=stg[mc % 2][:, :]), inc=(s_st[mc % 2], 16))
                else:
                    c.op("sync", lambda e, mc=mc, dst=dst_dram: e.dma_start(
                        out=dst[mc * 256:(mc + 1) * 256, :].rearrange(
                            "(tt p) d -> p tt d", p=128),
                        in_=stg[mc % 2].ap().rearrange(
                            "p (tt d) -> p tt d", tt=2)),
                        inc=(s_st[mc % 2], 16))
        return s_mm.n

    n0 = proj(0, 0, h_fm[:, :], s_fm_done, s_fm_done.n, None, src2, "fm", None)
    n1 = proj(3, 1, src2[:, :], s_ev, s_ev.n, akfm_out, None, "fm", None)
    n2 = proj(1, 0, h_fm[:, :], s_fm_done, s_fm_done.n, qfm_out, None, "fm", n0)
    n3 = proj(2, 1, h_fm[:, :], s_fm_done, s_fm_done.n, None, src2, "fm", n1)
    n4 = proj(4, 0, src2[:, :], s_ev, s_ev.n, utm_out, None, "tm", n2)
    c.wait("sync", s_st[0])
    c.wait("sync", s_st[1])
    return s_st


def _phase_attn1(c, x2_in, qfm_d, akfm_g, utm_g, x3_out, ident):
    NKB = T // 1024
    q_fm = c.sbuf("e_qfm", [128, NC * TSL], BF16)
    kfm = [c.sbuf(f"e_kfm{i}", [128, NC * 1024], BF16) for i in range(2)]
    ktm = [c.sbuf(f"e_ktm{i}", [128, 8 * 1024], BF16) for i in range(2)]
    Escr = [c.sbuf(f"e_Escr{i}", [128, 1024], BF16) for i in range(2)]
    EmT = [c.sbuf(f"e_EmT{i}", [128, 1024], BF16) for i in range(2)]
    acc = c.sbuf("e_acc", [128, NT * 1024], F32)
    Zparts = c.sbuf("e_Zparts", [128, NT * NKB], F32)
    fstat = c.sbuf("e_fstat", [128, 4 * NT + 9], F32)
    x_t = [c.sbuf(f"e_x_t{i}", [128, 1024], F32) for i in range(2)]
    psc = [c.psum(f"e_psc{i}", [128, 1024], F32) for i in range(2)]
    p_tr = c.psum("e_p_tr", [128, 1024], BF16)
    ps_av = c.psum("e_ps_av", [128, 1024], F32)

    mr = fstat[:, 0:NT]
    negm = fstat[:, NT:2 * NT]
    Zr = fstat[:, 2 * NT:3 * NT]
    invZ = fstat[:, 3 * NT:4 * NT]
    bm = fstat[:, 4 * NT:4 * NT + 1]
    dcol = fstat[:, 4 * NT + 1:4 * NT + 5]
    alph = fstat[:, 4 * NT + 5:4 * NT + 9]

    s_q = c.sem("e_q")
    s_kf = [c.sem(f"a_kf{i}") for i in range(2)]
    s_kt = [c.sem(f"a_kt{i}") for i in range(2)]
    s_xl = [c.sem(f"a_xl{i}") for i in range(2)]
    s_xo = [c.sem(f"a_xo{i}") for i in range(2)]
    s_sc = c.sem("a_sc")
    s_v = c.sem("e_v")
    s_al = c.sem("e_al")
    s_exp = c.sem("a_exp")
    s_tp = c.sem("a_tp")
    s_av = c.sem("a_av")
    s_ev = c.sem("a_ev")
    s_ac = c.sem("a_ac")
    B = {id(x): x.n for x in [s_sc, s_exp, s_tp, s_av, s_ev, s_ac] + s_kf +
         s_kt + s_xl + s_xo}
    v_cnt, al_cnt = {}, {}

    def emit_maxchain(i):
        qt = i % NT
        kb = i // NT
        c.wait("vector", s_sc, B[id(s_sc)] + i + 1)
        if kb == 0:
            c.chainop("vector", s_v, lambda e, qt=qt, i=i: e.reduce_max(
                mr[:, qt:qt + 1], psc[i % 2][:, :], axis=AX))
            c.chainop("vector", s_v, lambda e, qt=qt: e.tensor_scalar_mul(
                negm[:, qt:qt + 1], mr[:, qt:qt + 1], -1.0))
        else:
            if i - 4 in al_cnt:
                c.wait("vector", s_al, al_cnt[i - 4])
            c.chainop("vector", s_v, lambda e, i=i: e.reduce_max(
                bm, psc[i % 2][:, :], axis=AX))
            c.chainop("vector", s_v, lambda e, qt=qt, i=i: e.tensor_sub(
                dcol[:, i % 4:i % 4 + 1], mr[:, qt:qt + 1], bm))
            c.chainop("vector", s_v, lambda e, i=i: e.tensor_scalar_min(
                dcol[:, i % 4:i % 4 + 1], dcol[:, i % 4:i % 4 + 1], 0.0))
            c.chainop("vector", s_v, lambda e, qt=qt: e.tensor_max(
                mr[:, qt:qt + 1], mr[:, qt:qt + 1], bm))
            c.chainop("vector", s_v, lambda e, qt=qt: e.tensor_scalar_mul(
                negm[:, qt:qt + 1], mr[:, qt:qt + 1], -1.0))
        v_cnt[i] = s_v.n

    def emit_exp(i):
        qt = i % NT
        kb = i // NT
        c.wait("scalar", s_v, v_cnt[i])
        if kb > 0:
            if i >= 4:
                c.wait("scalar", s_ac, B[id(s_ac)] + i - 3)
            c.op("scalar", lambda e, i=i: e.activation(
                alph[:, i % 4:i % 4 + 1], dcol[:, i % 4:i % 4 + 1], AF.Exp),
                inc=(s_al, 1))
            al_cnt[i] = s_al.n
        if i >= 2:
            c.wait("scalar", s_tp, B[id(s_tp)] + i - 1)
        c.op("scalar", lambda e, i=i, qt=qt: e.activation(
            Escr[i % 2][:, :], psc[i % 2][:, :], AF.Exp,
            bias=negm[:, qt:qt + 1], accum_out=Zparts[:, i:i + 1]),
            inc=(s_exp, 1))

    def emit_transp(a):
        c.wait("tensor", s_exp, B[id(s_exp)] + a + 1)
        if a >= 1:
            c.wait("tensor", s_ev, B[id(s_ev)] + a)
        for ch in range(8):
            c.op("tensor", lambda e, a=a, ch=ch: e.transpose(
                p_tr[:, ch * 128:(ch + 1) * 128],
                Escr[a % 2][:, ch * 128:(ch + 1) * 128], ident[0][:, :]),
                inc=(s_tp, 1) if ch == 7 else None)

    def emit_evac(a):
        c.wait("scalar", s_tp, B[id(s_tp)] + a + 1)
        if a >= 2:
            c.wait("scalar", s_av, B[id(s_av)] + a - 1)
        c.op("scalar", lambda e, a=a: e.activation(
            EmT[a % 2][:, :], p_tr[:, 0:1024], AF.Identity), inc=(s_ev, 1))

    def emit_av(a):
        kb_a = a // NT
        c.wait("tensor", s_ev, B[id(s_ev)] + a + 1)
        c.wait("tensor", s_kt[kb_a % 2], s_kt[kb_a % 2].n)
        if a >= 1:
            c.wait("tensor", s_ac, B[id(s_ac)] + a)
        for sc in range(8):
            for hh in range(2):
                last = (sc == 7 and hh == 1)
                c.op("tensor", lambda e, sc=sc, hh=hh, a=a, kb_a=kb_a:
                     e.matmul(ps_av[:, hh * 512:(hh + 1) * 512],
                              EmT[a % 2][:, sc * 128:(sc + 1) * 128],
                              ktm[kb_a % 2][:, sc * 1024 + hh * 512:sc * 1024 + (hh + 1) * 512],
                              start=(sc == 0), stop=(sc == 7)),
                     inc=(s_av, 1) if last else None)

    def emit_update(a):
        qt = a % NT
        kb_a = a // NT
        at = acc[:, qt * 1024:(qt + 1) * 1024]
        c.wait("vector", s_av, B[id(s_av)] + a + 1)
        if kb_a == 0:
            c.op("vector", lambda e, at=at: e.tensor_copy(at, ps_av[:, :]))
            c.op("vector", lambda e, qt=qt, a=a: e.tensor_copy(
                Zr[:, qt:qt + 1], Zparts[:, a:a + 1]), inc=(s_ac, 1))
        else:
            c.wait("vector", s_al, al_cnt[a])
            c.op("vector", lambda e, at=at, a=a: e.scalar_tensor_tensor(
                at, at, alph[:, a % 4:a % 4 + 1], ps_av[:, :],
                ALU.mult, ALU.add))
            c.op("vector", lambda e, qt=qt, a=a: e.scalar_tensor_tensor(
                Zr[:, qt:qt + 1], Zr[:, qt:qt + 1], alph[:, a % 4:a % 4 + 1],
                Zparts[:, a:a + 1], ALU.mult, ALU.add), inc=(s_ac, 1))

    c.op("sync", lambda e: e.dma_start(
        out=q_fm.ap().rearrange("p (ch s) -> p ch s", ch=NC),
        in_=qfm_d.rearrange("(ch p) s -> p ch s", p=128)), inc=(s_q, 16))
    c.wait("tensor", s_q)
    c.wait("tensor", ident[1])

    for kb in range(NKB):
        if kb >= 2:
            c.wait("sync", s_sc, B[id(s_sc)] + 16 * (kb - 1))
        c.op("sync", lambda e, kb=kb: e.dma_start(
            out=kfm[kb % 2].ap().rearrange("p (ch s) -> p ch s", ch=NC),
            in_=akfm_g[(kb // 2) * 1024:(kb // 2 + 1) * 1024,
                       (kb % 2) * 1024:(kb % 2 + 1) * 1024]
                .rearrange("(ch p) s -> p ch s", p=128)), inc=(s_kf[kb % 2], 16))
        if kb >= 2:
            c.wait("sync", s_av, B[id(s_av)] + (kb - 1) * 16)
        c.op("sync", lambda e, kb=kb: e.dma_start(
            out=ktm[kb % 2].ap().rearrange("p (sc f) -> p sc f", sc=8),
            in_=utm_g[kb * 1024:(kb + 1) * 1024, :]
                .rearrange("(sc p) f -> p sc f", p=128)), inc=(s_kt[kb % 2], 16))
        for qt in range(NT):
            i = kb * NT + qt
            c.wait("tensor", s_kf[kb % 2], s_kf[kb % 2].n)
            if i >= 2:
                c.wait("tensor", s_exp, B[id(s_exp)] + i - 1)
            for ch in range(NC):
                for hh in range(2):
                    last = (ch == NC - 1 and hh == 1)
                    c.op("tensor", lambda e, ch=ch, hh=hh, qt=qt, kb=kb, i=i:
                         e.matmul(psc[i % 2][:, hh * 512:(hh + 1) * 512],
                                  q_fm[:, ch * TSL + qt * 128:ch * TSL + (qt + 1) * 128],
                                  kfm[kb % 2][:, ch * 1024 + hh * 512:ch * 1024 + (hh + 1) * 512],
                                  start=(ch == 0), stop=(ch == NC - 1)),
                         inc=(s_sc, 1) if last else None)
            emit_maxchain(i)
            emit_exp(i)
            if i >= 1:
                emit_transp(i - 1)
                emit_evac(i - 1)
            if i >= 2:
                emit_av(i - 2)
                emit_update(i - 2)

    last_t = NKB * NT - 1
    emit_transp(last_t)
    emit_evac(last_t)
    emit_av(last_t - 1)
    emit_update(last_t - 1)
    emit_av(last_t)
    emit_update(last_t)

    c.wait("vector", s_ac, B[id(s_ac)] + NKB * NT)
    c.chainop("vector", s_v, lambda e: e.reciprocal(invZ, Zr))
    for qt in range(NT):
        xt = x_t[qt % 2]
        if qt >= 2:
            c.wait("sync", s_xo[qt % 2], B[id(s_xo[qt % 2])] + 16 * (qt // 2))
        c.op("sync", lambda e, qt=qt, xt=xt: e.dma_start(
            out=xt[:, :], in_=x2_in[qt * 128:(qt + 1) * 128, :]),
            inc=(s_xl[qt % 2], 16))
        c.wait("vector", s_xl[qt % 2], s_xl[qt % 2].n)
        c.chainop("vector", s_v, lambda e, xt=xt, qt=qt: e.scalar_tensor_tensor(
            xt[:, :], acc[:, qt * 1024:(qt + 1) * 1024], invZ[:, qt:qt + 1],
            xt[:, :], ALU.mult, ALU.add))
        c.wait("sync", s_v)
        c.op("sync", lambda e, qt=qt, xt=xt: e.dma_start(
            out=x3_out[qt * 128:(qt + 1) * 128, :], in_=xt[:, :]),
            inc=(s_xo[qt % 2], 16))
    c.wait("sync", s_xo[0])
    c.wait("sync", s_xo[1])


def emit_cc_scales(c, name, gmax11, s_gm, cc_in, cc_out, rg, identf,
                   n_vals=1, bc=None):
    amr = c.sbuf(name + "_amr", [1, 4 * n_vals], F32)
    ones_row = c.sbuf(name + "_ones", [1, 128], F32)
    if bc is None:
        bc = c.sbuf(name + "_bc", [128, 3 * n_vals], F32)
    p_b = c.psum(name + "_pb", [128, 3 * n_vals], F32)
    s = c.sem(name + "_s")
    s_ccl = c.sem(name + "_cc")
    c.op("gpsimd", lambda e: e.memset(ones_row[:, :], 1.0), inc=(s, 1))
    c.wait("sync", s_gm, s_gm.n)
    c.op("sync", lambda e: e.dma_start(out=cc_in, in_=gmax11), inc=(s, 16))
    c.wait("gpsimd", s)
    c.op("gpsimd", lambda e: e.collective_compute(
        "AllReduce", ALU.max, replica_groups=rg, ins=[cc_in], outs=[cc_out]),
        inc=(s_ccl, 1))
    c.wait("sync", s_ccl, 1)
    c.op("sync", lambda e: e.dma_start(
        out=amr[:, 0:n_vals], in_=cc_out.rearrange("v o -> o v")),
        inc=(s, 16))
    c.wait("vector", s)
    for v in range(n_vals):
        c.chainop("vector", s, lambda e, v=v: e.tensor_scalar(
            amr[:, n_vals + 3 * v:n_vals + 3 * v + 1], amr[:, v:v + 1],
            1.0 / 32767.0, 1e-12, ALU.mult, ALU.max))
        c.chainop("vector", s, lambda e, v=v: e.reciprocal(
            amr[:, n_vals + 3 * v + 1:n_vals + 3 * v + 2],
            amr[:, n_vals + 3 * v:n_vals + 3 * v + 1]))
        c.chainop("vector", s, lambda e, v=v: e.tensor_scalar_mul(
            amr[:, n_vals + 3 * v + 2:n_vals + 3 * v + 3],
            amr[:, n_vals + 3 * v + 1:n_vals + 3 * v + 2], -1.0))
    c.wait("tensor", s)
    c.op("tensor", lambda e: e.matmul(
        p_b[:, :], ones_row[:, :], amr[:, n_vals:4 * n_vals],
        start=True, stop=True), inc=(s, 1))
    c.wait("vector", s)
    c.op("vector", lambda e: e.tensor_copy(bc[:, :], p_b[:, :]), inc=(s, 1))
    return bc, s


def emit_colmax_to_scalar(c, name, parts, width, identf, p_am):
    pfin = c.sbuf(name + "_pfin", [128, 1], F32)
    out11 = c.sbuf(name + "_o11", [1, 1], F32)
    s = c.sem(name + "_s")
    c.chainop("vector", s, lambda e: e.tensor_reduce(
        pfin[:, :], parts, axis=AX, op=ALU.max, apply_absolute_value=True))
    c.wait("tensor", s)
    c.wait("tensor", identf[1])
    c.op("tensor", lambda e: e.transpose(
        p_am[0:1, :], pfin[:, :], identf[0][:, :]), inc=(s, 1))
    c.wait("vector", s)
    c.chainop("vector", s, lambda e: e.tensor_reduce(
        out11[:, :], p_am[0:1, :], axis=AX, op=ALU.max,
        apply_absolute_value=True))
    return out11, s


def _phase_ffn_up(c, h3_tm, s_h3, bc1, s_bc1, wugT_d, s_wt, s_wt_ev,
                  up_out, gt_out, ident, identf, p_am):
    xq_tm = c.sbuf("g_xq_tm", [128, NT * 1024], BF16)
    xq_fm = c.sbuf("g_xq_fm", [128, NC * TSL], BF16)
    rb = mk_round_bufs(c, "g_rb", 1024)
    wub = [c.sbuf(f"g_wub{i}", [128, NC * 512], BF16) for i in range(4)]
    stg = [c.sbuf(f"g_stg{i}", [128, 512], BF16) for i in range(4)]
    parts = c.sbuf("g_parts", [128, 2 * 96], F32)
    pfin2 = c.sbuf("g_pfin2", [128, 2], F32)
    am2 = c.sbuf("g_am2", [2, 1], F32)

    s_ch = c.sem("f_ch")
    s_a = c.sem("f_a")
    s_xq = c.sem("f_xq")
    s_fm = c.sem("f_fm")
    s_wl = [c.sem(f"f_wl{i}") for i in range(4)]
    s_mm = c.sem("f_mm")
    s_evq = c.sem("f_evq")
    s_pt = c.sem("f_pt")
    s_st = [c.sem(f"f_st{i}") for i in range(4)]
    B = {id(x): x.n for x in [s_mm, s_evq, s_pt, s_xq]}

    c.wait("scalar", s_bc1, s_bc1.n)
    c.wait("vector", s_bc1, s_bc1.n)
    c.wait("scalar", s_h3, s_h3.n)
    for t in range(NT):
        src = h3_tm[:, t * 1024:(t + 1) * 1024]
        emit_round(c, s_a, src, 1024, rb, bc1[:, 2:3], s_ch)
        c.wait("scalar", s_ch)
        c.op("scalar", lambda e, t=t: e.activation(
            xq_tm[:, t * 1024:(t + 1) * 1024], rb["r"][:, 0:1024],
            AF.Identity, scale=bc1[:, 0:1]), inc=(s_xq, 1))
        c.wait("vector", s_xq)

    p_tr_cm = c.nc.psum_tensor("g_p_tr", [128, 2048], BF16)
    p_tr = p_tr_cm.__enter__()
    emit_transpose_tiles(c, NT, xq_tm, xq_fm, p_tr, ident, s_xq, s_fm,
                         name="lnP", s_src_base=B[id(s_xq)])
    p_tr_cm.__exit__(None, None, None)
    psQ = [c.psum(f"g_psQ{i}", [128, 512], F32) for i in range(4)]

    c.wait("tensor", s_fm)
    c.wait("tensor", s_wt_ev, s_wt_ev.n)
    c.wait("sync", s_wt[0], s_wt[0].n)
    c.wait("sync", s_wt[1], s_wt[1].n)
    for ic in range(6):
        for w2 in range(2):
            sl = 2 * (ic % 2) + w2
            if ic >= 2:
                c.wait("sync", s_mm, B[id(s_mm)] + (ic - 1) * 32)
            c.op("sync", lambda e, ic=ic, w2=w2, sl=sl: e.dma_start(
                out=wub[sl].ap().rearrange("p (ch j) -> p ch j", ch=NC),
                in_=wugT_d[w2 * D:(w2 + 1) * D, ic * 512:(ic + 1) * 512]
                    .rearrange("(ch p) j -> p ch j", p=128)),
                inc=(s_wl[sl], 16))
        for t in range(NT):
            for w2 in range(2):
                gi = ic * 32 + t * 2 + w2
                sl = 2 * (ic % 2) + w2
                ps = psQ[2 * (t % 2) + w2]
                c.wait("tensor", s_wl[sl], s_wl[sl].n)
                if gi >= 4:
                    c.wait("tensor", s_evq, B[id(s_evq)] + gi - 3)
                    c.wait("tensor", s_pt, B[id(s_pt)] + gi - 3)
                for ch in range(NC):
                    c.op("tensor", lambda e, ps=ps, ch=ch, t=t, sl=sl:
                         e.matmul(ps[:, :],
                                  xq_fm[:, ch * TSL + t * 128:ch * TSL + (t + 1) * 128],
                                  wub[sl][:, ch * 512:(ch + 1) * 512],
                                  start=(ch == 0), stop=(ch == NC - 1)),
                         inc=(s_mm, 1) if ch == NC - 1 else None)
                c.wait("scalar", s_mm)
                c.wait("scalar", s_st[2 * (t % 2) + w2],
                       s_st[2 * (t % 2) + w2].n)
                c.wait("scalar", s_pt, max(B[id(s_pt)], B[id(s_pt)] + gi - 3))
                c.op("scalar", lambda e, ps=ps, sl2=2 * (t % 2) + w2:
                     e.activation(stg[sl2][:, :], ps[:, :], AF.Identity),
                     inc=(s_evq, 1))
                c.wait("vector", s_evq)
                c.op("vector", lambda e, sl2=2 * (t % 2) + w2,
                     col=w2 * 96 + ic * 16 + t:
                     e.tensor_reduce(parts[:, col:col + 1], stg[sl2][:, :],
                                     axis=AX, op=ALU.max,
                                     apply_absolute_value=True),
                     inc=(s_pt, 1))
                c.wait("sync", s_evq)
                dst = (up_out if w2 == 0 else gt_out)[
                    t * 128:(t + 1) * 128, ic * 512:(ic + 1) * 512]
                c.op("sync", lambda e, dst=dst, sl2=2 * (t % 2) + w2:
                     e.dma_start(out=dst, in_=stg[sl2][:, :]),
                     inc=(s_st[2 * (t % 2) + w2], 16))
    s_am = c.sem("g_am_s")
    c.wait("vector", s_pt)
    c.chainop("vector", s_am, lambda e: e.tensor_reduce(
        pfin2[:, :], parts.ap().rearrange("p (w k) -> p w k", w=2), axis=AX,
        op=ALU.max, apply_absolute_value=True))
    c.wait("tensor", s_am)
    c.wait("tensor", identf[1])
    c.op("tensor", lambda e: e.transpose(
        p_am[0:2, :], pfin2[:, :], identf[0][:, :]), inc=(s_am, 1))
    c.wait("vector", s_am)
    c.chainop("vector", s_am, lambda e: e.tensor_reduce(
        am2[:, :], p_am[0:2, :], axis=AX, op=ALU.max,
        apply_absolute_value=True))
    for ss in s_st:
        c.wait("sync", ss)
    return am2, s_am


def _phase_ffn_h(c, up_in, gt_in, bc2, bc3, s_bc, upq_out, g2_out, identf,
                 p_am):
    W = DI // 2
    NCH = 2 * NT
    ldg = [c.sbuf(f"h_ldg{i}", [128, W], BF16) for i in range(2)]
    ldu = [c.sbuf(f"h_ldu{i}", [128, W], BF16) for i in range(2)]
    rb = [mk_round_bufs(c, f"h_rb{i}", W) for i in range(4)]
    sigb = [c.sbuf(f"h_sigb{i}", [128, W], F32) for i in range(2)]
    g1b = [c.sbuf(f"h_g1b{i}", [128, W], BF16) for i in range(2)]
    outg = [c.sbuf(f"h_outg{i}", [128, W], BF16) for i in range(2)]
    outu = [c.sbuf(f"h_outu{i}", [128, W], BF16) for i in range(2)]
    parts = c.sbuf("h_parts", [128, NCH], F32)
    s_lg = [c.sem(f"f_wl{i}") for i in range(2)]
    s_lu = [c.sem(f"f_wl{i + 2}") for i in range(2)]
    s_ch = c.sem("f_ch")
    s_a = c.sem("f_a")
    s_stg = [c.sem(f"f_st{i}") for i in range(2)]
    s_stu = [c.sem(f"f_st{i + 2}") for i in range(2)]
    a1c, v1c, a2c, v2c, a3c, v3c = {}, {}, {}, {}, {}, {}

    c.wait("scalar", s_bc, s_bc.n)
    c.wait("vector", s_bc, s_bc.n)

    def chunk(src, k):
        t, co = k // 2, (k % 2) * W
        return src[t * 128:(t + 1) * 128, co:co + W]

    for it in range(NCH + 3):
        k = it
        if k < NCH:
            if k >= 2:
                c.wait("sync", s_ch, v1c[k - 2])
            c.op("sync", lambda e, k=k: e.dma_start(
                out=ldg[k % 2][:, :], in_=chunk(gt_in, k)),
                inc=(s_lg[k % 2], 16))
            r = rb[k % 4]
            if k >= 4:
                c.wait("scalar", s_ch, v3c[k - 4])
            c.wait("scalar", s_lg[k % 2], s_lg[k % 2].n)
            c.chainop("scalar", s_a, lambda e, r=r, k=k: e.activation(
                r["i"][:, :], ldg[k % 2][:, :], AF.Identity,
                scale=bc3[:, 2:3]))
            c.chainop("scalar", s_a, lambda e, r=r: e.activation(
                r["cf"][:, :], r["i"][:, :], AF.Copy))
            a1c[k] = s_a.n
        k = it - 1
        if 0 <= k < NCH:
            r = rb[k % 4]
            c.wait("vector", s_a, a1c[k])
            c.chainop("vector", s_ch, lambda e, r=r, k=k:
                      e.scalar_tensor_tensor(
                          r["d"][:, :], ldg[k % 2][:, :], bc3[:, 2:3],
                          r["cf"][:, :], ALU.mult, ALU.subtract))
            c.chainop("vector", s_ch, lambda e, r=r: e.scalar_tensor_tensor(
                r["a"][:, :], r["d"][:, :], 0.5, r["cf"][:, :],
                ALU.is_ge, ALU.add))
            c.chainop("vector", s_ch, lambda e, r=r: e.scalar_tensor_tensor(
                r["r"][:, :], r["d"][:, :], -0.5, r["a"][:, :],
                ALU.is_le, ALU.subtract))
            v1c[k] = s_ch.n
            c.wait("scalar", s_ch, v1c[k])
            if k >= 2:
                c.wait("scalar", s_ch, v2c[k - 2])
            c.chainop("scalar", s_a, lambda e, r=r, k=k: e.activation(
                sigb[k % 2][:, :], r["r"][:, :], AF.Sigmoid,
                scale=bc3[:, 0:1]))
            c.chainop("scalar", s_a, lambda e, r=r, k=k: e.activation(
                g1b[k % 2][:, :], r["r"][:, :], AF.Identity,
                scale=bc3[:, 0:1]))
            c.chainop("scalar", s_a, lambda e, r=r, k=k: e.activation(
                r["i"][:, :], sigb[k % 2][:, :], AF.Identity, scale=-65536.0))
            c.chainop("scalar", s_a, lambda e, r=r: e.activation(
                r["cf"][:, :], r["i"][:, :], AF.Copy))
            a2c[k] = s_a.n
        k = it - 2
        if 0 <= k < NCH:
            r = rb[k % 4]
            c.wait("vector", s_a, a2c[k])
            if k >= 2:
                c.wait("vector", s_stg[k % 2], s_stg[k % 2].n)
            c.chainop("vector", s_ch, lambda e, r=r, k=k:
                      e.scalar_tensor_tensor(
                          r["d"][:, :], sigb[k % 2][:, :], -65536.0,
                          r["cf"][:, :], ALU.mult, ALU.subtract))
            c.chainop("vector", s_ch, lambda e, r=r: e.scalar_tensor_tensor(
                r["a"][:, :], r["d"][:, :], 0.5, r["cf"][:, :],
                ALU.is_ge, ALU.add))
            c.chainop("vector", s_ch, lambda e, r=r: e.scalar_tensor_tensor(
                r["r"][:, :], r["d"][:, :], -0.5, r["a"][:, :],
                ALU.is_le, ALU.subtract))
            c.chainop("vector", s_ch, lambda e, r=r: e.tensor_scalar_min(
                r["r"][:, :], r["r"][:, :], 65535.0))
            c.chainop("vector", s_ch, lambda e, r=r, k=k:
                      e.scalar_tensor_tensor(
                          outg[k % 2][:, :], r["r"][:, :], 1.0 / 65536.0,
                          g1b[k % 2][:, :], ALU.mult, ALU.mult))
            c.chainop("vector", s_ch, lambda e, k=k: e.tensor_reduce(
                parts[:, k:k + 1], outg[k % 2][:, :], axis=AX, op=ALU.max,
                apply_absolute_value=True))
            v2c[k] = s_ch.n
            c.wait("sync", s_ch, v2c[k])
            c.op("sync", lambda e, k=k: e.dma_start(
                out=chunk(g2_out, k), in_=outg[k % 2][:, :]),
                inc=(s_stg[k % 2], 16))
            if k >= 2:
                c.wait("sync", s_ch, v3c[k - 2])
            c.op("sync", lambda e, k=k: e.dma_start(
                out=ldu[k % 2][:, :], in_=chunk(up_in, k)),
                inc=(s_lu[k % 2], 16))
        k = it - 3
        if 0 <= k < NCH:
            r = rb[k % 4]
            c.wait("scalar", s_ch, v2c[k])
            c.wait("scalar", s_lu[k % 2], s_lu[k % 2].n)
            c.chainop("scalar", s_a, lambda e, r=r, k=k: e.activation(
                r["i"][:, :], ldu[k % 2][:, :], AF.Identity,
                scale=bc2[:, 2:3]))
            c.chainop("scalar", s_a, lambda e, r=r: e.activation(
                r["cf"][:, :], r["i"][:, :], AF.Copy))
            a3c[k] = s_a.n
            c.wait("vector", s_a, a3c[k])
            c.chainop("vector", s_ch, lambda e, r=r, k=k:
                      e.scalar_tensor_tensor(
                          r["d"][:, :], ldu[k % 2][:, :], bc2[:, 2:3],
                          r["cf"][:, :], ALU.mult, ALU.subtract))
            c.chainop("vector", s_ch, lambda e, r=r: e.scalar_tensor_tensor(
                r["a"][:, :], r["d"][:, :], 0.5, r["cf"][:, :],
                ALU.is_ge, ALU.add))
            c.chainop("vector", s_ch, lambda e, r=r: e.scalar_tensor_tensor(
                r["r"][:, :], r["d"][:, :], -0.5, r["a"][:, :],
                ALU.is_le, ALU.subtract))
            v3c[k] = s_ch.n
            c.wait("scalar", s_ch, v3c[k])
            if k >= 2:
                c.wait("scalar", s_stu[k % 2], s_stu[k % 2].n)
            c.op("scalar", lambda e, r=r, k=k: e.activation(
                outu[k % 2][:, :], r["r"][:, :], AF.Identity,
                scale=bc2[:, 0:1]), inc=(s_a, 1))
            c.wait("sync", s_a)
            c.op("sync", lambda e, k=k: e.dma_start(
                out=chunk(upq_out, k), in_=outu[k % 2][:, :]),
                inc=(s_stu[k % 2], 16))
    c.wait("vector", s_ch)
    c.wait("vector", s_a)
    am4, s_am = emit_colmax_to_scalar(c, "h_am", parts[:, :], NCH, identf,
                                      p_am)
    for ss in s_stg + s_stu:
        c.wait("sync", ss)
    return am4, s_am


def _phase_ffn_i(c, g2_in, upq_in, bc4, s_bc, o_out, identf, p_am):
    ldg = [c.sbuf(f"i_ldg{i}", [128, DI], BF16) for i in range(2)]
    ldu = [c.sbuf(f"i_ldu{i}", [128, DI], BF16) for i in range(2)]
    rb = [mk_round_bufs(c, f"i_rb{i}", DI) for i in range(2)]
    outb = [c.sbuf(f"i_outb{i}", [128, DI], BF16) for i in range(2)]
    parts = c.sbuf("i_parts", [128, NT], F32)
    s_lg = [c.sem(f"f_wl{i}") for i in range(2)]
    s_lu = [c.sem(f"f_wl{i + 2}") for i in range(2)]
    s_ch = c.sem("f_ch")
    s_a = c.sem("f_a")
    s_st = [c.sem(f"f_st{i}") for i in range(2)]
    ac, vc = {}, {}

    c.wait("scalar", s_bc, s_bc.n)
    c.wait("vector", s_bc, s_bc.n)
    for it in range(NT + 1):
        t = it
        if t < NT:
            sl = t % 2
            if t >= 2:
                c.wait("sync", s_ch, vc[t - 2])
            c.op("sync", lambda e, t=t, sl=sl: e.dma_start(
                out=ldg[sl][:, :], in_=g2_in[t * 128:(t + 1) * 128, :]),
                inc=(s_lg[sl], 16))
            c.op("sync", lambda e, t=t, sl=sl: e.dma_start(
                out=ldu[sl][:, :], in_=upq_in[t * 128:(t + 1) * 128, :]),
                inc=(s_lu[sl], 16))
            r = rb[sl]
            if t >= 2:
                c.wait("scalar", s_ch, vc[t - 2])
            c.wait("scalar", s_lg[sl], s_lg[sl].n)
            c.chainop("scalar", s_a, lambda e, r=r, sl=sl: e.activation(
                r["i"][:, :], ldg[sl][:, :], AF.Identity, scale=bc4[:, 2:3]))
            c.chainop("scalar", s_a, lambda e, r=r: e.activation(
                r["cf"][:, :], r["i"][:, :], AF.Copy))
            ac[t] = s_a.n
        t = it - 1
        if 0 <= t < NT:
            sl = t % 2
            r = rb[sl]
            c.wait("vector", s_a, ac[t])
            c.wait("vector", s_lu[sl], s_lu[sl].n)
            c.chainop("vector", s_ch, lambda e, r=r, sl=sl:
                      e.scalar_tensor_tensor(
                          r["d"][:, :], ldg[sl][:, :], bc4[:, 2:3],
                          r["cf"][:, :], ALU.mult, ALU.subtract))
            c.chainop("vector", s_ch, lambda e, r=r: e.scalar_tensor_tensor(
                r["a"][:, :], r["d"][:, :], 0.5, r["cf"][:, :],
                ALU.is_ge, ALU.add))
            c.chainop("vector", s_ch, lambda e, r=r: e.scalar_tensor_tensor(
                r["r"][:, :], r["d"][:, :], -0.5, r["a"][:, :],
                ALU.is_le, ALU.subtract))
            ob = outb[sl]
            c.wait("vector", s_st[sl], s_st[sl].n)
            c.chainop("vector", s_ch, lambda e, ob=ob, r=r, sl=sl:
                      e.scalar_tensor_tensor(ob[:, :], r["r"][:, :],
                                             bc4[:, 0:1], ldu[sl][:, :],
                                             ALU.mult, ALU.mult))
            c.chainop("vector", s_ch, lambda e, ob=ob, t=t: e.tensor_reduce(
                parts[:, t:t + 1], ob[:, :], axis=AX, op=ALU.max,
                apply_absolute_value=True))
            vc[t] = s_ch.n
            c.wait("sync", s_ch, vc[t])
            c.op("sync", lambda e, ob=ob, t=t: e.dma_start(
                out=o_out[t * 128:(t + 1) * 128, :], in_=ob[:, :]),
                inc=(s_st[sl], 16))
    c.wait("vector", s_ch)
    am5, s_am = emit_colmax_to_scalar(c, "i_am", parts[:, :], NT, identf, p_am)
    c.wait("sync", s_st[0])
    c.wait("sync", s_st[1])
    return am5, s_am


def _phase_ffn_j(c, o_in, bc5, s_bc, x3_in, wdnT_d, s_wt, s_wt_ev, y_out,
                 ident, rb):
    ldo = [c.sbuf(f"j_ldo{i}", [128, DI], BF16) for i in range(2)]
    o_b = c.sbuf("j_o_b", [128, DI], BF16)
    o_fm = [c.sbuf(f"j_o_fm{i}", [128, DI], BF16) for i in range(2)]
    wdT = c.sbuf("j_wdT", [128, (DI // 128) * 1024], BF16)
    x_t = [c.sbuf(f"j_x_t{i}", [128, 1024], F32) for i in range(2)]
    s_lo = [c.sem(f"f_wl{i}") for i in range(2)]
    s_ch = c.sem("f_ch")
    s_a = c.sem("f_a")
    s_ob = c.sem("j_ob")
    s_tp = c.sem("a_tp")
    s_ofm = c.sem("j_ofm")
    s_mm = c.sem("f_mm")
    s_wl = c.sem("j_wl")
    s_xl = [c.sem(f"a_xl{i}") for i in range(2)]
    s_xo = [c.sem(f"a_xo{i}") for i in range(2)]
    s_fin = c.sem("a_fin")
    B = {id(x): x.n for x in [s_ob, s_tp, s_ofm, s_mm, s_fin] + s_xl + s_xo}

    c.wait("sync", s_wt[0], s_wt[0].n)
    c.wait("sync", s_wt[1], s_wt[1].n)
    c.op("sync", lambda e: e.dma_start(
        out=wdT.ap().rearrange("p (ic j) -> p ic j", ic=DI // 128),
        in_=wdnT_d.rearrange("(ic p) j -> p ic j", p=128)), inc=(s_wl, 16))
    p_tr_cm = c.nc.psum_tensor("j_p_tr", [128, 2048], BF16)
    p_tr = p_tr_cm.__enter__()
    p_dn_cm = [c.nc.psum_tensor(f"j_p_dn{i}", [128, 1024], F32)
               for i in range(2)]
    p_dn = [p.__enter__() for p in p_dn_cm]

    c.wait("scalar", s_bc, s_bc.n)
    c.wait("vector", s_bc, s_bc.n)
    c.wait("tensor", s_wt_ev, s_wt_ev.n)
    c.wait("tensor", ident[1])
    for t in range(NT):
        sl = t % 2
        if t >= 2:
            c.wait("sync", s_ch, s_ch.n)
        c.op("sync", lambda e, t=t, sl=sl: e.dma_start(
            out=ldo[sl][:, :], in_=o_in[t * 128:(t + 1) * 128, :]),
            inc=(s_lo[sl], 16))
        c.wait("scalar", s_lo[sl], s_lo[sl].n)
        c.wait("vector", s_lo[sl], s_lo[sl].n)
        emit_round(c, s_a, ldo[sl][:, :], DI, rb, bc5[:, 2:3], s_ch)
        c.wait("scalar", s_ch)
        if t >= 1:
            c.wait("scalar", s_tp, B[id(s_tp)] + t * 3)
        c.op("scalar", lambda e: e.activation(
            o_b[:, :], rb["r"][:, :], AF.Identity, scale=bc5[:, 0:1]),
            inc=(s_ob, 1))
        c.wait("vector", s_ob)
        c.wait("tensor", s_ob)
        for g in range(3):
            gg = t * 3 + g
            half = (gg % 2) * 1024
            if gg >= 2:
                c.wait("tensor", s_ofm, B[id(s_ofm)] + gg - 1)
            for icc in range(8):
                c.op("tensor", lambda e, g=g, icc=icc, half=half: e.transpose(
                    p_tr[:, half + icc * 128:half + (icc + 1) * 128],
                    o_b[:, (g * 8 + icc) * 128:(g * 8 + icc + 1) * 128],
                    ident[0][:, :]), inc=(s_tp, 1) if icc == 7 else None)
            c.wait("vector", s_tp)
            if t >= 2:
                c.wait("vector", s_mm, B[id(s_mm)] + t - 1)
            c.op("vector", lambda e, sl=sl, g=g, half=half: e.tensor_copy(
                o_fm[sl][:, g * 1024:(g + 1) * 1024], p_tr[:, half:half + 1024]),
                inc=(s_ofm, 1))
        c.wait("tensor", s_ofm)
        c.wait("tensor", s_wl)
        pd = p_dn[t % 2]
        if t >= 2:
            c.wait("tensor", s_fin, B[id(s_fin)] + 2 * (t - 1))
        for icc in range(24):
            for hh in range(2):
                last = (icc == 23 and hh == 1)
                c.op("tensor", lambda e, pd=pd, icc=icc, hh=hh, sl=sl:
                     e.matmul(pd[:, hh * 512:(hh + 1) * 512],
                              o_fm[sl][:, icc * 128:(icc + 1) * 128],
                              wdT[:, icc * 1024 + hh * 512:icc * 1024 + (hh + 1) * 512],
                              start=(icc == 0), stop=(icc == 23)),
                     inc=(s_mm, 1) if last else None)
        xt = x_t[t % 2]
        if t >= 2:
            c.wait("sync", s_xo[t % 2], B[id(s_xo[t % 2])] + 16 * (t // 2))
        c.op("sync", lambda e, t=t, xt=xt: e.dma_start(
            out=xt[:, :], in_=x3_in[t * 128:(t + 1) * 128, :]),
            inc=(s_xl[t % 2], 16))
        c.wait("vector", s_xl[t % 2], s_xl[t % 2].n)
        c.wait("vector", s_mm)
        c.op("vector", lambda e, xt=xt, pd=pd: e.scalar_tensor_tensor(
            xt[:, :], pd[:, :], 1.0, xt[:, :], ALU.mult, ALU.add),
            inc=(s_fin, 2))
        c.wait("sync", s_fin)
        c.op("sync", lambda e, t=t, xt=xt: e.dma_start(
            out=y_out[t * 128:(t + 1) * 128, :], in_=xt[:, :]),
            inc=(s_xo[t % 2], 16))
    c.wait("sync", s_xo[0])
    c.wait("sync", s_xo[1])
    p_dn_cm[1].__exit__(None, None, None)
    p_dn_cm[0].__exit__(None, None, None)
    p_tr_cm.__exit__(None, None, None)


def build_fused():
    nc = bass.Bass("TRN2", target_bir_lowering=False)
    RG = [list(range(NCORES))]
    x_in = nc.dram_tensor("x", [TSL, D], F32, kind="ExternalInput").ap()
    W_in = nc.dram_tensor("W", [D, D], F32, kind="ExternalInput").ap()
    gw_in = nc.dram_tensor("gate_w", [8, D], F32, kind="ExternalInput").ap()
    ew_in = nc.dram_tensor("expert_w", [8 * D, D], F32, kind="ExternalInput").ap()
    attw_in = nc.dram_tensor("attw", [5 * D, D], F32, kind="ExternalInput").ap()
    wug_in = nc.dram_tensor("wug", [2 * DI, D], F32, kind="ExternalInput").ap()
    wd_in = nc.dram_tensor("w_down", [D, DI], F32, kind="ExternalInput").ap()
    y_out = nc.dram_tensor("y", [TSL, D], F32, kind="ExternalOutput").ap()

    dt_ = nc.dram_tensor
    x1_d = dt_("x1_d", [TSL, D], F32).ap()
    x2_d = dt_("x2_d", [TSL, D], F32).ap()
    x3_d = dt_("x3_d", [TSL, D], F32).ap()
    E_dram = dt_("E_scr", [TSL, T], BF16).ap()
    wsum_d = dt_("wsum_d", [1, D], F32).ap()
    h0tm_in = dt_("h0tm_in", [TSL, D], BF16).ap()
    h0fm_in = dt_("h0fm_in", [D, TSL], BF16).ap()
    h0tm_g = dt_("h0tm_g", [T, D], BF16, addr_space="Shared").ap()
    h0fm_g = dt_("h0fm_g", [NCORES * D, TSL], BF16, addr_space="Shared").ap()
    ewT_d = dt_("ewT", [8 * D, D], BF16).ap()
    attwT_d = dt_("attwT", [5 * D, D], BF16).ap()
    qfm_d = dt_("qfm_d", [D, TSL], BF16).ap()
    akfm_in = dt_("akfm_in", [D, TSL], BF16).ap()
    utm_in = dt_("utm_in", [TSL, D], BF16).ap()
    akfm_g = dt_("akfm_g", [NCORES * D, TSL], BF16, addr_space="Shared").ap()
    utm_g = dt_("utm_g", [T, D], BF16, addr_space="Shared").ap()
    up_d = dt_("up_d", [TSL, DI], BF16).ap()
    gt_d = dt_("gt_d", [TSL, DI], BF16).ap()
    upq_d = dt_("upq_d", [TSL, DI], BF16).ap()
    g2_d = dt_("g2_d", [TSL, DI], BF16).ap()
    o_d = dt_("o_d", [TSL, DI], BF16).ap()
    wugdq_d = dt_("wugdq", [2 * DI, D], BF16).ap()
    wugT_d = dt_("wugT", [2 * D, DI], BF16).ap()
    wdndq_d = dt_("wdndq", [D, DI], BF16).ap()
    wdnT_d = dt_("wdnT", [DI, D], BF16).ap()
    cc1_in = dt_("cc1_in", [1, 1], F32).ap()
    cc1_out = dt_("cc1_out", [1, 1], F32).ap()
    cc2_in = dt_("cc2_in", [2, 1], F32).ap()
    cc2_out = dt_("cc2_out", [2, 1], F32).ap()
    cc4_in = dt_("cc4_in", [1, 1], F32).ap()
    cc4_out = dt_("cc4_out", [1, 1], F32).ap()
    cc5_in = dt_("cc5_in", [1, 1], F32).ap()
    cc5_out = dt_("cc5_out", [1, 1], F32).ap()

    c = Ctx(nc)
    ident = make_identity(c, "ident", BF16)
    identf = make_identity(c, "identf", F32)
    consts = Consts(c, [EPS, -32.0])
    s_cc = c.sem("s_cc")
    bc1_t = c.sbuf("bc1_t", [128, 3], F32)
    bc23_t = c.sbuf("bc23_t", [128, 6], F32)
    bc4_t = c.sbuf("bc4_t", [128, 3], F32)
    bc5_t = c.sbuf("bc5_t", [128, 3], F32)

    c.push_scope()
    h0_fm = c.sbuf("p1_h0fm", [128, NC * TSL], BF16)
    c.push_scope()
    h0_tm = c.sbuf("p1_h0tm", [128, NT * 1024], BF16)
    p_tr1_cm = nc.psum_tensor("p1_ptr", [128, 2048], BF16)
    p_tr1 = p_tr1_cm.__enter__()
    c.wait("scalar", consts.sem)
    s_h, s_fm, _xsb = _phase_ln(c, "p1", x_in, None, ident, consts[EPS],
                                h0_tm, h0_fm, p_tr1, None)
    c.wait("sync", s_h)
    c.op("sync", lambda e: e.dma_start(
        out=h0tm_in.rearrange("(t p) d -> p t d", p=128),
        in_=h0_tm.ap().rearrange("p (t d) -> p t d", t=NT)), inc=(s_cc, 16))
    c.wait("sync", s_fm)
    c.op("sync", lambda e: e.dma_start(
        out=h0fm_in.rearrange("(ch p) s2 -> p ch s2", p=128),
        in_=h0_fm.ap().rearrange("p (ch s2) -> p ch s2", ch=NC)), inc=(s_cc, 16))
    c.wait("gpsimd", s_cc)
    s_g1 = c.sem("s_g1")
    s_g2 = c.sem("s_g2")
    c.op("gpsimd", lambda e: e.collective_compute(
        "AllGather", ALU.bypass, replica_groups=RG, ins=[h0tm_in],
        outs=[h0tm_g]), inc=(s_g1, 1))
    c.wait("gpsimd", s_g1, 1)
    c.op("gpsimd", lambda e: e.collective_compute(
        "AllGather", ALU.bypass, replica_groups=RG, ins=[h0fm_in],
        outs=[h0fm_g]), inc=(s_g2, 1))
    p_tr1_cm.__exit__(None, None, None)
    c.pop_scope()
    c.barrier()

    c.push_scope()
    wsum_pack = emit_wsum_rep(c, W_in, wsum_d, p_bf="defer",
                              ld_sem_names=("a_eld0", "a_eld1"))
    c.wait("sync", s_g1, 1)
    c.wait("sync", s_g2, 1)
    p_w = c.psum("p2_pw", [128, 2048], BF16)
    wt_ldb = [c.sbuf(f"wt_ldb{i}", [128, D], F32) for i in range(3)]
    wt_cab = [c.sbuf(f"wt_cab{i}", [128, D], BF16) for i in range(3)]
    wt_evb = [c.sbuf(f"wt_evb{i}", [128, D], BF16) for i in range(2)]
    wtb = (wt_ldb, wt_cab, wt_evb)
    s_ewt, EWT_NOC, s_ewt_ev, ewt_oc = emit_weight_transpose(
        c, "p3ewt", ew_in, ewT_d, 8 * D, D, ident, p_w, o_block=D, sp="wt",
        bufs=wtb, defer=True)
    s_awt, AWT_NOC, s_awt_ev, awt_oc = emit_weight_transpose(
        c, "p4awt", attw_in, attwT_d, 5 * D, D, ident, p_w, o_block=D,
        sp="wt", bufs=wtb, defer=True)

    def wt_cb(i):
        if i < 8 or (i - 8) % 2:
            return
        n = (i - 8) // 2
        if n < EWT_NOC:
            ewt_oc(n)
        elif n < EWT_NOC + AWT_NOC:
            if n == EWT_NOC:
                c.wait("tensor", s_ewt_ev, s_ewt_ev.n)
                c.wait("sync", s_ewt_ev, s_ewt_ev.n)
                c.wait("vector", s_ewt[0], s_ewt[0].n)
                c.wait("vector", s_ewt[1], s_ewt[1].n)
                c.wait("vector", s_ewt_ev, s_ewt_ev.n)
            awt_oc(n - EWT_NOC)

    _phase_attn0(c, x_in, h0fm_g, h0tm_g, W_in, x1_d, E_dram, wsum_d,
                 h0_fm, ident, consts, s_fm, tile_cb=wt_cb,
                 wsum_pack=wsum_pack)
    c.pop_scope()
    c.pop_scope()
    c.barrier()

    c.push_scope()
    _phase_moe(c, x1_d, gw_in, ewT_d, x2_d, ident, consts, s_ewt, s_ewt_ev)
    c.pop_scope()
    c.barrier()

    c.push_scope()
    h2_fm = c.sbuf("p4_h2fm", [128, NC * TSL], BF16)
    c.push_scope()
    h2_tm = c.sbuf("p4_h2tm", [128, NT * 1024], BF16)
    p_tr4_cm = nc.psum_tensor("p4_ptr", [128, 2048], BF16)
    p_tr4 = p_tr4_cm.__enter__()
    c.wait("tensor", s_awt_ev, s_awt_ev.n)
    s_h2, s_fm2, _ = _phase_ln(c, "p4", x2_d, None, ident, consts[EPS],
                               h2_tm, h2_fm, p_tr4, None)
    p_tr4_cm.__exit__(None, None, None)
    c.pop_scope()
    c.wait("sync", s_h2, s_h2.n)
    c.wait("sync", s_fm2, s_fm2.n)
    c.wait("vector", s_fm2, s_fm2.n)
    c.wait("tensor", s_fm2, s_fm2.n)
    c.wait("scalar", s_h2, s_h2.n)
    s_pst = _phase_proj(c, h2_fm, s_fm2, attwT_d, s_awt, s_awt_ev,
                        qfm_d, akfm_in, utm_in, None)
    c.wait("gpsimd", s_pst[0], s_pst[0].n)
    c.wait("gpsimd", s_pst[1], s_pst[1].n)
    s_g3 = c.sem("s_g3")
    s_g4 = c.sem("s_g4")
    c.op("gpsimd", lambda e: e.collective_compute(
        "AllGather", ALU.bypass, replica_groups=RG, ins=[akfm_in],
        outs=[akfm_g]), inc=(s_g3, 1))
    c.wait("gpsimd", s_g3, 1)
    c.op("gpsimd", lambda e: e.collective_compute(
        "AllGather", ALU.bypass, replica_groups=RG, ins=[utm_in],
        outs=[utm_g]), inc=(s_g4, 1))
    c.pop_scope()
    c.barrier()
    c.wait("tensor", s_g3, 1)
    c.wait("tensor", s_g4, 1)
    c.wait("sync", s_g3, 1)
    c.wait("sync", s_g4, 1)

    c.push_scope()
    _phase_attn1(c, x2_d, qfm_d, akfm_g, utm_g, x3_d, ident)
    c.pop_scope()
    c.barrier()

    c.push_scope()
    h3_tm = c.sbuf("p6_h3tm", [128, NT * 1024], BF16)
    s_dq, s_dqch, s_dqa = emit_dequant4(c, "p7dq", wug_in, wugdq_d, 2 * DI, D,
                                        sp="dq")
    c.wait("sync", s_dq[0], s_dq[0].n)
    c.wait("sync", s_dq[1], s_dq[1].n)
    p_w_cm = nc.psum_tensor("p7_pw", [128, 2048], BF16)
    p_w = p_w_cm.__enter__()
    s_wt7, _, s_wt7_ev = emit_weight_transpose(
        c, "p7wt", wugdq_d, wugT_d, 2 * DI, D, ident, p_w, o_block=DI,
        in_f32=False, sp="wt")
    c.push_scope()
    p_tr6_cm = nc.psum_tensor("p6_ptr", [128, 2048], BF16)
    p_tr6 = p_tr6_cm.__enter__()
    s_h3, _, _ = _phase_ln(c, "p6", x3_d, None, ident, consts[EPS],
                           h3_tm, None, p_tr6, None)
    p_tr6_cm.__exit__(None, None, None)
    p_am = c.psum("p6_pam", [2, 128], F32)
    c.wait("vector", s_h3)
    am1_sb = c.sbuf("p6_am_keep", [1, 1], F32)
    am1, s_am1 = emit_colmax_to_scalar(c, "p6_am", h3_tm[:, :], NT * 1024,
                                       identf, p_am)
    c.wait("sync", s_am1, s_am1.n)
    bc1, s_bc1 = emit_cc_scales(c, "p6_sc", am1[:, :], s_am1, cc1_in, cc1_out,
                                RG, identf, bc=bc1_t)
    c.pop_scope()
    c.barrier()

    p_w_cm.__exit__(None, None, None)
    c.push_scope()
    p_am = c.psum("p7_pam", [2, 128], F32)
    am2, s_am2 = _phase_ffn_up(c, h3_tm, s_h3, bc1, s_bc1, wugT_d, s_wt7,
                               s_wt7_ev, up_d, gt_d, ident, identf, p_am)
    c.wait("sync", s_am2, s_am2.n)
    bc23, s_bc23 = emit_cc_scales(c, "p7_sc", am2[:, :], s_am2, cc2_in,
                                  cc2_out, RG, identf, n_vals=2, bc=bc23_t)
    c.pop_scope()
    c.pop_scope()
    c.barrier()

    c.push_scope()
    p_am = c.psum("p8_pam", [2, 128], F32)
    bc2 = bc23[:, 0:3]
    bc3 = bc23[:, 3:6]
    am4, s_am4 = _phase_ffn_h(c, up_d, gt_d, bc2, bc3, s_bc23, upq_d, g2_d,
                              identf, p_am)
    c.wait("sync", s_am4, s_am4.n)
    bc4, s_bc4 = emit_cc_scales(c, "p8_sc", am4[:, :], s_am4, cc4_in, cc4_out,
                                RG, identf, bc=bc4_t)
    c.pop_scope()
    c.barrier()

    c.push_scope()
    p_am = c.psum("p9_pam", [2, 128], F32)
    am5, s_am5 = _phase_ffn_i(c, g2_d, upq_d, bc4, s_bc4, o_d, identf, p_am)
    c.wait("sync", s_am5, s_am5.n)
    bc5, s_bc5 = emit_cc_scales(c, "p9_sc", am5[:, :], s_am5, cc5_in, cc5_out,
                                RG, identf, bc=bc5_t)
    c.pop_scope()
    c.barrier()

    c.push_scope()
    rbJ = mk_round_bufs(c, "p10_rb", DI)
    s_dqd, s_dqdch, s_dqda = emit_dequant4(c, "p10dq", wd_in, wdndq_d, D, DI,
                                           rbufs=rbJ, nld=1, sp="dq")
    c.wait("sync", s_dqd[0], s_dqd[0].n)
    c.wait("sync", s_dqd[1], s_dqd[1].n)
    c.wait("scalar", s_dqdch, s_dqdch.n)
    c.wait("vector", s_dqdch, s_dqdch.n)
    c.wait("scalar", s_dqda, s_dqda.n)
    c.wait("vector", s_dqda, s_dqda.n)
    p_w_cm = nc.psum_tensor("p10_pw", [128, 2048], BF16)
    p_w = p_w_cm.__enter__()
    s_wtd, _, s_wtd_ev = emit_weight_transpose(
        c, "p10wt", wdndq_d, wdnT_d, D, DI, ident, p_w, in_f32=False, sp="wt")
    p_w_cm.__exit__(None, None, None)
    _phase_ffn_j(c, o_d, bc5, s_bc5, x3_d, wdnT_d, s_wtd, s_wtd_ev, y_out,
                 ident, rbJ)
    c.pop_scope()

    c.build_block()
    c.close()
    return nc


def _run(nc, im):
    return run_bass_kernel_spmd(nc, im, core_ids=list(range(NCORES))).results


def kernel(L0_attn_x, L0_attn_W, L0_ffn_gate_w, L0_ffn_expert_w,
           L1_attn_Wk, L1_attn_Wq, L1_attn_Wv, L1_attn_Wa, L1_attn_Wb,
           L1_ffn_w_gate, L1_ffn_w_up, L1_ffn_w_down):
    f32 = np.float32
    x = np.ascontiguousarray(np.asarray(L0_attn_x, f32).reshape(T, D))
    W = np.ascontiguousarray(np.asarray(L0_attn_W, f32))
    gw = np.ascontiguousarray(np.asarray(L0_ffn_gate_w, f32))
    ew = np.ascontiguousarray(np.asarray(L0_ffn_expert_w, f32).reshape(8 * D, D))
    attw = np.ascontiguousarray(np.concatenate([
        np.asarray(L1_attn_Wk, f32), np.asarray(L1_attn_Wq, f32),
        np.asarray(L1_attn_Wv, f32), np.asarray(L1_attn_Wa, f32),
        np.asarray(L1_attn_Wb, f32)], 0))
    wug = np.ascontiguousarray(np.concatenate([
        np.asarray(L1_ffn_w_up, f32), np.asarray(L1_ffn_w_gate, f32)], 0))
    wd = np.ascontiguousarray(np.asarray(L1_ffn_w_down, f32))
    xs = _split_tokens(x)
    nc = build_fused()
    im = [{"x": xs[i], "W": W, "gate_w": gw, "expert_w": ew, "attw": attw,
           "wug": wug, "w_down": wd} for i in range(NCORES)]
    r = _run(nc, im)
    y = np.concatenate([q["y"] for q in r], 0)
    return y.reshape(1, T, D)


def kernel_unfused(L0_attn_x, L0_attn_W, L0_ffn_gate_w, L0_ffn_expert_w,
                   L1_attn_Wk, L1_attn_Wq, L1_attn_Wv, L1_attn_Wa, L1_attn_Wb,
                   L1_ffn_w_gate, L1_ffn_w_up, L1_ffn_w_down):
    f32 = np.float32
    x = np.ascontiguousarray(np.asarray(L0_attn_x, f32).reshape(T, D))
    W = np.ascontiguousarray(np.asarray(L0_attn_W, f32))
    gw = np.ascontiguousarray(np.asarray(L0_ffn_gate_w, f32))
    ew = np.ascontiguousarray(np.asarray(L0_ffn_expert_w, f32).reshape(8 * D, D))
    attw = np.ascontiguousarray(np.concatenate([
        np.asarray(L1_attn_Wk, f32), np.asarray(L1_attn_Wq, f32),
        np.asarray(L1_attn_Wv, f32), np.asarray(L1_attn_Wa, f32),
        np.asarray(L1_attn_Wb, f32)], 0))
    wug = np.ascontiguousarray(np.concatenate([
        np.asarray(L1_ffn_w_up, f32), np.asarray(L1_ffn_w_gate, f32)], 0))
    wd = np.ascontiguousarray(np.asarray(L1_ffn_w_down, f32))

    xs = _split_tokens(x)
    r = _run(build_launch_A(), [{"x": xs[i]} for i in range(NCORES)])
    h0tm = np.ascontiguousarray(np.concatenate([q["h_tm"] for q in r], 0))
    h0fm = np.ascontiguousarray(np.concatenate([q["h_fm"] for q in r], 1))
    r = _run(build_launch_B(), [
        {"x": xs[i], "h_fm": h0fm, "h_tm": h0tm, "W": W,
         "hq": np.ascontiguousarray(h0fm[:, i * TSL:(i + 1) * TSL])}
        for i in range(NCORES)])
    x1 = [np.ascontiguousarray(q["x1"]) for q in r]
    r = _run(build_launch_C(), [
        {"x1": x1[i], "gate_w": gw, "expert_w": ew} for i in range(NCORES)])
    x2 = [np.ascontiguousarray(q["x2"]) for q in r]
    r = _run(build_launch_A(), [{"x": x2[i]} for i in range(NCORES)])
    h2fm = [np.ascontiguousarray(q["h_fm"]) for q in r]
    r = _run(build_launch_D(), [
        {"h_fm": h2fm[i], "attw": attw} for i in range(NCORES)])
    qfm = [np.ascontiguousarray(q["q_fm"]) for q in r]
    akfm = np.ascontiguousarray(np.concatenate([q["ak_fm"] for q in r], 1))
    utm = np.ascontiguousarray(np.concatenate([q["u_tm"] for q in r], 0))
    r = _run(build_launch_E(), [
        {"x2": x2[i], "q_fm": qfm[i], "ak_fm": akfm, "u_tm": utm}
        for i in range(NCORES)])
    x3 = [np.ascontiguousarray(q["x3"]) for q in r]
    r = _run(build_launch_A(with_fm=False, with_absmax=True),
             [{"x": x3[i]} for i in range(NCORES)])
    h3 = [np.ascontiguousarray(q["h_tm"]) for q in r]
    am1 = np.ascontiguousarray(np.stack([q["amax"][0] for q in r]))
    r = _run(build_launch_G(), [
        {"h3": h3[i], "amax1": am1, "wug": wug} for i in range(NCORES)])
    upr = [np.ascontiguousarray(q["up_raw"]) for q in r]
    gtr = [np.ascontiguousarray(q["gate_raw"]) for q in r]
    am23 = np.ascontiguousarray(np.stack([q["amax23"][:, 0] for q in r]))
    r = _run(build_launch_H(), [
        {"up_raw": upr[i], "gate_raw": gtr[i], "amax23": am23}
        for i in range(NCORES)])
    upq = [np.ascontiguousarray(q["up_q"]) for q in r]
    g2r = [np.ascontiguousarray(q["gate2raw"]) for q in r]
    am4 = np.ascontiguousarray(np.stack([q["amax4"][0] for q in r]))
    r = _run(build_launch_I(), [
        {"gate2raw": g2r[i], "up_q": upq[i], "amax4": am4}
        for i in range(NCORES)])
    orw = [np.ascontiguousarray(q["o_raw"]) for q in r]
    am5 = np.ascontiguousarray(np.stack([q["amax5"][0] for q in r]))
    r = _run(build_launch_J(), [
        {"o_raw": orw[i], "amax5": am5, "x3": x3[i], "w_down": wd}
        for i in range(NCORES)])
    x4 = np.concatenate([q["x4"] for q in r], 0)
    return x4.reshape(1, T, D)
